# version 01
# baseline (speedup 1.0000x reference)
import sys
if "/opt/trn_rl_repo" not in sys.path:
    sys.path.insert(0, "/opt/trn_rl_repo")
from contextlib import ExitStack

import numpy as np
import ml_dtypes

import concourse.bass as bass
import concourse.mybir as mybir
from concourse.bass_utils import run_bass_kernel_spmd

F32 = mybir.dt.float32
BF16 = mybir.dt.bfloat16
AF = mybir.ActivationFunctionType
ALU = mybir.AluOpType
AX = mybir.AxisListType

TOK = 4096
D = 1024
NB = TOK // 128
NT = TOK // 512
ENGINES = ("sync", "gpsimd", "vector", "scalar", "tensor")
DEBUG_P1 = False
A3_CUT = set()
B3_CUT = set()


def const_inputs():
    ident = np.eye(128, dtype=np.float32)
    tril = np.tril(np.ones((128, 128), bool))
    trif = np.where(tril, np.float32(0.0), np.float32(-1e30)).astype(np.float32)
    return {
        "c_identbf": ident.astype(ml_dtypes.bfloat16),
        "c_identf": ident,
        "trif": trif,
    }


class Sem:

    def __init__(self, stack, nc, name):
        self.h = stack.enter_context(nc.semaphore(name))
        self.v = 0


class Prog:

    def __init__(self, nc, stack):
        self.nc = nc
        self.stack = stack
        self.steps = {e: [] for e in ENGINES}
        self.hwm = {e: {} for e in ENGINES}
        self.sems = {}
        self.common = None

    def sem(self, name):
        if name not in self.sems:
            self.sems[name] = Sem(self.stack, self.nc, name)
        return self.sems[name]

    def emit(self, eng, fn, waits=(), inc=None, k=1):
        real_waits = []
        hw = self.hwm[eng]
        for s, v in waits:
            if v is None or v <= 0:
                continue
            if hw.get(s, 0) >= v:
                continue
            hw[s] = v
            real_waits.append((s.h, v))
        incs = ()
        ret = None
        if inc is not None:
            inc.v += k
            ret = inc.v
            incs = ((inc.h, k),)
        self.steps[eng].append((fn, tuple(real_waits), incs))
        return ret

    def barrier(self):
        vals = [(s, s.v) for s in self.sems.values() if s.v > 0]
        for e in ENGINES:
            for s, v in vals:
                if self.hwm[e].get(s, 0) < v:
                    self.hwm[e][s] = v
                    self.steps[e].append((None, ((s.h, v),), ()))

    def build(self, block):
        def section(steps):
            def body(eng):
                for fn, waits, incs in steps:
                    for h, v in waits:
                        eng.wait_ge(h, v)
                    if fn is None:
                        continue
                    inst = fn(eng)
                    for h, k in incs:
                        inst.then_inc(h, k)
            return body

        for name in ENGINES:
            getattr(block, name)(section(self.steps[name]))


class Common:

    def __init__(self, p, stack, need_stg=True, need_ones512=True):
        nc = p.nc
        self.identbf = stack.enter_context(nc.sbuf_tensor("identbf", [128, 128], BF16))
        self.identf = stack.enter_context(nc.sbuf_tensor("identf", [128, 128], F32))
        self.ident_in_bf = nc.dram_tensor("c_identbf", [128, 128], BF16,
                                          kind="ExternalInput")
        self.ident_in_f = nc.dram_tensor("c_identf", [128, 128], F32,
                                         kind="ExternalInput")
        self.ones_row = stack.enter_context(nc.sbuf_tensor("ones_row", [1, 512], F32))
        self.onescol = stack.enter_context(nc.sbuf_tensor("onescol", [128, 1], F32))
        self.ones512 = (stack.enter_context(nc.sbuf_tensor("ones512", [128, 512], F32))
                        if need_ones512 else None)
        self.stg = (stack.enter_context(nc.sbuf_tensor("wstg", [128, 4096], BF16))
                    if need_stg else None)
        self.stg_free = [0, 0]
        sIN = p.sem("sIN")
        sGE = p.sem("sGE")
        sVE = p.sem("sVE")
        p.emit("sync", lambda e: e.dma_start(self.identbf[:, :], self.ident_in_bf.ap()),
               inc=sIN, k=16)
        p.emit("sync", lambda e: e.dma_start(self.identf[:, :], self.ident_in_f.ap()),
               inc=sIN, k=16)
        self.ready_in = sIN.v
        p.emit("vector", lambda e: e.memset(self.ones_row[:, :], 1.0), inc=sVE)
        p.emit("vector", lambda e: e.memset(self.onescol[:, :], 1.0), inc=sVE)
        if need_ones512:
            p.emit("vector", lambda e: e.memset(self.ones512[:, :], 1.0), inc=sVE)
        self.ready_ge = sGE.v
        self.ready_ve = sVE.v
        self.sIN_sem = sIN


def stream_transpose_weight(p, stack, name, dram, rows, cols, pT, pT_free,
                            prep_ready=None, dst=None):
    nc = p.nc
    com = p.common
    sTE, sVE = p.sem("sTE"), p.sem("sVE")
    sIN = p.sem("sIN")
    sST = [p.sem("sST0"), p.sem("sST1")]
    nrc = rows // 128
    ncc = cols // 128
    assert cols <= 2048
    if dst is None:
        dst = stack.enter_context(nc.sbuf_tensor(name, [128, nrc * cols], BF16))
    last = 0
    idready = prep_ready if prep_ready is not None else (sIN, com.ready_in)
    for a in range(nrc):
        half = a % 2
        lv = p.emit(
            "gpsimd",
            (lambda a, half: lambda e: e.dma_start(
                com.stg[:, half * 2048: half * 2048 + cols],
                dram.ap()[a * 128:(a + 1) * 128, :]))(a, half),
            waits=[(sTE, com.stg_free[half])], inc=sST[half], k=16)
        for kk in range(ncc):
            slot = (a * ncc + kk) % 2
            waits = [(sST[half], lv), idready]
            if pT_free[slot]:
                waits.append((sVE, pT_free[slot]))
            tv = p.emit(
                "tensor",
                (lambda half, kk, slot: lambda e: e.transpose(
                    pT[slot][:, :],
                    com.stg[:, half * 2048 + kk * 128: half * 2048 + (kk + 1) * 128],
                    com.identbf[:, :]))(half, kk, slot),
                waits=waits, inc=sTE)
            if kk == ncc - 1:
                com.stg_free[half] = tv
            pT_free[slot] = last = p.emit(
                "vector",
                (lambda a, kk, slot: lambda e: e.tensor_copy(
                    dst[:, kk * rows + a * 128: kk * rows + (a + 1) * 128],
                    pT[slot][:, :]))(a, kk, slot),
                waits=[(sTE, tv)], inc=sVE)
    return dst, last


def columnize(p, row_ap_fn, n, dst, dst_col0, psum, guard, waits=()):
    sTE, sVE = p.sem("sTE"), p.sem("sVE")
    for i in range(n):
        w = list(waits)
        if guard[0]:
            w.append((sVE, guard[0]))
        tv = p.emit(
            "tensor",
            (lambda i: lambda e: e.matmul(
                psum[:, 0:1], row_ap_fn(i), p.common.ones_row[0:1, 0:1],
                start=True, stop=True))(i),
            waits=w, inc=sTE)
        guard[0] = p.emit(
            "vector",
            (lambda i: lambda e: e.tensor_copy(dst[:, dst_col0 + i:dst_col0 + i + 1],
                                               psum[:, 0:1]))(i),
            waits=[(sTE, tv)], inc=sVE)
    return guard[0]


def emit_ab(p, stack, ab_dram, st1, st2, stats_ready, pfx="", store_sem=None):
    nc = p.nc
    sVE, sSE, sOUT = p.sem("sVE"), p.sem("sSE"), p.sem("sOUT")
    if store_sem is None:
        store_sem = sOUT
    ab = stack.enter_context(nc.sbuf_tensor(pfx + "ab_sb", [128, 2 * NB], F32))
    tmp = stack.enter_context(nc.sbuf_tensor(pfx + "ab_tmp", [128, NB], F32))
    var = stack.enter_context(nc.sbuf_tensor(pfx + "ab_var", [128, NB], F32))
    mean = stack.enter_context(nc.sbuf_tensor(pfx + "ab_mean", [128, NB], F32))
    eps = stack.enter_context(nc.sbuf_tensor(pfx + "ab_eps", [128, 1], F32))
    inv = 1.0 / D
    p.emit("vector", lambda e: e.memset(eps[:, :], 1e-5), inc=sVE)
    v = p.emit("vector", lambda e: e.tensor_scalar(mean[:, :], st1[:, :], inv, None,
                                                   op0=ALU.mult),
               waits=stats_ready, inc=sVE)
    v = p.emit("vector", lambda e: e.tensor_scalar(var[:, :], st2[:, :], inv, None,
                                                   op0=ALU.mult), waits=[(sVE, v)], inc=sVE)
    v = p.emit("vector", lambda e: e.tensor_tensor(tmp[:, :], mean[:, :], mean[:, :],
                                                   op=ALU.mult), waits=[(sVE, v)], inc=sVE)
    v = p.emit("vector", lambda e: e.tensor_tensor(var[:, :], var[:, :], tmp[:, :],
                                                   op=ALU.subtract),
               waits=[(sVE, v)], inc=sVE)
    s = p.emit("scalar", lambda e: e.activation(tmp[:, :], var[:, :], AF.Ln,
                                                bias=eps[:, 0:1]),
               waits=[(sVE, v)], inc=sSE)
    v = p.emit("vector", lambda e: e.tensor_scalar(tmp[:, :], tmp[:, :], -0.5, None,
                                                   op0=ALU.mult),
               waits=[(sSE, s)], inc=sVE)
    s = p.emit("scalar", lambda e: e.activation(ab[:, 0:NB], tmp[:, :], AF.Exp),
               waits=[(sVE, v)], inc=sSE)
    v = p.emit("vector",
               lambda e: e.scalar_tensor_tensor(ab[:, NB:2 * NB], mean[:, :], -1.0,
                                                ab[:, 0:NB], op0=ALU.mult,
                                                op1=ALU.mult),
               waits=[(sSE, s)], inc=sVE)
    p.emit("gpsimd", lambda e: e.dma_start(ab_dram.ap()[:, :], ab[:, :]),
           waits=[(sVE, v)], inc=store_sem, k=16)



def build_l0_attn(p, stack, io):
    nc = p.nc
    com = p.common
    sIN, sWG = p.sem("sIN"), p.sem("sWG")
    sTE, sVE, sSE = p.sem("sTE"), p.sem("sVE"), p.sem("sSE")
    sOUT, sGE = p.sem("sOUT"), p.sem("sGE")

    cols = stack.enter_context(nc.sbuf_tensor("cols", [128, 4 * NB], F32))
    s_own_c, s_pre_c = cols[:, 0:NB], cols[:, NB:2 * NB]
    e_own_c, e_pre_c = cols[:, 2 * NB:3 * NB], cols[:, 3 * NB:4 * NB]
    cm = stack.enter_context(nc.sbuf_tensor("colmisc", [128, 8], F32))
    zm_col, moff_col, negm_col, b2c128 = cm[:, 0:1], cm[:, 1:2], cm[:, 2:3], cm[:, 3:4]
    mcol_own, mcol_pre, tcol = cm[:, 4:5], cm[:, 5:6], cm[:, 6:7]
    misc = stack.enter_context(nc.sbuf_tensor("miscrow", [1, 144], F32))
    negm, dc, zm, b2s = misc[0:1, 0:1], misc[0:1, 1:2], misc[0:1, 2:3], misc[0:1, 3:4]
    mrow = misc[0:1, 16:144]
    xtm = [stack.enter_context(nc.sbuf_tensor(f"xtm{i}", [128, D], F32)) for i in range(2)]
    xbf = [stack.enter_context(nc.sbuf_tensor(f"xbf{i}", [128, D], BF16)) for i in range(2)]
    hbf = [stack.enter_context(nc.sbuf_tensor(f"hbf{i}", [128, 512], BF16)) for i in range(2)]
    sXT = [p.sem("sXT0"), p.sem("sXT1")]
    sXS = [p.sem("sXS0"), p.sem("sXS1")]
    sAB = p.sem("sAB")
    b1c = stack.enter_context(nc.sbuf_tensor("b1c", [128, 8], F32))
    w2c = stack.enter_context(nc.sbuf_tensor("w2c", [128, 8], BF16))
    wtm = stack.enter_context(nc.sbuf_tensor("wtm", [128, D], F32))
    carry = stack.enter_context(nc.sbuf_tensor("carry", [128, 8], F32))
    bc_b = stack.enter_context(nc.sbuf_tensor("bc_b", [128, D], F32))
    st1 = stack.enter_context(nc.sbuf_tensor("st1", [128, NB], F32))
    st2 = stack.enter_context(nc.sbuf_tensor("st2", [128, NB], F32))
    x_fm = stack.enter_context(nc.sbuf_tensor("x_fm", [128, 8 * TOK], BF16))
    wcT = stack.enter_context(nc.sbuf_tensor("wcT", [128, 16 * D], BF16))

    pB = stack.enter_context(nc.psum_tensor("pB", [128, 512], F32))

    with ExitStack() as ps1:
        pT = [ps1.enter_context(nc.psum_tensor(f"pT{i}", [128, 128], BF16))
              for i in range(2)]
        pH = [ps1.enter_context(nc.psum_tensor(f"pH{i}", [128, 512], F32))
              for i in range(2)]
        pSc = ps1.enter_context(nc.psum_tensor("pSc", [128, 4], F32))
        pCa = ps1.enter_context(nc.psum_tensor("pCa", [128, 8], F32))
        pT_free = [0, 0]

        x_fm_pre = ps1.enter_context(nc.sbuf_tensor("x_fm_pre", [128, 8 * 512], BF16))
        b1row = ps1.enter_context(nc.sbuf_tensor("b1row", [1, D], F32))
        w2row = ps1.enter_context(nc.sbuf_tensor("w2row", [1, D], F32))
        bcrow = ps1.enter_context(nc.sbuf_tensor("bcrow", [1, D], F32))
        vin = 0
        for name, ap in [("b1", b1row[:, :]), ("w2", w2row[:, :]),
                         ("bc", bcrow[:, :]), ("b2", b2s), ("zmask", zm)]:
            vin = p.emit("sync",
                         (lambda name, ap: lambda e: e.dma_start(ap, io[name].ap()))(name, ap),
                         inc=sIN, k=16)

        _, wcT_ready = stream_transpose_weight(p, stack, "wcT", io["Wc"], D, 2 * D,
                                               pT, pT_free, prep_ready=(sIN, vin),
                                               dst=wcT)
        w1T, w1T_ready = stream_transpose_weight(p, ps1, "w1T", io["W1"], D, D,
                                                 pT, pT_free, prep_ready=(sIN, vin))
        guard = [0]
        columnize(p, lambda i: b1row[0:1, i * 128:(i + 1) * 128], 8, b1c, 0,
                  pB, guard, waits=[(sIN, vin), (sVE, com.ready_ve)])
        b1c_ready = guard[0]
        columnize(p, lambda i: w2row[0:1, i * 128:(i + 1) * 128], 8, w2c, 0,
                  pB, guard, waits=[(sIN, vin)])
        w2c_ready = guard[0]
        for src, dstc in [(zm, zm_col), (b2s, b2c128)]:
            tv = p.emit("tensor",
                        (lambda src: lambda e: e.matmul(
                            pB[:, 0:1], com.ones_row[0:1, 0:128], src,
                            start=True, stop=True))(src),
                        waits=[(sIN, vin), (sVE, guard[0])], inc=sTE)
            guard[0] = p.emit("vector",
                              (lambda dstc: lambda e: e.tensor_copy(dstc, pB[:, 0:1]))(dstc),
                              waits=[(sTE, tv)], inc=sVE)
        p.emit("vector", lambda e: e.tensor_scalar(moff_col, zm_col, -1.0, 1e30,
                                                   op0=ALU.add, op1=ALU.mult),
               inc=sVE)
        for hh in range(2):
            tv = p.emit("tensor",
                        (lambda hh: lambda e: e.matmul(
                            pB[:, :], com.ones_row[0:1, 0:128],
                            bcrow[0:1, hh * 512:(hh + 1) * 512],
                            start=True, stop=True))(hh),
                        waits=[(sIN, vin), (sVE, guard[0])], inc=sTE)
            guard[0] = p.emit("vector",
                              (lambda hh: lambda e: e.tensor_copy(
                                  bc_b[:, hh * 512:(hh + 1) * 512], pB[:, :]))(hh),
                              waits=[(sTE, tv)], inc=sVE)

        xtm_free = [0, 0]
        xbf_free = [0, 0]
        pH_free = [0, 0]
        hbf_free = [0, 0]
        fm_free = [0]
        pSc_evac = [0]

        def pass1(src_dram, fm_dst, fm_off_fn, s_dst, reuse_fm):
            for i in range(NT):
                for b4 in range(4):
                    b = 4 * i + b4
                    slot = b % 2
                    lv = p.emit("sync",
                                (lambda b, slot: lambda e: e.dma_start(
                                    xtm[slot][:, :],
                                    src_dram.ap()[b * 128:(b + 1) * 128, :]))(b, slot),
                                waits=[(sVE, xtm_free[slot])], inc=sXT[slot], k=16)
                    cv = p.emit("vector",
                                (lambda slot: lambda e: e.tensor_copy(
                                    xbf[slot][:, :], xtm[slot][:, :]))(slot),
                                waits=[(sXT[slot], lv), (sTE, xbf_free[slot])], inc=sVE)
                    xtm_free[slot] = cv
                    for k in range(8):
                        ts = (b * 8 + k) % 2
                        tv = p.emit("tensor",
                                    (lambda slot, k, ts: lambda e: e.transpose(
                                        pT[ts][:, :], xbf[slot][:, k * 128:(k + 1) * 128],
                                        com.identbf[:, :]))(slot, k, ts),
                                    waits=[(sVE, cv), (sVE, pT_free[ts]),
                                           (sIN, com.ready_in)], inc=sTE)
                        if k == 7:
                            xbf_free[slot] = tv
                        o0 = fm_off_fn(k, b)
                        pT_free[ts] = p.emit(
                            "vector",
                            (lambda o0, ts: lambda e: e.tensor_copy(
                                fm_dst[:, o0:o0 + 128], pT[ts][:, :]))(o0, ts),
                            waits=[(sTE, tv)] + ([(sTE, fm_free[0])] if reuse_fm else []),
                            inc=sVE)
                fm_ready = sVE.v
                for a in range(8):
                    hs = a % 2
                    tv = None
                    for k in range(8):
                        o0 = fm_off_fn(k, 4 * i)
                        tv = p.emit(
                            "tensor",
                            (lambda a, k, hs, o0: lambda e: e.matmul(
                                pH[hs][:, :],
                                w1T[:, k * D + a * 128: k * D + (a + 1) * 128],
                                fm_dst[:, o0:o0 + 512],
                                start=(k == 0), stop=(k == 7)))(a, k, hs, o0),
                            waits=[(sVE, fm_ready), (sVE, w1T_ready),
                                   (sSE, pH_free[hs])],
                            inc=(sTE if k == 7 else None))
                    av = p.emit("scalar",
                                (lambda a, hs: lambda e: e.activation(
                                    hbf[hs][:, :], pH[hs][:, :], AF.Tanh,
                                    bias=b1c[:, a:a + 1]))(a, hs),
                                waits=[(sTE, tv), (sTE, hbf_free[hs]),
                                       (sVE, b1c_ready)], inc=sSE)
                    pH_free[hs] = av
                    sv = None
                    for b4 in range(4):
                        sv = p.emit("tensor",
                                    (lambda a, hs, b4: lambda e: e.matmul(
                                        pSc[:, b4:b4 + 1], hbf[hs][:, b4 * 128:(b4 + 1) * 128],
                                        w2c[:, a:a + 1],
                                        start=(a == 0 and b4 == 0),
                                        stop=(a == 7 and b4 == 3)))(a, hs, b4),
                                    waits=[(sSE, av), (sVE, pSc_evac[0]),
                                           (sVE, w2c_ready)],
                                    inc=(sTE if b4 == 3 else None))
                    hbf_free[hs] = sv
                    if a == 7:
                        for b4 in range(4):
                            pSc_evac[0] = p.emit(
                                "vector",
                                (lambda i, b4: lambda e: e.tensor_scalar(
                                    s_dst[:, 4 * i + b4: 4 * i + b4 + 1],
                                    pSc[:, b4:b4 + 1], b2c128, None,
                                    op0=ALU.add))(i, b4),
                                waits=[(sTE, sv), (sVE, guard[0])], inc=sVE)
                if reuse_fm:
                    fm_free[0] = sTE.v

        pass1(io["x_own"], x_fm, lambda k, b: k * TOK + b * 128, s_own_c, False)
        pass1(io["x_pre"], x_fm_pre, lambda k, b: k * 512 + (b % 4) * 128, s_pre_c, True)

        p.emit("vector", lambda e: e.reduce_max(mcol_own, s_own_c, axis=AX.X),
               waits=[(sVE, pSc_evac[0])], inc=sVE)
        p.emit("vector", lambda e: e.reduce_max(mcol_pre, s_pre_c, axis=AX.X), inc=sVE)
        p.emit("vector", lambda e: e.tensor_scalar(mcol_pre, mcol_pre, zm_col, moff_col,
                                                   op0=ALU.mult, op1=ALU.add), inc=sVE)
        v0 = p.emit("vector", lambda e: e.tensor_tensor(tcol, mcol_own, mcol_pre,
                                                        op=ALU.max), inc=sVE)
        tv = p.emit("tensor", lambda e: e.transpose(pB[0:1, 0:128], tcol, com.identf[:, :]),
                    waits=[(sVE, v0)], inc=sTE)
        v1 = p.emit("vector", lambda e: e.tensor_copy(mrow, pB[0:1, 0:128]),
                    waits=[(sTE, tv)], inc=sVE)
        v2 = p.emit("vector", lambda e: e.reduce_max(negm, mrow, axis=AX.X, negate=True),
                    inc=sVE)
        tv = p.emit("tensor", lambda e: e.matmul(pB[:, 0:1], com.ones_row[0:1, 0:128],
                                                 negm, start=True, stop=True),
                    waits=[(sVE, v2)], inc=sTE)
        v3 = p.emit("vector", lambda e: e.tensor_copy(negm_col, pB[:, 0:1]),
                    waits=[(sTE, tv)], inc=sVE)
        e1 = p.emit("scalar", lambda e: e.activation(e_own_c, s_own_c, AF.Exp,
                                                     bias=negm_col),
                    waits=[(sVE, v3)], inc=sSE)
        e2 = p.emit("scalar", lambda e: e.activation(e_pre_c, s_pre_c, AF.Exp,
                                                     bias=negm_col),
                    waits=[(sVE, v3)], inc=sSE)
        e3 = p.emit("vector", lambda e: e.tensor_scalar(e_pre_c, e_pre_c, zm_col, None,
                                                        op0=ALU.mult),
                    waits=[(sSE, e2)], inc=sVE)
        v4 = p.emit("vector", lambda e: e.reduce_sum(tcol, e_pre_c, axis=AX.X),
                    waits=[(sVE, e3)], inc=sVE)
        tv = p.emit("tensor", lambda e: e.transpose(pB[0:1, 0:128], tcol, com.identf[:, :]),
                    waits=[(sVE, v4)], inc=sTE)
        v5 = p.emit("vector", lambda e: e.tensor_copy(mrow, pB[0:1, 0:128]),
                    waits=[(sTE, tv)], inc=sVE)
        p.emit("vector", lambda e: e.reduce_sum(dc, mrow, axis=AX.X), inc=sVE)

        if DEBUG_P1:
            p.emit("gpsimd", lambda e: e.dma_start(io["dbg_cols"].ap()[:, :], cols[:, :]),
                   waits=[(sVE, sVE.v), (sSE, sSE.v)], inc=sOUT, k=16)
            p.emit("gpsimd", lambda e: e.dma_start(io["dbg_misc"].ap()[:, :], misc[:, :]),
                   waits=[(sVE, sVE.v)], inc=sOUT, k=16)

        wtm_free = 0
        for b in range(NB):
            slot = b % 2
            lv = p.emit("sync",
                        (lambda b, slot: lambda e: e.dma_start(
                            xtm[slot][:, :], io["x_pre"].ap()[b * 128:(b + 1) * 128, :]))(b, slot),
                        waits=[(sVE, xtm_free[slot])], inc=sXT[slot], k=16)
            wv = p.emit("vector",
                        (lambda slot, b: lambda e: e.tensor_scalar(
                            wtm[:, :], xtm[slot][:, :], e_pre_c[:, b:b + 1], None,
                            op0=ALU.mult))(slot, b),
                        waits=[(sXT[slot], lv), (sVE, e3), (sTE, wtm_free)], inc=sVE)
            xtm_free[slot] = wv
            for c in range(8):
                v = p.emit(
                    "tensor",
                    (lambda b, c: lambda e: e.matmul(
                        pCa[:, c:c + 1], wtm[:, c * 128:(c + 1) * 128],
                        com.onescol[:, 0:1],
                        start=(b == 0 and c == 0),
                        stop=(b == NB - 1 and c == 7)))(b, c),
                    waits=[(sVE, wv)], inc=(sTE if c == 7 else None))
                if c == 7:
                    wtm_free = v
        cv = p.emit("vector", lambda e: e.tensor_copy(carry[:, :], pCa[:, :]),
                    waits=[(sTE, wtm_free)], inc=sVE)
        psum_clear = cv
        if DEBUG_P1:
            p.emit("gpsimd", lambda e: e.dma_start(io["dbg_carry"].ap()[:, :], carry[:, :]),
                   waits=[(sVE, cv)], inc=sOUT, k=16)

    with ExitStack() as ps2:
        ebc = ps2.enter_context(nc.sbuf_tensor("ebc", [128, 512], F32))
        rden = ps2.enter_context(nc.sbuf_tensor("rden", [128, 512], F32))
        wfm = ps2.enter_context(nc.sbuf_tensor("wfm", [128, 512], F32))
        num = ps2.enter_context(nc.sbuf_tensor("num", [128, 512], F32))
        ctxbf = ps2.enter_context(nc.sbuf_tensor("ctxbf", [128, 8 * 512], BF16))
        den_row = ps2.enter_context(nc.sbuf_tensor("den_row", [1, 512], F32))
        e_row = ps2.enter_context(nc.sbuf_tensor("e_row", [1, 512], F32))
        of32 = ps2.enter_context(nc.sbuf_tensor("of32", [128, D], F32))
        o2 = ps2.enter_context(nc.sbuf_tensor("o2", [128, D], F32))
        sqs = ps2.enter_context(nc.sbuf_tensor("sqs", [128, D], F32))
        xres = [ps2.enter_context(nc.sbuf_tensor(f"xres{i}", [128, D], F32))
                for i in range(2)]
        pO = ps2.enter_context(nc.psum_tensor("pO", [128, 1024], F32))
        pR = ps2.enter_context(nc.psum_tensor("pR", [1, 512], F32))
        if DEBUG_P1:
            dbg_c8 = ps2.enter_context(nc.sbuf_tensor("dbg_c8_sb", [128, 72], F32))
        xres_free = [0, 0]
        pB_free = sVE.v
        pO_free = psum_clear
        ctx_free = 0
        er_ev = 0
        den_mm = 0
        xtm_free2 = xtm_free
        for i in range(NT):
            tvs = None
            for b4 in range(4):
                tvs = p.emit("tensor",
                             (lambda i, b4: lambda e: e.transpose(
                                 pR[0:1, b4 * 128:(b4 + 1) * 128],
                                 e_own_c[:, 4 * i + b4:4 * i + b4 + 1],
                                 com.identf[:, :]))(i, b4),
                             waits=[(sSE, e1), (sVE, er_ev)], inc=sTE)
            erv = p.emit("vector", lambda e: e.tensor_copy(e_row[0:1, :], pR[0:1, :]),
                         waits=[(sTE, tvs), (sTE, den_mm)], inc=sVE)
            er_ev = erv
            tv = p.emit("tensor",
                        lambda e: e.matmul(pB[:, :], com.ones_row[0:1, 0:128],
                                           e_row[0:1, :], start=True, stop=True),
                        waits=[(sVE, erv), (sVE, pB_free)], inc=sTE)
            bv = p.emit("vector", lambda e: e.tensor_copy(ebc[:, :], pB[:, :]),
                        waits=[(sTE, tv)], inc=sVE)
            dv = p.emit("vector",
                        (lambda i: lambda e: e.tensor_tensor_scan(
                            den_row[0:1, :], com.ones512[0:1, :],
                            e_row[0:1, :],
                            dc, op0=ALU.mult, op1=ALU.add))(i),
                        waits=[(sVE, erv)], inc=sVE)
            p.emit("vector", lambda e: e.tensor_copy(dc, den_row[0:1, 511:512]),
                   waits=[(sVE, dv)], inc=sVE)
            den_mm = p.emit("tensor",
                            lambda e: e.matmul(pB[:, :], com.ones_row[0:1, 0:128],
                                               den_row[0:1, :], start=True, stop=True),
                            waits=[(sVE, dv), (sVE, bv)], inc=sTE)
            rv = p.emit("vector", lambda e: e.reciprocal(rden[:, :], pB[:, :]),
                        waits=[(sTE, den_mm)], inc=sVE)
            pB_free = rv
            for k in range(8):
                p.emit("vector",
                       (lambda i, k: lambda e: e.tensor_tensor(
                           wfm[:, :],
                           x_fm[:, k * TOK + i * 512: k * TOK + i * 512 + 512],
                           ebc[:, :], op=ALU.mult))(i, k),
                       waits=[(sVE, bv)], inc=sVE)
                sv_k = p.emit("vector",
                              (lambda k: lambda e: e.tensor_tensor_scan(
                                  num[:, :], com.ones512[:, :], wfm[:, :],
                                  carry[:, k:k + 1], op0=ALU.mult, op1=ALU.add))(k),
                              inc=sVE)
                p.emit("vector",
                       (lambda k: lambda e: e.tensor_copy(
                           carry[:, k:k + 1], num[:, 511:512]))(k),
                       waits=[(sVE, sv_k)], inc=sVE)
                p.emit("vector",
                       (lambda k: lambda e: e.tensor_tensor(
                           ctxbf[:, k * 512:(k + 1) * 512], num[:, :],
                           rden[:, :], op=ALU.mult))(k),
                       waits=[(sVE, rv), (sTE, ctx_free)], inc=sVE)
            ctx_ready = sVE.v
            if DEBUG_P1:
                p.emit("vector",
                       (lambda i: lambda e: e.tensor_copy(
                           dbg_c8[:, i * 8:(i + 1) * 8], carry[:, :]))(i),
                       inc=sVE)
                p.emit("vector",
                       (lambda i: lambda e: e.tensor_copy(
                           dbg_c8[0:1, 64 + i:64 + i + 1], dc))(i),
                       inc=sVE)
            if DEBUG_P1 and i == 0:
                dctx = ps2.enter_context(nc.sbuf_tensor("dctx", [128, 8 * 512], BF16))
                drd = ps2.enter_context(nc.sbuf_tensor("drd", [128, 512], F32))
                dv1 = p.emit("vector", lambda e: e.tensor_copy(dctx[:, :], ctxbf[:, :]),
                             waits=[(sVE, ctx_ready)], inc=sVE)
                dv2 = p.emit("vector", lambda e: e.tensor_copy(drd[:, :], rden[:, :]),
                             inc=sVE)
                p.emit("gpsimd", lambda e: e.dma_start(io["dbg_ctx"].ap()[:, :], dctx[:, :]),
                       waits=[(sVE, dv1)], inc=sOUT, k=16)
                p.emit("gpsimd", lambda e: e.dma_start(io["dbg_rden"].ap()[:, :], drd[:, :]),
                       waits=[(sVE, dv2)], inc=sOUT, k=16)
            for b4 in range(4):
                b = 4 * i + b4
                slot = b % 2
                lv = p.emit("sync",
                            (lambda b, slot: lambda e: e.dma_start(
                                xtm[slot][:, :],
                                io["x_own"].ap()[b * 128:(b + 1) * 128, :]))(b, slot),
                            waits=[(sVE, xtm_free2[slot])], inc=sXT[slot], k=16)
                last_mm = None
                for dh in range(2):
                    for c in range(8):
                        p.emit("tensor",
                               (lambda b4, dh, c: lambda e: e.matmul(
                                   pO[:, dh * 512:(dh + 1) * 512],
                                   ctxbf[:, c * 512 + b4 * 128: c * 512 + b4 * 128 + 128],
                                   wcT[:, c * D + dh * 512: c * D + dh * 512 + 512],
                                   start=(c == 0), stop=False))(b4, dh, c),
                               waits=[(sVE, ctx_ready), (sVE, wcT_ready),
                                      (sVE, pO_free)])
                        last_mm = p.emit(
                            "tensor",
                            (lambda b, dh, c: lambda e: e.matmul(
                                pO[:, dh * 512:(dh + 1) * 512],
                                x_fm[:, c * TOK + b * 128: c * TOK + b * 128 + 128],
                                wcT[:, (8 + c) * D + dh * 512: (8 + c) * D + dh * 512 + 512],
                                start=False, stop=(c == 7)))(b, dh, c),
                            inc=(sTE if (c == 7 and dh == 1) else None))
                ctx_free = last_mm
                ov = p.emit("vector",
                            lambda e: e.tensor_tensor(of32[:, :], pO[:, :], bc_b[:, :],
                                                      op=ALU.add),
                            waits=[(sTE, last_mm)], inc=sVE)
                pO_free = ov
                av = p.emit("scalar",
                            lambda e: e.activation(o2[:, :], of32[:, :], AF.Tanh),
                            waits=[(sVE, ov)], inc=sSE)
                if DEBUG_P1 and b == 0:
                    do2 = ps2.enter_context(nc.sbuf_tensor("do2", [128, D], F32))
                    dv3 = p.emit("vector", lambda e: e.tensor_copy(do2[:, :], o2[:, :]),
                                 waits=[(sSE, av)], inc=sVE)
                    p.emit("gpsimd", lambda e: e.dma_start(io["dbg_o2"].ap()[:, :],
                                                           do2[:, :]),
                           waits=[(sVE, dv3)], inc=sOUT, k=16)
                    p.emit("gpsimd", lambda e: e.dma_start(io["dbg_wct"].ap()[:, :],
                                                           wcT[:, :]),
                           waits=[(sVE, wcT_ready)], inc=sOUT, k=16)
                    p.emit("gpsimd", lambda e: e.dma_start(io["dbg_bcb"].ap()[:, :],
                                                           bc_b[:, :]),
                           waits=[(sVE, sVE.v)], inc=sOUT, k=16)
                xv = p.emit("vector",
                            (lambda slot: lambda e: e.tensor_tensor(
                                xres[slot][:, :], o2[:, :], xtm[slot][:, :],
                                op=ALU.add))(slot),
                            waits=[(sSE, av), (sXT[slot], lv),
                                   (sXS[slot], xres_free[slot])],
                            inc=sVE)
                xtm_free2[slot] = xv
                p.emit("vector",
                       (lambda slot, b: lambda e: e.reduce_sum(
                           st1[:, b:b + 1], xres[slot][:, :], axis=AX.X))(slot, b),
                       waits=[(sVE, xv)], inc=sVE)
                sqv = p.emit("scalar",
                             (lambda slot, b: lambda e: e.activation(
                                 sqs[:, :], xres[slot][:, :], AF.Square,
                                 accum_out=st2[:, b:b + 1]))(slot, b),
                             waits=[(sVE, xv)], inc=sSE)
                xres_free[slot] = p.emit(
                    "gpsimd",
                    (lambda slot, b: lambda e: e.dma_start(
                        io["stream_out"].ap()[b * 128:(b + 1) * 128, :],
                        xres[slot][:, :]))(slot, b),
                    waits=[(sVE, xv), (sSE, sqv)], inc=sXS[slot], k=16)

        if DEBUG_P1:
            p.emit("gpsimd", lambda e: e.dma_start(io["dbg_c8"].ap()[:, :], dbg_c8[:, :]),
                   waits=[(sVE, sVE.v)], inc=sAB, k=16)

    emit_ab(p, stack, io["ab_out"], st1, st2,
            stats_ready=[(sVE, sVE.v), (sSE, sSE.v)], pfx="p1", store_sem=sAB)

    p.emit("gpsimd", lambda e: e.engine_nop(),
           waits=[(sXS[0], sXS[0].v), (sXS[1], sXS[1].v), (sAB, sAB.v)])



def build_l0_ffn(p, stack, io):
    nc = p.nc
    com = p.common
    sIN, sWG = p.sem("sIN"), p.sem("sWG")
    sTE, sVE, sSE = p.sem("sTE"), p.sem("sVE"), p.sem("sSE")
    sOUT, sGE = p.sem("sOUT"), p.sem("sGE")
    E, H = 8, 2048
    NHC = H // 128
    MEGA = 1024
    NMEGA = TOK // MEGA
    TPM = MEGA // 512

    absb = stack.enter_context(nc.sbuf_tensor("f_absb", [128, 64], F32))
    gates = stack.enter_context(nc.sbuf_tensor("f_gates", [128, NB * 8], F32))
    gfm = stack.enter_context(nc.sbuf_tensor("f_gfm", [8, NB * 128], BF16))
    b1c = stack.enter_context(nc.sbuf_tensor("f_b1c", [128, NHC * 8], F32))
    b2bf = stack.enter_context(nc.sbuf_tensor("f_b2bf", [8, D], BF16))
    wgbf = stack.enter_context(nc.sbuf_tensor("f_wgbf", [128, 64], BF16))
    bgb = stack.enter_context(nc.sbuf_tensor("f_bgb", [128, 8], F32))
    gc = stack.enter_context(nc.sbuf_tensor("f_gcols", [128, 16], F32))
    (nmcol, escol, rscol, m1c, m2c, dnc, rgc) = [gc[:, j:j + 1] for j in range(7)]
    gsc = stack.enter_context(nc.sbuf_tensor("f_gsc", [128, 48], F32))
    lg, pr, mk1, im, p2t, mk2 = [gsc[:, j * 8:(j + 1) * 8] for j in range(6)]
    xtm = [stack.enter_context(nc.sbuf_tensor(f"f_xtm{i}", [128, D], F32)) for i in range(2)]
    xbf = stack.enter_context(nc.sbuf_tensor("f_xbf", [128, D], BF16))
    xfmb = stack.enter_context(nc.sbuf_tensor("f_xfmb", [128, 8 * 512], BF16))
    xfmt = [stack.enter_context(nc.sbuf_tensor(f"f_xfmt{i}", [128, 8 * 512], BF16))
            for i in range(2)]
    hsb = stack.enter_context(nc.sbuf_tensor("f_hsb", [128, 8 * 512], BF16))
    y_acc = stack.enter_context(nc.sbuf_tensor("f_yacc", [128, 8 * D], F32))
    w1sb = [stack.enter_context(nc.sbuf_tensor(f"f_w1sb{i}", [128, 8 * H], BF16))
            for i in range(2)]
    w2sb = stack.enter_context(nc.sbuf_tensor("f_w2sb", [128, NHC * D], BF16))
    xres = [stack.enter_context(nc.sbuf_tensor(f"f_xres{i}", [128, D], F32))
            for i in range(2)]
    sqs = stack.enter_context(nc.sbuf_tensor("f_sqs", [128, D], F32))
    st1 = stack.enter_context(nc.sbuf_tensor("f_st1", [128, NB], F32))
    st2 = stack.enter_context(nc.sbuf_tensor("f_st2", [128, NB], F32))

    pG = stack.enter_context(nc.psum_tensor("f_pG", [128, 8], F32))
    pTf = stack.enter_context(nc.psum_tensor("f_pTf", [128, 128], F32))
    pT = [stack.enter_context(nc.psum_tensor(f"f_pT{i}", [128, 128], BF16))
          for i in range(2)]
    pH = [stack.enter_context(nc.psum_tensor(f"f_pH{i}", [128, 512], F32))
          for i in range(2)]
    pY = stack.enter_context(nc.psum_tensor("f_pY", [128, 1024], F32))
    sXT = [p.sem("sXT0"), p.sem("sXT1")]
    sXF = [p.sem("sXF0"), p.sem("sXF1")]
    sW1s = [p.sem("sW1a"), p.sem("sW1b")]
    sW2s = p.sem("sW2")
    sXS = [p.sem("sXS0"), p.sem("sXS1")]
    sSP = p.sem("sSP")
    sAB = p.sem("sAB")
    sB1 = p.sem("sB1")

    vin = p.emit("sync", lambda e: e.dma_start(absb[:, :], io["ab_in"].ap()),
                 inc=sIN, k=16)
    for a in range(8):
        p.emit("gpsimd",
               (lambda a: lambda e: e.dma_start(
                   wgbf[:, a * 8:(a + 1) * 8],
                   io["Wg"].ap()[a * 128:(a + 1) * 128, :]))(a),
               inc=sWG, k=16)
    p.emit("gpsimd", lambda e: e.dma_start(b2bf[:, :], io["b2"].ap()), inc=sWG, k=16)
    wg_ready = b2_ready = p.sem("sWG").v
    with ExitStack() as prep:
        bgrow = prep.enter_context(nc.sbuf_tensor("f_bgrow", [1, 8], F32))
        b1sb = prep.enter_context(nc.sbuf_tensor("f_b1sb", [8, H // 2], F32))
        v1 = p.emit("sync", lambda e: e.dma_start(bgrow[:, :], io["bg"].ap()),
                    inc=sIN, k=16)
        vin = v1
        tv = p.emit("tensor", lambda e: e.matmul(pG[:, :], com.ones_row[0:1, 0:128],
                                                 bgrow[0:1, :], start=True, stop=True),
                    waits=[(sIN, v1), (sVE, com.ready_ve)], inc=sTE)
        bg_ready = p.emit("vector", lambda e: e.tensor_copy(bgb[:, :], pG[:, :]),
                          waits=[(sTE, tv)], inc=sVE)
        b1c_ready = 0
        ptf_free = 0
        b1_used = 0
        for hh in range(2):
            v2 = p.emit("sync",
                        (lambda hh: lambda e: e.dma_start(
                            b1sb[:, :], io["b1"].ap()[:, hh * 1024:(hh + 1) * 1024]))(hh),
                        waits=[(sTE, b1_used)], inc=sB1, k=16)
            for mm in range(NHC // 2):
                m = hh * (NHC // 2) + mm
                b1_used = tv = p.emit(
                    "tensor",
                    (lambda mm: lambda e: e.transpose(
                        pTf[0:128, 0:8], b1sb[:, mm * 128:(mm + 1) * 128],
                        com.identf[0:8, 0:8]))(mm),
                    waits=[(sB1, v2), (sIN, vin), (sVE, ptf_free)],
                    inc=sTE)
                b1c_ready = ptf_free = p.emit(
                    "vector",
                    (lambda m: lambda e: e.tensor_copy(
                        b1c[:, m * 8:(m + 1) * 8], pTf[0:128, 0:8]))(m),
                    waits=[(sTE, tv)], inc=sVE)

        xtm_free = [0, 0]
        xbf_free = 0
        pT_free = [0, 0]
        xfmb_free = 0
        gfm_last = 0
        for i in range(NT):
            evs = []
            for b4 in range(4):
                b = 4 * i + b4
                slot = b % 2
                lv = p.emit("sync",
                            (lambda b, slot: lambda e: e.dma_start(
                                xtm[slot][:, :],
                                io["stream_in"].ap()[b * 128:(b + 1) * 128, :]))(b, slot),
                            waits=[(sVE, xtm_free[slot])], inc=sXT[slot], k=16)
                av = p.emit("vector",
                            (lambda b, slot: lambda e: e.tensor_scalar(
                                xtm[slot][:, :], xtm[slot][:, :],
                                absb[:, b:b + 1], absb[:, 32 + b:32 + b + 1],
                                op0=ALU.mult, op1=ALU.add))(b, slot),
                            waits=[(sXT[slot], lv), (sIN, vin)], inc=sVE)
                cv = p.emit("vector",
                            (lambda slot: lambda e: e.tensor_copy(
                                xbf[:, :], xtm[slot][:, :]))(slot),
                            waits=[(sVE, av), (sTE, xbf_free)], inc=sVE)
                xtm_free[slot] = cv
                for k in range(8):
                    ts = k % 2
                    tv = p.emit("tensor",
                                (lambda k, ts: lambda e: e.transpose(
                                    pT[ts][:, :], xbf[:, k * 128:(k + 1) * 128],
                                    com.identbf[:, :]))(k, ts),
                                waits=[(sVE, cv), (sVE, pT_free[ts])], inc=sTE)
                    if k == 7:
                        xbf_free = tv
                    pT_free[ts] = p.emit(
                        "vector",
                        (lambda k, b4, ts: lambda e: e.tensor_copy(
                            xfmb[:, k * 512 + b4 * 128: k * 512 + (b4 + 1) * 128],
                            pT[ts][:, :]))(k, b4, ts),
                        waits=[(sTE, tv), (sSP, xfmb_free)], inc=sVE)
                fm_b_ready = sVE.v
                gv = None
                for k in range(8):
                    gv = p.emit("tensor",
                                (lambda k, b4: lambda e: e.matmul(
                                    pG[:, :],
                                    xfmb[:, k * 512 + b4 * 128: k * 512 + (b4 + 1) * 128],
                                    wgbf[:, k * 8:(k + 1) * 8],
                                    start=(k == 0), stop=(k == 7)))(k, b4),
                                waits=[(sVE, fm_b_ready), (sWG, wg_ready),
                                       (sVE, bg_ready)],
                                inc=(sTE if k == 7 else None))
                vv = p.emit("vector", lambda e: e.tensor_tensor(lg, pG[:, :], bgb[:, :],
                                                                op=ALU.add),
                            waits=[(sTE, gv)], inc=sVE)
                vv = p.emit("vector", lambda e: e.tensor_scalar(lg, lg, 0.2, None,
                                                                op0=ALU.mult),
                            waits=[(sVE, vv)], inc=sVE)
                vv = p.emit("vector", lambda e: e.reduce_max(nmcol, lg, axis=AX.X,
                                                             negate=True),
                            waits=[(sVE, vv)], inc=sVE)
                sv = p.emit("scalar", lambda e: e.activation(pr, lg, AF.Exp,
                                                             bias=nmcol),
                            waits=[(sVE, vv)], inc=sSE)
                vv = p.emit("vector", lambda e: e.reduce_sum(escol, pr, axis=AX.X),
                            waits=[(sSE, sv)], inc=sVE)
                vv = p.emit("vector", lambda e: e.reciprocal(rscol, escol),
                            waits=[(sVE, vv)], inc=sVE)
                vv = p.emit("vector", lambda e: e.tensor_scalar(pr, pr, rscol, None,
                                                                op0=ALU.mult),
                            waits=[(sVE, vv)], inc=sVE)
                vv = p.emit("vector", lambda e: e.reduce_max(m1c, pr, axis=AX.X),
                            waits=[(sVE, vv)], inc=sVE)
                vv = p.emit("vector", lambda e: e.tensor_scalar(mk1, pr, m1c, None,
                                                                op0=ALU.is_ge),
                            waits=[(sVE, vv)], inc=sVE)
                vv = p.emit("vector", lambda e: e.tensor_scalar(im, mk1, -1.0, 1.0,
                                                                op0=ALU.mult,
                                                                op1=ALU.add),
                            waits=[(sVE, vv)], inc=sVE)
                vv = p.emit("vector", lambda e: e.tensor_tensor(p2t, pr, im,
                                                                op=ALU.mult),
                            waits=[(sVE, vv)], inc=sVE)
                vv = p.emit("vector", lambda e: e.reduce_max(m2c, p2t, axis=AX.X),
                            waits=[(sVE, vv)], inc=sVE)
                vv = p.emit("vector", lambda e: e.tensor_scalar(mk2, p2t, m2c, None,
                                                                op0=ALU.is_ge),
                            waits=[(sVE, vv)], inc=sVE)
                vv = p.emit("vector", lambda e: e.tensor_tensor(dnc, m1c, m2c,
                                                                op=ALU.add),
                            waits=[(sVE, vv)], inc=sVE)
                vv = p.emit("vector", lambda e: e.tensor_scalar(dnc, dnc, 1e-9, None,
                                                                op0=ALU.add),
                            waits=[(sVE, vv)], inc=sVE)
                vv = p.emit("vector", lambda e: e.reciprocal(rgc, dnc),
                            waits=[(sVE, vv)], inc=sVE)
                vv = p.emit("vector", lambda e: e.tensor_tensor(mk1, mk1, mk2,
                                                                op=ALU.add),
                            waits=[(sVE, vv)], inc=sVE)
                vv = p.emit("vector", lambda e: e.tensor_tensor(mk1, pr, mk1,
                                                                op=ALU.mult),
                            waits=[(sVE, vv)], inc=sVE)
                vv = p.emit("vector",
                            (lambda b: lambda e: e.tensor_scalar(
                                gates[:, b * 8:(b + 1) * 8], mk1, rgc, None,
                                op0=ALU.mult))(b),
                            waits=[(sVE, vv)], inc=sVE)
                tv = p.emit("tensor",
                            (lambda b: lambda e: e.transpose(
                                pTf[0:8, :], gates[:, b * 8:(b + 1) * 8],
                                com.identf[:, :]))(b),
                            waits=[(sVE, vv), (sVE, ptf_free)], inc=sTE)
                gfm_last = ptf_free = p.emit(
                    "vector",
                    (lambda b: lambda e: e.tensor_copy(
                        gfm[0:8, b * 128:(b + 1) * 128], pTf[0:8, 0:128]))(b),
                    waits=[(sTE, tv)], inc=sVE)
            xfmb_free = p.emit(
                "gpsimd",
                (lambda i: lambda e: e.dma_start(
                    bass.AP(io["xfm"], i * 512,
                            [[TOK, 128], [128 * TOK, 8], [1, 512]]),
                    xfmb[:, :].rearrange("p (k t) -> p k t", k=8)))(i),
                waits=[(sVE, sVE.v)], inc=sSP, k=16)

    def load_w1(e, par, guard):
        last = 0
        for k in range(8):
            last = p.emit(
                "gpsimd",
                (lambda e, k, par: lambda e_: e_.dma_start(
                    w1sb[par][:, k * H:(k + 1) * H],
                    io["W1"].ap()[e * D + k * 128: e * D + (k + 1) * 128, :]))(e, k, par),
                waits=[(sTE, guard)], inc=sW1s[par], k=16)
        return last

    def load_w2(e, guard):
        last = 0
        for m in range(NHC):
            last = p.emit(
                "gpsimd",
                (lambda e, m: lambda e_: e_.dma_start(
                    w2sb[:, m * D:(m + 1) * D],
                    io["W2"].ap()[e * H + m * 128: e * H + (m + 1) * 128, :]))(e, m),
                waits=[(sTE, guard)], inc=sW2s, k=16)
        return last

    w1_ready = [0, 0]
    w1_users = [0, 0]
    w2_ready = 0
    w2_users = 0
    w1_ready[0] = load_w1(0, 0, 0)
    w2_ready = load_w2(0, 0)
    xt_free = [0, 0]
    pH_free = [0, 0]
    hsb_free = 0
    pY_free = 0
    yacc_wr = 0
    xres_free = [0, 0]
    for mg in range(NMEGA):
        for sb in range(MEGA // 128):
            b = mg * (MEGA // 128) + sb
            tv = None
            for dh in range(2):
                tv = p.emit("tensor",
                            (lambda b, dh: lambda e: e.matmul(
                                pY[:, dh * 512:(dh + 1) * 512],
                                gfm[0:8, b * 128:(b + 1) * 128],
                                b2bf[0:8, dh * 512:(dh + 1) * 512],
                                start=True, stop=True))(b, dh),
                            waits=[(sVE, gfm_last), (sWG, b2_ready),
                                   (sVE, pY_free)], inc=sTE)
            pY_free = yacc_wr = p.emit(
                "vector",
                (lambda sb: lambda e: e.tensor_copy(
                    y_acc[:, sb * D:(sb + 1) * D], pY[:, :]))(sb),
                waits=[(sTE, tv), (sXS[0], xres_free[0]), (sXS[1], xres_free[1])],
                inc=sVE)
        for e in range(8):
            par = e % 2
            nxt = e + 1
            if nxt < 8:
                w1_ready[1 - par] = load_w1(nxt, 1 - par, w1_users[1 - par])
            for ti in range(TPM):
                i = mg * TPM + ti
                slot = ti % 2
                lxv = p.emit(
                    "sync",
                    (lambda i, slot: lambda e_: e_.dma_start(
                        xfmt[slot][:, :].rearrange("p (k t) -> p k t", k=8),
                        bass.AP(io["xfm"], i * 512,
                                [[TOK, 128], [128 * TOK, 8], [1, 512]])))(i, slot),
                    waits=[(sTE, xt_free[slot]), (sSP, xfmb_free)], inc=sXF[slot], k=16)
                for half in range(2):
                    hv = None
                    for mm in range(8):
                        m = half * 8 + mm
                        hs = mm % 2
                        tv = None
                        for k in range(8):
                            tv = p.emit(
                                "tensor",
                                (lambda m, k, hs, slot, par: lambda e_: e_.matmul(
                                    pH[hs][:, :],
                                    w1sb[par][:, k * H + m * 128: k * H + (m + 1) * 128],
                                    xfmt[slot][:, k * 512:(k + 1) * 512],
                                    start=(k == 0), stop=(k == 7)))(m, k, hs, slot, par),
                                waits=[(sXF[slot], lxv), (sW1s[par], w1_ready[par]),
                                       (sSE, pH_free[hs]), (sTE, hsb_free)],
                                inc=(sTE if k == 7 else None))
                            if mm == 7 and k == 7:
                                if half == 1:
                                    w1_users[par] = tv
                                    xt_free[slot] = tv
                        hv = p.emit("scalar",
                                    (lambda mm, hs, e: lambda e_: e_.activation(
                                        hsb[:, mm * 512:(mm + 1) * 512], pH[hs][:, :],
                                        AF.Gelu,
                                        bias=b1c[:, (half * 8 + mm) * 8 + e:
                                                 (half * 8 + mm) * 8 + e + 1]))(mm, hs, e),
                                    waits=[(sTE, tv), (sVE, b1c_ready),
                                           (sTE, hsb_free)], inc=sSE)
                        pH_free[hs] = hv
                    for b4 in range(4):
                        sb = ti * 4 + b4
                        b = i * 4 + b4
                        tv = None
                        for dh in range(2):
                            for mm in range(8):
                                m = half * 8 + mm
                                tv = p.emit(
                                    "tensor",
                                    (lambda m, mm, b4, dh: lambda e_: e_.matmul(
                                        pY[:, dh * 512:(dh + 1) * 512],
                                        hsb[:, mm * 512 + b4 * 128: mm * 512 + (b4 + 1) * 128],
                                        w2sb[:, m * D + dh * 512: m * D + (dh + 1) * 512],
                                        start=(mm == 0), stop=(mm == 7)))(m, mm, b4, dh),
                                    waits=[(sSE, hv), (sW2s, w2_ready), (sVE, pY_free)],
                                    inc=(sTE if mm == 7 else None))
                        if b4 == 3:
                            hsb_free = tv
                            if half == 1:
                                w2_users = tv
                        pY_free = yacc_wr = p.emit(
                            "vector",
                            (lambda sb, b, e: lambda e_: e_.scalar_tensor_tensor(
                                y_acc[:, sb * D:(sb + 1) * D], pY[:, :],
                                gates[:, b * 8 + e:b * 8 + e + 1],
                                y_acc[:, sb * D:(sb + 1) * D],
                                op0=ALU.mult, op1=ALU.add))(sb, b, e),
                            waits=[(sTE, tv)], inc=sVE)
            if nxt < 8:
                w2_ready = load_w2(nxt, w2_users)
            elif mg + 1 < NMEGA:
                w1_ready[0] = load_w1(0, 0, w1_users[0])
                w2_ready = load_w2(0, w2_users)
        for sb in range(MEGA // 128):
            b = mg * (MEGA // 128) + sb
            slot = b % 2
            lv = p.emit("sync",
                        (lambda b, slot: lambda e_: e_.dma_start(
                            xtm[slot][:, :],
                            io["stream_in"].ap()[b * 128:(b + 1) * 128, :]))(b, slot),
                        waits=[(sVE, xtm_free[slot])], inc=sXT[slot], k=16)
            av = p.emit("vector",
                        (lambda b, slot: lambda e_: e_.tensor_scalar(
                            xtm[slot][:, :], xtm[slot][:, :],
                            absb[:, b:b + 1], absb[:, 32 + b:32 + b + 1],
                            op0=ALU.mult, op1=ALU.add))(b, slot),
                        waits=[(sXT[slot], lv)], inc=sVE)
            xv = p.emit("vector",
                        (lambda sb, slot: lambda e_: e_.tensor_tensor(
                            xres[slot][:, :], y_acc[:, sb * D:(sb + 1) * D],
                            xtm[slot][:, :], op=ALU.add))(sb, slot),
                        waits=[(sVE, av), (sVE, yacc_wr), (sXS[slot], xres_free[slot])],
                        inc=sVE)
            xtm_free[slot] = xv
            p.emit("vector",
                   (lambda slot, b: lambda e_: e_.reduce_sum(
                       st1[:, b:b + 1], xres[slot][:, :], axis=AX.X))(slot, b),
                   waits=[(sVE, xv)], inc=sVE)
            sqv = p.emit("scalar",
                         (lambda slot, b: lambda e_: e_.activation(
                             sqs[:, :], xres[slot][:, :], AF.Square,
                             accum_out=st2[:, b:b + 1]))(slot, b),
                         waits=[(sVE, xv)], inc=sSE)
            xres_free[slot] = p.emit(
                "gpsimd",
                (lambda slot, b: lambda e_: e_.dma_start(
                    io["stream_out"].ap()[b * 128:(b + 1) * 128, :],
                    xres[slot][:, :]))(slot, b),
                waits=[(sVE, xv), (sSE, sqv)], inc=sXS[slot], k=16)

    emit_ab(p, stack, io["ab_out"], st1, st2,
            stats_ready=[(sVE, p.sem("sVE").v), (sSE, p.sem("sSE").v)], pfx="p2",
            store_sem=sAB)
    p.emit("gpsimd", lambda e: e.engine_nop(),
           waits=[(sXS[0], sXS[0].v), (sXS[1], sXS[1].v), (sAB, sAB.v)])



def build_l1_attn_a(p, stack, io):
    nc = p.nc
    com = p.common
    sIN, sWG = p.sem("sIN"), p.sem("sWG")
    sTE, sVE, sSE = p.sem("sTE"), p.sem("sVE"), p.sem("sSE")
    sOUT, sGE = p.sem("sOUT"), p.sem("sGE")
    C3 = 3072

    absb = stack.enter_context(nc.sbuf_tensor("a_absb", [128, 64], F32))
    stats = stack.enter_context(nc.sbuf_tensor("a_stats", [128, 48], F32))
    scols = stack.enter_context(nc.sbuf_tensor("a_scols", [128, 4], F32))
    c1, c2 = scols[:, 0:1], scols[:, 1:2]
    xtm = [stack.enter_context(nc.sbuf_tensor(f"a_xtm{i}", [128, D], F32)) for i in range(2)]
    xbf = stack.enter_context(nc.sbuf_tensor("a_xbf", [128, D], BF16))
    xfmb = stack.enter_context(nc.sbuf_tensor("a_xfmb", [128, 8 * 512], BF16))
    xabuf = stack.enter_context(nc.sbuf_tensor("a_xabuf", [128, 24 * 512], BF16))
    vtmb = stack.enter_context(nc.sbuf_tensor("a_vtmb", [128, 4 * D], BF16))
    sqscr = stack.enter_context(nc.sbuf_tensor("a_sqscr", [128, 512], F32))
    bbf = stack.enter_context(nc.sbuf_tensor("a_bbf", [64, C3], BF16))
    atT = stack.enter_context(nc.sbuf_tensor("a_atT", [64, D], BF16))

    pB = stack.enter_context(nc.psum_tensor("a_pB", [128, 512], F32))
    pT = [stack.enter_context(nc.psum_tensor(f"a_pT{i}", [128, 128], BF16))
          for i in range(2)]
    pXA = [stack.enter_context(nc.psum_tensor(f"a_pXA{i}", [128, 512], F32))
           for i in range(2)]
    pT_free = [0, 0]
    sXT = [p.sem("sXT0"), p.sem("sXT1")]
    sSP, sSPV, sSB = p.sem("sSP"), p.sem("sSPV"), p.sem("sSB")

    vin = p.emit("sync", lambda e: e.dma_start(absb[:, :], io["ab_in"].ap()),
                 inc=sIN, k=16)
    wmT, wmT_ready = stream_transpose_weight(p, stack, "a_wmT", io["W_attn"],
                                             C3, D, pT, pT_free,
                                             prep_ready=(sIN, vin))
    with ExitStack() as prep:
        abf = prep.enter_context(nc.sbuf_tensor("a_abf", [128, 8 * 64], BF16))
        lv = p.emit("gpsimd", lambda e: e.dma_start(
            abf[:, :].rearrange("p (a r) -> p a r", a=8),
            io["lora_attn_A"].ap().rearrange("(a p) r -> p a r", p=128)),
            inc=sWG, k=16)
        bv = p.emit("gpsimd", lambda e: e.dma_start(bbf[:, :], io["lora_attn_B"].ap()),
                    inc=sWG, k=16)
        atv = 0
        for a in range(8):
            ts = a % 2
            tv = p.emit("tensor",
                        (lambda a, ts: lambda e: e.transpose(
                            pT[ts][0:64, :], abf[:, a * 64:(a + 1) * 64],
                            com.identbf[:, :]))(a, ts),
                        waits=[(sWG, bv), (sVE, pT_free[ts]), (sIN, vin)],
                        inc=sTE)
            atv = pT_free[ts] = p.emit(
                "vector",
                (lambda a, ts: lambda e: e.tensor_copy(
                    atT[0:64, a * 128:(a + 1) * 128], pT[ts][0:64, 0:128]))(a, ts),
                waits=[(sTE, tv)], inc=sVE)
        lora_done = 0
        pB_free = 0
        for k3 in (range(3) if "lora" not in A3_CUT else range(0)):
            for kk in range(8):
                for ih in range(2):
                    tv = p.emit(
                        "tensor",
                        (lambda k3, kk, ih: lambda e: e.matmul(
                            pB[:, :],
                            bbf[0:64, 1024 * k3 + kk * 128: 1024 * k3 + (kk + 1) * 128],
                            atT[0:64, ih * 512:(ih + 1) * 512],
                            start=True, stop=True))(k3, kk, ih),
                        waits=[(sWG, bv), (sVE, atv), (sVE, wmT_ready),
                               (sVE, pB_free)],
                        inc=sTE)
                    base = kk * C3 + k3 + 3 * (ih * 512)
                    lora_done = pB_free = p.emit(
                        "vector",
                        (lambda base: lambda e: e.tensor_tensor(
                            wmT[:, base:base + 3 * 512 - 2:3],
                            wmT[:, base:base + 3 * 512 - 2:3],
                            pB[:, :], op=ALU.add))(base),
                        waits=[(sTE, tv)], inc=sVE)

    xtm_free = [0, 0]
    xbf_free = 0
    pXA_free_v = [0, 0]
    pXA_free_s = [0, 0]
    xabuf_free = 0
    vtmb_free = 0
    st_ready_v = 0
    for i in range(NT):
        for b4 in range(4):
            b = 4 * i + b4
            slot = b % 2
            lv = p.emit("sync",
                        (lambda b, slot: lambda e: e.dma_start(
                            xtm[slot][:, :],
                            io["stream_in"].ap()[b * 128:(b + 1) * 128, :]))(b, slot),
                        waits=[(sVE, xtm_free[slot])], inc=sXT[slot], k=16)
            av = p.emit("vector",
                        (lambda b, slot: lambda e: e.tensor_scalar(
                            xtm[slot][:, :], xtm[slot][:, :],
                            absb[:, b:b + 1], absb[:, 32 + b:32 + b + 1],
                            op0=ALU.mult, op1=ALU.add))(b, slot),
                        waits=[(sXT[slot], lv), (sIN, vin)], inc=sVE)
            cv = p.emit("vector",
                        (lambda slot: lambda e: e.tensor_copy(
                            xbf[:, :], xtm[slot][:, :]))(slot),
                        waits=[(sVE, av), (sTE, xbf_free)], inc=sVE)
            xtm_free[slot] = cv
            for k in range(8):
                ts = k % 2
                tv = p.emit("tensor",
                            (lambda k, ts: lambda e: e.transpose(
                                pT[ts][:, :], xbf[:, k * 128:(k + 1) * 128],
                                com.identbf[:, :]))(k, ts),
                            waits=[(sVE, cv), (sVE, pT_free[ts])], inc=sTE)
                if k == 7:
                    xbf_free = tv
                pT_free[ts] = p.emit(
                    "vector",
                    (lambda k, b4, ts: lambda e: e.tensor_copy(
                        xfmb[:, k * 512 + b4 * 128: k * 512 + (b4 + 1) * 128],
                        pT[ts][:, :]))(k, b4, ts),
                    waits=[(sTE, tv)], inc=sVE)
        fm_ready = sVE.v
        for m in range(24):
            ms = m % 2
            tv = None
            for kk in range(8):
                tv = p.emit(
                    "tensor",
                    (lambda m, kk, ms: lambda e: e.matmul(
                        pXA[ms][:, :],
                        wmT[:, kk * C3 + m * 128: kk * C3 + (m + 1) * 128],
                        xfmb[:, kk * 512:(kk + 1) * 512],
                        start=(kk == 0), stop=(kk == 7)))(m, kk, ms),
                    waits=[(sVE, fm_ready), (sVE, lora_done),
                           (sVE, pXA_free_v[ms]), (sSE, pXA_free_s[ms])],
                    inc=(sTE if kk == 7 else None))
            ev = p.emit("vector",
                        (lambda m, ms: lambda e: e.tensor_copy(
                            xabuf[:, m * 512:(m + 1) * 512], pXA[ms][:, :]))(m, ms),
                        waits=[(sTE, tv), (sSP, xabuf_free)], inc=sVE)
            if "stats" in A3_CUT:
                pXA_free_v[ms] = ev
                continue
            v1 = p.emit("vector",
                        (lambda ms: lambda e: e.reduce_sum(c1, pXA[ms][:, :],
                                                           axis=AX.X))(ms),
                        waits=[(sVE, ev)], inc=sVE)
            pXA_free_v[ms] = v1
            sq = p.emit("vector",
                        (lambda m: lambda e: e.tensor_tensor(
                            sqscr[:, :], xabuf[:, m * 512:(m + 1) * 512],
                            xabuf[:, m * 512:(m + 1) * 512], op=ALU.mult))(m),
                        waits=[(sVE, v1)], inc=sVE)
            v2 = p.emit("vector",
                        lambda e: e.reduce_sum(c2, sqscr[:, :], axis=AX.X),
                        waits=[(sVE, sq)], inc=sVE)
            if i == 0:
                v3 = p.emit("vector",
                            (lambda m: lambda e: e.tensor_copy(stats[:, m:m + 1], c1))(m),
                            waits=[(sVE, v2)], inc=sVE)
                v4 = p.emit("vector",
                            (lambda m: lambda e: e.tensor_copy(stats[:, 24 + m:25 + m], c2))(m),
                            waits=[(sVE, v3)], inc=sVE)
            else:
                v3 = p.emit("vector",
                            (lambda m: lambda e: e.tensor_tensor(
                                stats[:, m:m + 1], stats[:, m:m + 1], c1,
                                op=ALU.add))(m),
                            waits=[(sVE, v2)], inc=sVE)
                v4 = p.emit("vector",
                            (lambda m: lambda e: e.tensor_tensor(
                                stats[:, 24 + m:25 + m], stats[:, 24 + m:25 + m], c2,
                                op=ALU.add))(m),
                            waits=[(sVE, v3)], inc=sVE)
            st_ready_v = v4
        last_vt = 0
        for m8 in (range(8) if "vt" not in A3_CUT else range(0)):
            m = 16 + m8
            for b4 in range(4):
                ts = (m8 * 4 + b4) % 2
                tv = p.emit("tensor",
                            (lambda m, b4, ts: lambda e: e.transpose(
                                pT[ts][:, :],
                                xabuf[:, m * 512 + b4 * 128: m * 512 + (b4 + 1) * 128],
                                com.identbf[:, :]))(m, b4, ts),
                            waits=[(sVE, sVE.v if (m8 == 0 and b4 == 0) else 0),
                                   (sVE, pT_free[ts]), (sSPV, vtmb_free)],
                            inc=sTE)
                pT_free[ts] = last_vt = p.emit(
                    "vector",
                    (lambda m8, b4, ts: lambda e: e.tensor_copy(
                        vtmb[:, b4 * D + m8 * 128: b4 * D + (m8 + 1) * 128],
                        pT[ts][:, :]))(m8, b4, ts),
                    waits=[(sTE, tv)], inc=sVE)
        if "spill" in A3_CUT:
            continue
        xabuf_free = p.emit(
            "gpsimd",
            (lambda i: lambda e: e.dma_start(
                bass.AP(io["xa_qk"], i * 512,
                        [[TOK, 128], [128 * TOK, 16], [1, 512]]),
                xabuf[:, 0:16 * 512].rearrange("p (k t) -> p k t", k=16)))(i),
            waits=[(sVE, sVE.v)], inc=sSP, k=16)
        vtmb_free = p.emit(
            "gpsimd",
            (lambda i: lambda e: e.dma_start(
                bass.AP(io["v_tm"], i * 512 * D,
                        [[D, 128], [128 * D, 4], [1, D]]),
                vtmb[:, :].rearrange("p (s c) -> p s c", s=4)))(i),
            waits=[(sVE, last_vt)], inc=sSPV, k=16)
    p.emit("gpsimd", lambda e: e.dma_start(io["stats"].ap()[:, :], stats[:, :]),
           waits=[(sVE, p.sem("sVE").v)], inc=sSB, k=16)
    p.emit("gpsimd", lambda e: e.engine_nop(),
           waits=[(sSP, sSP.v), (sSPV, sSPV.v), (sSB, sSB.v)])


def build_phase3a_program():
    nc = bass.Bass("TRN2", target_bir_lowering=False)
    io = {
        "stream_in": nc.dram_tensor("stream_in", [TOK, D], F32, kind="ExternalInput"),
        "ab_in": nc.dram_tensor("ab_in", [128, 64], F32, kind="ExternalInput"),
        "W_attn": nc.dram_tensor("W_attn", [3072, D], F32, kind="ExternalInput"),
        "lora_attn_A": nc.dram_tensor("lora_attn_A", [D, 64], F32, kind="ExternalInput"),
        "lora_attn_B": nc.dram_tensor("lora_attn_B", [64, 3072], F32, kind="ExternalInput"),
        "xa_qk": nc.dram_tensor("xa_qk", [16 * 128, TOK], BF16, kind="ExternalOutput"),
        "v_tm": nc.dram_tensor("v_tm", [TOK, D], BF16, kind="ExternalOutput"),
        "stats": nc.dram_tensor("stats", [128, 48], F32, kind="ExternalOutput"),
    }
    with ExitStack() as stack:
        p = Prog(nc, stack)
        with nc.Block() as block:
            p.common = Common(p, stack, need_ones512=False)
            build_l1_attn_a(p, stack, io)
            p.build(block)
    return nc


def phase3a_inmaps(inputs, streams, abs_):
    Wa = np.ascontiguousarray(np.asarray(inputs["L1_attn_W_attn"], np.float32))
    lA = np.ascontiguousarray(np.asarray(inputs["L1_attn_lora_attn_A"], np.float32))
    lB = np.ascontiguousarray(np.asarray(inputs["L1_attn_lora_attn_B"], np.float32))
    ci = const_inputs()
    return [{"stream_in": streams[c], "ab_in": abs_[c],
             "W_attn": Wa, "lora_attn_A": lA, "lora_attn_B": lB,
             "c_identbf": ci["c_identbf"], "c_identf": ci["c_identf"]}
            for c in range(8)]



def build_l1_attn_b(p, stack, io):
    nc = p.nc
    com = p.common
    sIN, sWG = p.sem("sIN"), p.sem("sWG")
    sTE, sVE, sSE = p.sem("sTE"), p.sem("sVE"), p.sem("sSE")
    sOUT, sGE = p.sem("sOUT"), p.sem("sGE")
    QG = 512
    NG = TOK // QG
    NMT = 39

    absb = stack.enter_context(nc.sbuf_tensor("b_absb", [128, 64], F32))
    anbn = stack.enter_context(nc.sbuf_tensor("b_anbn", [128, 96], F32))
    an, bn = anbn[:, 0:24], anbn[:, 24:48]
    anq, bnq = anbn[:, 48:56], anbn[:, 56:64]
    bias1c = anbn[:, 64:72]
    mcols = stack.enter_context(nc.sbuf_tensor("b_mcols", [128, 16], F32))
    bnvbf = stack.enter_context(nc.sbuf_tensor("b_bnvbf", [128, 8], BF16))
    st1 = stack.enter_context(nc.sbuf_tensor("b_st1", [128, NB], F32))
    st2 = stack.enter_context(nc.sbuf_tensor("b_st2", [128, NB], F32))
    MT = stack.enter_context(nc.sbuf_tensor("b_MT", [128, NMT * 128], BF16))
    Avb = stack.enter_context(nc.sbuf_tensor("b_Avb", [128, D], F32))
    wpT = None

    pS = stack.enter_context(nc.psum_tensor("b_pS", [128, 1024], F32))
    pO = stack.enter_context(nc.psum_tensor("b_pO", [128, 1024], F32))
    pPT = [stack.enter_context(nc.psum_tensor(f"b_pPT{i}", [128, 128], BF16))
           for i in range(2)]
    pB = stack.enter_context(nc.psum_tensor("b_pB", [128, 512], F32))
    pT_free = [0, 0]
    sXT = [p.sem("sXT0"), p.sem("sXT1")]
    sXS = [p.sem("sXS0"), p.sem("sXS1")]
    sQF = p.sem("sQF")
    sKF = [p.sem("sK0"), p.sem("sK1")]
    sVT = [p.sem("sV0"), p.sem("sV1")]
    sAB = p.sem("sAB")

    wmpT = stack.enter_context(nc.sbuf_tensor("b_wmpT", [128, 8 * D], BF16))
    zsc = stack.enter_context(nc.sbuf_tensor("b_zsc", [128, 2], F32))
    zcol, iecol = zsc[:, 0:1], zsc[:, 1:2]
    vin = p.emit("sync", lambda e: e.dma_start(absb[:, :], io["ab_in"].ap()),
                 inc=sIN, k=16)
    vz = p.emit("sync", lambda e: e.dma_start(zsc[:, :], io["zmask"].ap()),
                inc=sIN, k=16)
    zsc_in = [vz]
    wpT, wpT_ready = stream_transpose_weight(p, stack, "b_wpT", io["W_proj"],
                                             D, D, pPT, pT_free,
                                             prep_ready=(sIN, vz))

    with ExitStack() as prep:
        abf = prep.enter_context(nc.sbuf_tensor("b_abf", [128, 8 * 64], BF16))
        bpbf = prep.enter_context(nc.sbuf_tensor("b_bpbf", [64, D], BF16))
        atT = prep.enter_context(nc.sbuf_tensor("b_atT", [64, D], BF16))
        grow = prep.enter_context(nc.sbuf_tensor("b_grow", [1, 3072], F32))
        brow = prep.enter_context(nc.sbuf_tensor("b_brow", [1, 3072], F32))
        TRI0 = prep.enter_context(nc.sbuf_tensor("b_TRI", [128, 128], F32))
        srows = prep.enter_context(nc.sbuf_tensor("b_srows", [128, 8 * 48], F32))
        tots = prep.enter_context(nc.sbuf_tensor("b_tots", [128, 48], F32))
        tmp24 = prep.enter_context(nc.sbuf_tensor("b_tmp24", [128, 24], F32))
        gcol = prep.enter_context(nc.sbuf_tensor("b_gcol", [128, 24], F32))
        bcol = prep.enter_context(nc.sbuf_tensor("b_bcol", [128, 24], F32))
        epsc = prep.enter_context(nc.sbuf_tensor("b_epsc", [128, 1], F32))
        arow = prep.enter_context(nc.sbuf_tensor("b_arow", [1, D], F32))


        lv = p.emit("gpsimd", lambda e: e.dma_start(
            abf[:, :].rearrange("p (a r) -> p a r", a=8),
            io["lora_proj_A"].ap().rearrange("(a p) r -> p a r", p=128)),
            inc=sWG, k=16)
        bv = p.emit("gpsimd", lambda e: e.dma_start(bpbf[:, :], io["lora_proj_B"].ap()),
                    inc=sWG, k=16)
        atv = 0
        for a in range(8):
            ts = a % 2
            tv = p.emit("tensor",
                        (lambda a, ts: lambda e: e.transpose(
                            pPT[ts][0:64, :], abf[:, a * 64:(a + 1) * 64],
                            com.identbf[:, :]))(a, ts),
                        waits=[(sWG, lv), (sVE, pT_free[ts]), (sIN, com.ready_in)],
                        inc=sTE)
            atv = pT_free[ts] = p.emit(
                "vector",
                (lambda a, ts: lambda e: e.tensor_copy(
                    atT[0:64, a * 128:(a + 1) * 128], pPT[ts][0:64, 0:128]))(a, ts),
                waits=[(sTE, tv)], inc=sVE)
        pB_free = 0
        for kk in range(8):
            for ih in range(2):
                tv = p.emit(
                    "tensor",
                    (lambda kk, ih: lambda e: e.matmul(
                        pB[:, :], bpbf[0:64, kk * 128:(kk + 1) * 128],
                        atT[0:64, ih * 512:(ih + 1) * 512],
                        start=True, stop=True))(kk, ih),
                    waits=[(sWG, bv), (sVE, atv), (sVE, wpT_ready), (sVE, pB_free)],
                    inc=sTE)
                pB_free = p.emit(
                    "vector",
                    (lambda kk, ih: lambda e: e.tensor_tensor(
                        wmpT[:, kk * D + ih * 512: kk * D + (ih + 1) * 512],
                        wpT[:, kk * D + ih * 512: kk * D + (ih + 1) * 512],
                        pB[:, :], op=ALU.add))(kk, ih),
                    waits=[(sTE, tv)], inc=sVE)
        wmpT_ready = pB_free

        v = p.emit("sync", lambda e: e.dma_start(
            srows[:, :].rearrange("p (r f) -> p r f", r=8),
            io["stats_all"].ap().rearrange("(r p) f -> p r f", p=128)),
            inc=sIN, k=16)
        vg = p.emit("sync", lambda e: e.dma_start(grow[:, :], io["bn_gamma"].ap()),
                    inc=sIN, k=16)
        vg = p.emit("sync", lambda e: e.dma_start(brow[:, :], io["bn_beta"].ap()),
                    inc=sIN, k=16)
        vtr = p.emit("sync", lambda e: e.dma_start(TRI0[:, :], io["trif"].ap()),
                     inc=sIN, k=16)
        prep_tot = p.sem("sIN").v
        vv = p.emit("vector", lambda e: e.tensor_copy(tots[:, :], srows[:, 0:48]),
                    waits=[(sIN, prep_tot)], inc=sVE)
        for r in range(1, 8):
            vv = p.emit("vector",
                        (lambda r: lambda e: e.tensor_tensor(
                            tots[:, :], tots[:, :],
                            srows[:, r * 48:(r + 1) * 48], op=ALU.add))(r),
                        waits=[(sVE, vv)], inc=sVE)
        ninv = 1.0 / (8 * TOK)
        vv = p.emit("vector", lambda e: e.tensor_scalar(tots[:, 0:24], tots[:, 0:24],
                                                        ninv, None, op0=ALU.mult),
                    waits=[(sVE, vv)], inc=sVE)
        vv = p.emit("vector", lambda e: e.tensor_scalar(tots[:, 24:48], tots[:, 24:48],
                                                        ninv, None, op0=ALU.mult),
                    waits=[(sVE, vv)], inc=sVE)
        vv = p.emit("vector", lambda e: e.tensor_tensor(tmp24[:, :], tots[:, 0:24],
                                                        tots[:, 0:24], op=ALU.mult),
                    waits=[(sVE, vv)], inc=sVE)
        vv = p.emit("vector", lambda e: e.tensor_tensor(tots[:, 24:48], tots[:, 24:48],
                                                        tmp24[:, :], op=ALU.subtract),
                    waits=[(sVE, vv)], inc=sVE)
        vv = p.emit("vector", lambda e: e.memset(epsc[:, :], 1e-5), waits=[(sVE, vv)],
                    inc=sVE)
        sv = p.emit("scalar", lambda e: e.activation(tmp24[:, :], tots[:, 24:48],
                                                     AF.Ln, bias=epsc[:, 0:1]),
                    waits=[(sVE, vv)], inc=sSE)
        vv = p.emit("vector", lambda e: e.tensor_scalar(tmp24[:, :], tmp24[:, :],
                                                        -0.5, None, op0=ALU.mult),
                    waits=[(sSE, sv)], inc=sVE)
        sv = p.emit("scalar", lambda e: e.activation(tmp24[:, :], tmp24[:, :], AF.Exp),
                    waits=[(sVE, vv)], inc=sSE)
        guard = [0]
        columnize(p, lambda i: grow[0:1, i * 128:(i + 1) * 128], 24, gcol, 0,
                  pB, guard, waits=[(sIN, prep_tot), (sVE, wmpT_ready)])
        columnize(p, lambda i: brow[0:1, i * 128:(i + 1) * 128], 24, bcol, 0,
                  pB, guard, waits=[(sIN, prep_tot)])
        vv = p.emit("vector", lambda e: e.tensor_tensor(an, gcol[:, :], tmp24[:, :],
                                                        op=ALU.mult),
                    waits=[(sSE, sv), (sVE, guard[0])], inc=sVE)
        vv = p.emit("vector",
                    lambda e: e.scalar_tensor_tensor(bn, tots[:, 0:24], -1.0,
                                                     an, op0=ALU.mult, op1=ALU.mult),
                    waits=[(sVE, vv)], inc=sVE)
        vv = p.emit("vector", lambda e: e.tensor_tensor(bn, bn, bcol[:, :], op=ALU.add),
                    waits=[(sVE, vv)], inc=sVE)
        vv = p.emit("vector", lambda e: e.tensor_scalar(anq, an[:, 0:8], 1.0 / 32, None,
                                                        op0=ALU.mult),
                    waits=[(sVE, vv)], inc=sVE)
        vv = p.emit("vector", lambda e: e.tensor_scalar(bnq, bn[:, 0:8], 1.0 / 32, None,
                                                        op0=ALU.mult),
                    waits=[(sVE, vv)], inc=sVE)
        vv = p.emit("vector", lambda e: e.tensor_copy(bnvbf[:, :], bn[:, 16:24]),
                    waits=[(sVE, vv)], inc=sVE)
        an_ready = vv
        avv = 0
        for j in range(8):
            tv = p.emit("tensor",
                        (lambda j: lambda e: e.transpose(
                            pB[0:1, 0:128], an[:, 16 + j:17 + j], com.identf[:, :]))(j),
                        waits=[(sVE, an_ready), (sVE, avv)], inc=sTE)
            avv = p.emit("vector",
                         (lambda j: lambda e: e.tensor_copy(
                             arow[0:1, j * 128:(j + 1) * 128], pB[0:1, 0:128]))(j),
                         waits=[(sTE, tv)], inc=sVE)
        for hh in range(2):
            tv = p.emit("tensor",
                        (lambda hh: lambda e: e.matmul(
                            pB[:, :], com.ones_row[0:1, 0:128],
                            arow[0:1, hh * 512:(hh + 1) * 512],
                            start=True, stop=True))(hh),
                        waits=[(sVE, avv)], inc=sTE)
            avv = p.emit("vector",
                         (lambda hh: lambda e: e.tensor_copy(
                             Avb[:, hh * 512:(hh + 1) * 512], pB[:, :]))(hh),
                         waits=[(sTE, tv)], inc=sVE)
        av_ready = avv
        b1v = 0
        for dch in range(8):
            tv = None
            for c in range(8):
                tv = p.emit("tensor",
                            (lambda dch, c: lambda e: e.matmul(
                                pB[:, 0:1],
                                wpT[:, c * D + dch * 128: c * D + (dch + 1) * 128],
                                bnvbf[:, c:c + 1],
                                start=(c == 0), stop=(c == 7)))(dch, c),
                            waits=[(sVE, an_ready), (sVE, b1v), (sVE, av_ready)],
                            inc=(sTE if c == 7 else None))
            b1v = p.emit("vector",
                         (lambda dch: lambda e: e.tensor_copy(
                             bias1c[:, dch:dch + 1], pB[:, 0:1]))(dch),
                         waits=[(sTE, tv)], inc=sVE)
        bias1_ready = b1v
        zc_ready = zsc_in[0]
        TRI = TRI0
        gvv = prep_tot
        vv = p.emit("vector", lambda e: e.memset(MT[:, 0:3 * 128], 0.0), inc=sVE)
        vv = p.emit("vector", lambda e: e.memset(MT[:, 4 * 128:NMT * 128], -1e30),
                    waits=[(sVE, vv)], inc=sVE)
        vv = p.emit("vector", lambda e: e.tensor_scalar(MT[:, 3 * 128:4 * 128],
                                                        TRI[:, :], iecol, None,
                                                        op0=ALU.mult),
                    waits=[(sIN, prep_tot), (sVE, vv)], inc=sVE)
        vv = p.emit("vector", lambda e: e.tensor_scalar(MT[:, 4 * 128:35 * 128],
                                                        MT[:, 4 * 128:35 * 128],
                                                        iecol, None, op0=ALU.mult),
                    waits=[(sVE, vv)], inc=sVE)
        vv = p.emit("vector", lambda e: e.tensor_scalar(MT[:, 35 * 128:36 * 128],
                                                        MT[:, 35 * 128:36 * 128],
                                                        iecol, None, op0=ALU.mult),
                    waits=[(sVE, vv)], inc=sVE)
        vv = p.emit("vector",
                    lambda e: e.scalar_tensor_tensor(
                        MT[:, 35 * 128:36 * 128], TRI[:, :], zcol,
                        MT[:, 35 * 128:36 * 128], op0=ALU.mult, op1=ALU.add),
                    waits=[(sVE, vv)], inc=sVE)
        mt_ready = vv
        if "dbg" in B3_CUT:
            p.emit("gpsimd", lambda e: e.dma_start(io["dbg_mt"].ap(), MT[:, :]),
                   waits=[(sVE, mt_ready)], inc=sOUT, k=16)
            p.emit("gpsimd", lambda e: e.dma_start(io["dbg_z"].ap(), zsc[:, :]),
                   waits=[(sVE, mt_ready)], inc=sOUT, k=16)

    p.barrier()
    qf = stack.enter_context(nc.sbuf_tensor("b_qf", [128, 8 * QG], BF16))
    kf = [stack.enter_context(nc.sbuf_tensor(f"b_kf{i}", [128, 8 * 1024], BF16))
          for i in range(2)]
    vt = [stack.enter_context(nc.sbuf_tensor(f"b_vt{i}", [128, 8 * 1024], BF16))
          for i in range(2)]
    O_sb = stack.enter_context(nc.sbuf_tensor("b_O", [128, 4 * D], F32))
    S_sb = stack.enter_context(nc.sbuf_tensor("b_S", [128, 1024], F32))
    P_bf = stack.enter_context(nc.sbuf_tensor("b_P", [128, 1024], BF16))
    PT_sb = stack.enter_context(nc.sbuf_tensor("b_PT", [128, 8 * 128], BF16))
    ocfm = stack.enter_context(nc.sbuf_tensor("b_ocfm", [128, 8 * 512], BF16))
    ocb = stack.enter_context(nc.sbuf_tensor("b_ocb", [128, D], BF16))
    y1bf = stack.enter_context(nc.sbuf_tensor("b_y1", [128, 8 * 512], BF16))
    xtm = [stack.enter_context(nc.sbuf_tensor(f"b_xtm{i}", [128, D], F32)) for i in range(2)]
    xres = [stack.enter_context(nc.sbuf_tensor(f"b_xres{i}", [128, D], F32)) for i in range(2)]
    sqs = stack.enter_context(nc.sbuf_tensor("b_sqs", [128, D], F32))
    oc32 = stack.enter_context(nc.sbuf_tensor("b_oc32", [128, D], F32))

    m_run = mcols[:, 0:4]
    l_run = mcols[:, 4:8]
    ms_c, dm_c, sc_c, ng_c, rl_c = (mcols[:, 8:9], mcols[:, 9:10], mcols[:, 10:11],
                                    mcols[:, 11:12], mcols[:, 12:13])

    kf_free = [0, 0]
    vt_free = [0, 0]
    kf_norm = [0, 0]
    vt_ready = [0, 0]
    qf_norm = 0
    pS_free = 0
    pS_free_s = 0
    pO_free = 0
    PT_free = 0
    P_free = 0
    S_free = [0]
    oc_free = 0
    ocfm_free = 0
    y1_free = 0
    xtm_free = [0, 0]
    xres_free = [0, 0]
    ml_guard = 0

    for g in (range(NG) if "flash" not in B3_CUT else range(0)):
        lq = p.emit("sync",
                    (lambda g: lambda e: e.dma_start(
                        qf[:, :].rearrange("p (c t) -> p c t", c=8),
                        bass.AP(io["xa_q"], g * QG,
                                [[TOK, 128], [128 * TOK, 8], [1, QG]])))(g),
                    waits=[(sTE, oc_free)], inc=sQF, k=16)
        qv = 0
        for c in range(8):
            qv = p.emit("vector",
                        (lambda c: lambda e: e.tensor_scalar(
                            qf[:, c * QG:(c + 1) * QG], qf[:, c * QG:(c + 1) * QG],
                            anq[:, c:c + 1], bnq[:, c:c + 1],
                            op0=ALU.mult, op1=ALU.add))(c),
                        waits=[(sQF, lq), (sVE, an_ready)], inc=sVE)
        qf_norm = qv
        smax = (32 + 4 * g + 3) // 8 + 1
        for s in range(smax):
            ks = s % 2
            lk = p.emit("sync",
                        (lambda s, ks: lambda e: e.dma_start(
                            kf[ks][:, :].rearrange("p (c t) -> p c t", c=8),
                            bass.AP(io["k_all"], s * 1024,
                                    [[2 * TOK, 128], [128 * 2 * TOK, 8], [1, 1024]])))(s, ks),
                        waits=[(sTE, kf_free[ks])], inc=sKF[ks], k=16)
            kn = 0
            for c in range(8):
                kn = p.emit("vector",
                            (lambda c, ks: lambda e: e.tensor_scalar(
                                kf[ks][:, c * 1024:(c + 1) * 1024],
                                kf[ks][:, c * 1024:(c + 1) * 1024],
                                an[:, 8 + c:9 + c], bn[:, 8 + c:9 + c],
                                op0=ALU.mult, op1=ALU.add))(c, ks),
                            waits=[(sKF[ks], lk), (sVE, an_ready)], inc=sVE)
            kf_norm[ks] = kn
            lv2 = p.emit("sync",
                         (lambda s, ks: lambda e: e.dma_start(
                             vt[ks][:, :].rearrange("p (j c) -> p j c", j=8),
                             bass.AP(io["v_all"], s * 1024 * D,
                                     [[D, 128], [128 * D, 8], [1, D]])))(s, ks),
                         waits=[(sTE, vt_free[ks])], inc=sVT[ks], k=16)
            vt_ready[ks] = lv2
            for qt in range(4):
                QT = 4 * g + qt
                if 8 * s > 32 + QT:
                    continue
                nh = sum(1 for half in range(2) if 4 * (2 * s + half) <= 32 + QT)
                W = nh * 512
                last_qk = None
                for half in range(nh):
                    for c in range(8):
                        last_qk = p.emit(
                            "tensor",
                            (lambda qt, half, c, ks: lambda e: e.matmul(
                                pS[:, half * 512:(half + 1) * 512],
                                qf[:, c * QG + qt * 128: c * QG + (qt + 1) * 128],
                                kf[ks][:, c * 1024 + half * 512: c * 1024 + (half + 1) * 512],
                                start=(c == 0), stop=(c == 7)))(qt, half, c, ks),
                            waits=[(sVE, qf_norm), (sVE, kf_norm[ks]),
                                   (sVE, pS_free), (sSE, pS_free_s),
                                   (sVE, mt_ready)],
                            inc=(sTE if c == 7 else None))
                ev = 0
                for half in range(nh):
                    kc512 = 2 * s + half
                    d0 = 4 * kc512 - QT
                    if d0 + 3 < 0:
                        ev = p.emit("vector",
                                    (lambda half: lambda e: e.tensor_copy(
                                        S_sb[:, half * 512:(half + 1) * 512],
                                        pS[:, half * 512:(half + 1) * 512]))(half),
                                    waits=[(sTE, last_qk), (sSE, S_free[0])], inc=sVE)
                    else:
                        t0 = (d0 + 3) * 128
                        ev = p.emit("vector",
                                    (lambda half, t0: lambda e: e.tensor_tensor(
                                        S_sb[:, half * 512:(half + 1) * 512],
                                        pS[:, half * 512:(half + 1) * 512],
                                        MT[:, t0:t0 + 512], op=ALU.add))(half, t0),
                                    waits=[(sTE, last_qk), (sSE, S_free[0])], inc=sVE)
                pS_free = ev
                vv = p.emit("vector",
                            (lambda W: lambda e: e.reduce_max(ms_c, S_sb[:, 0:W],
                                                              axis=AX.X))(W),
                            waits=[(sVE, ev), (sVE, ml_guard)], inc=sVE)
                if s == 0:
                    vv = p.emit("vector",
                                (lambda qt: lambda e: e.tensor_copy(
                                    m_run[:, qt:qt + 1], ms_c))(qt),
                                waits=[(sVE, vv)], inc=sVE)
                else:
                    vv = p.emit("vector",
                                (lambda qt: lambda e: e.tensor_tensor(
                                    ms_c, ms_c, m_run[:, qt:qt + 1], op=ALU.max))(qt),
                                waits=[(sVE, vv)], inc=sVE)
                    vv = p.emit("vector",
                                (lambda qt: lambda e: e.tensor_tensor(
                                    dm_c, m_run[:, qt:qt + 1], ms_c,
                                    op=ALU.subtract))(qt),
                                waits=[(sVE, vv)], inc=sVE)
                    sv = p.emit("scalar", lambda e: e.activation(sc_c, dm_c, AF.Exp),
                                waits=[(sVE, vv)], inc=sSE)
                    vv = p.emit("vector",
                                (lambda qt: lambda e: e.tensor_copy(
                                    m_run[:, qt:qt + 1], ms_c))(qt),
                                waits=[(sSE, sv)], inc=sVE)
                vv = p.emit("vector", lambda e: e.tensor_scalar(ng_c, ms_c, -1.0, None,
                                                                op0=ALU.mult),
                            waits=[(sVE, vv)], inc=sVE)
                mxv = vv
                sv = p.emit("scalar",
                            (lambda W: lambda e: e.activation(
                                P_bf[:, 0:W], S_sb[:, 0:W], AF.Exp, bias=ng_c))(W),
                            waits=[(sVE, mxv), (sTE, P_free)], inc=sSE)
                S_free[0] = sv
                pS_free_s = sv
                lsv = p.emit("vector",
                             (lambda W: lambda e: e.reduce_sum(ms_c, P_bf[:, 0:W],
                                                               axis=AX.X))(W),
                             waits=[(sSE, sv)], inc=sVE)
                if s == 0:
                    lv3 = p.emit("vector",
                                 (lambda qt: lambda e: e.tensor_copy(
                                     l_run[:, qt:qt + 1], ms_c))(qt),
                                 waits=[(sVE, lsv)], inc=sVE)
                else:
                    lv3 = p.emit("vector",
                                 (lambda qt: lambda e: e.scalar_tensor_tensor(
                                     l_run[:, qt:qt + 1], l_run[:, qt:qt + 1],
                                     sc_c, ms_c, op0=ALU.mult, op1=ALU.add))(qt),
                                 waits=[(sVE, lsv)], inc=sVE)
                ml_guard = lv3
                ptv = 0
                for j in range(nh * 4):
                    ts = j % 2
                    tv = p.emit("tensor",
                                (lambda j, ts: lambda e: e.transpose(
                                    pPT[ts][:, :], P_bf[:, j * 128:(j + 1) * 128],
                                    com.identbf[:, :]))(j, ts),
                                waits=[(sSE, sv), (sVE, pT_free[ts])], inc=sTE)
                    if j == nh * 4 - 1:
                        P_free = tv
                    pT_free[ts] = ptv = p.emit(
                        "vector",
                        (lambda j, ts: lambda e: e.tensor_copy(
                            PT_sb[:, j * 128:(j + 1) * 128], pPT[ts][:, :]))(j, ts),
                        waits=[(sTE, tv), (sTE, PT_free)], inc=sVE)
                last_pv = None
                for j in range(nh * 4):
                    for dh in range(2):
                        last_pv = p.emit(
                            "tensor",
                            (lambda j, dh, ks: lambda e: e.matmul(
                                pO[:, dh * 512:(dh + 1) * 512],
                                PT_sb[:, j * 128:(j + 1) * 128],
                                vt[ks][:, j * 1024 + dh * 512: j * 1024 + (dh + 1) * 512],
                                start=(j == 0), stop=(j == nh * 4 - 1)))(j, dh, ks),
                            waits=[(sVE, ptv), (sVT[ks], vt_ready[ks]), (sVE, pO_free)],
                            inc=(sTE if (j == nh * 4 - 1 and dh == 1) else None))
                PT_free = last_pv
                if s == 0:
                    ov = p.emit("vector",
                                (lambda qt: lambda e: e.tensor_copy(
                                    O_sb[:, qt * D:(qt + 1) * D], pO[:, :]))(qt),
                                waits=[(sTE, last_pv)], inc=sVE)
                else:
                    ov = p.emit("vector",
                                (lambda qt: lambda e: e.scalar_tensor_tensor(
                                    O_sb[:, qt * D:(qt + 1) * D],
                                    O_sb[:, qt * D:(qt + 1) * D], sc_c, pO[:, :],
                                    op0=ALU.mult, op1=ALU.add))(qt),
                                waits=[(sTE, last_pv)], inc=sVE)
                pO_free = ov
            if s == smax - 1:
                kf_free[ks] = sTE.v
                vt_free[ks] = sTE.v
            else:
                kf_free[ks] = sTE.v
                vt_free[ks] = sTE.v
        if g == 0 and "dbg" in B3_CUT:
            dv = p.emit("vector", lambda e: e.tensor_copy(sqs[:, 0:16], mcols[:, :]),
                        waits=[(sVE, p.sem("sVE").v)], inc=sVE)
            p.emit("gpsimd", lambda e: e.dma_start(io["dbg_ml"].ap()[:, 0:16], sqs[:, 0:16]),
                   waits=[(sVE, dv)], inc=sOUT, k=16)
            dv = p.emit("vector", lambda e: e.tensor_copy(sqs[:, :], O_sb[:, 0:D]),
                        waits=[(sVE, dv)], inc=sVE)
            p.emit("gpsimd", lambda e: e.dma_start(io["dbg_O"].ap(), sqs[:, :]),
                   waits=[(sVE, dv)], inc=sOUT, k=16)
            dv = p.emit("vector", lambda e: e.tensor_copy(sqs[:, :], S_sb[:, 0:D]),
                        waits=[(sVE, dv)], inc=sVE)
            p.emit("gpsimd", lambda e: e.dma_start(io["dbg_S"].ap(), sqs[:, :]),
                   waits=[(sVE, dv)], inc=sOUT, k=16)
        if "proj" in B3_CUT:
            continue
        oc_last = 0
        for qt in range(4):
            vv = p.emit("vector",
                        (lambda qt: lambda e: e.reciprocal(rl_c, l_run[:, qt:qt + 1]))(qt),
                        waits=[(sVE, sVE.v)], inc=sVE)
            vv = p.emit("vector",
                        (lambda qt: lambda e: e.scalar_tensor_tensor(
                            oc32[:, :], O_sb[:, qt * D:(qt + 1) * D], rl_c,
                            Avb[:, :], op0=ALU.mult, op1=ALU.mult))(qt),
                        waits=[(sVE, vv), (sVE, av_ready)], inc=sVE)
            cv = p.emit("vector",
                        (lambda: lambda e: e.tensor_copy(ocb[:, :], oc32[:, :]))(),
                        waits=[(sVE, vv), (sTE, oc_free)], inc=sVE)
            for k in range(8):
                ts = k % 2
                tv = p.emit("tensor",
                            (lambda k, ts: lambda e: e.transpose(
                                pPT[ts][:, :], ocb[:, k * 128:(k + 1) * 128],
                                com.identbf[:, :]))(k, ts),
                            waits=[(sVE, cv), (sVE, pT_free[ts])], inc=sTE)
                if k == 7:
                    oc_free = tv
                pT_free[ts] = oc_last = p.emit(
                    "vector",
                    (lambda k, qt, ts: lambda e: e.tensor_copy(
                        ocfm[:, k * 512 + qt * 128: k * 512 + (qt + 1) * 128],
                        pPT[ts][:, :]))(k, qt, ts),
                    waits=[(sTE, tv), (sTE, ocfm_free)], inc=sVE)
        y1v = p.sem("sVE").v
        for dch in range(8):
            tv = None
            for c in range(8):
                tv = p.emit("tensor",
                            (lambda dch, c: lambda e: e.matmul(
                                pB[:, :],
                                wpT[:, c * D + dch * 128: c * D + (dch + 1) * 128],
                                ocfm[:, c * 512:(c + 1) * 512],
                                start=(c == 0), stop=(c == 7)))(dch, c),
                            waits=[(sVE, oc_last), (sVE, y1_free),
                                   (sVE, bias1_ready), (sVE, y1v)],
                            inc=(sTE if c == 7 else None))
            if dch == 7:
                ocfm_free = tv
            y1v = p.emit("vector",
                         (lambda dch: lambda e: e.tensor_scalar(
                             y1bf[:, dch * 512:(dch + 1) * 512], pB[:, :],
                             bias1c[:, dch:dch + 1], None, op0=ALU.add))(dch),
                         waits=[(sTE, tv)], inc=sVE)
        for qt in (range(4) if "resid" not in B3_CUT else range(0)):
            b = g * 4 + qt
            slot = b % 2
            lv4 = p.emit("sync",
                         (lambda b, slot: lambda e: e.dma_start(
                             xtm[slot][:, :],
                             io["stream_in"].ap()[b * 128:(b + 1) * 128, :]))(b, slot),
                         waits=[(sVE, xtm_free[slot])], inc=sXT[slot], k=16)
            av2 = p.emit("vector",
                         (lambda b, slot: lambda e: e.tensor_scalar(
                             xtm[slot][:, :], xtm[slot][:, :],
                             absb[:, b:b + 1], absb[:, 32 + b:32 + b + 1],
                             op0=ALU.mult, op1=ALU.add))(b, slot),
                         waits=[(sXT[slot], lv4), (sIN, p.sem("sIN").v)], inc=sVE)
            last_y2 = None
            for dh in range(2):
                for c in range(8):
                    last_y2 = p.emit(
                        "tensor",
                        (lambda qt, dh, c: lambda e: e.matmul(
                            pO[:, dh * 512:(dh + 1) * 512],
                            y1bf[:, c * 512 + qt * 128: c * 512 + (qt + 1) * 128],
                            wmpT[:, c * D + dh * 512: c * D + (dh + 1) * 512],
                            start=(c == 0), stop=(c == 7)))(qt, dh, c),
                        waits=[(sVE, y1v), (sVE, wmpT_ready), (sVE, pO_free)],
                        inc=(sTE if (c == 7 and dh == 1) else None))
            if qt == 3:
                y1_free = last_y2
            xv = p.emit("vector",
                        (lambda slot: lambda e: e.tensor_tensor(
                            xres[slot][:, :], pO[:, :], xtm[slot][:, :],
                            op=ALU.add))(slot),
                        waits=[(sTE, last_y2), (sVE, av2),
                               (sXS[slot], xres_free[slot])], inc=sVE)
            pO_free = xv
            xtm_free[slot] = xv
            p.emit("vector",
                   (lambda slot, b: lambda e: e.reduce_sum(
                       st1[:, b:b + 1], xres[slot][:, :], axis=AX.X))(slot, b),
                   waits=[(sVE, xv)], inc=sVE)
            sqv = p.emit("scalar",
                         (lambda slot, b: lambda e: e.activation(
                             sqs[:, :], xres[slot][:, :], AF.Square,
                             accum_out=st2[:, b:b + 1]))(slot, b),
                         waits=[(sVE, xv), (sVE, sVE.v)], inc=sSE)
            xres_free[slot] = p.emit(
                "gpsimd",
                (lambda slot, b: lambda e: e.dma_start(
                    io["stream_out"].ap()[b * 128:(b + 1) * 128, :],
                    xres[slot][:, :]))(slot, b),
                waits=[(sVE, xv), (sSE, sqv)], inc=sXS[slot], k=16)

    if "resid" not in B3_CUT:
        emit_ab(p, stack, io["ab_out"], st1, st2,
                stats_ready=[(sVE, p.sem("sVE").v), (sSE, p.sem("sSE").v)], pfx="p3",
                store_sem=sAB)
    p.emit("gpsimd", lambda e: e.engine_nop(),
           waits=[(sXS[0], sXS[0].v), (sXS[1], sXS[1].v), (sAB, sAB.v)])


def build_phase3b_program():
    nc = bass.Bass("TRN2", target_bir_lowering=False)
    io = {
        "stream_in": nc.dram_tensor("stream_in", [TOK, D], F32, kind="ExternalInput"),
        "ab_in": nc.dram_tensor("ab_in", [128, 64], F32, kind="ExternalInput"),
        "xa_q": nc.dram_tensor("xa_q", [8 * 128, TOK], BF16, kind="ExternalInput"),
        "k_all": nc.dram_tensor("k_all", [8 * 128, 2 * TOK], BF16, kind="ExternalInput"),
        "v_all": nc.dram_tensor("v_all", [2 * TOK, D], BF16, kind="ExternalInput"),
        "stats_all": nc.dram_tensor("stats_all", [8 * 128, 48], F32, kind="ExternalInput"),
        "zmask": nc.dram_tensor("zmask", [128, 2], F32, kind="ExternalInput"),
        "trif": nc.dram_tensor("trif", [128, 128], F32, kind="ExternalInput"),
        "W_proj": nc.dram_tensor("W_proj", [D, D], F32, kind="ExternalInput"),
        "lora_proj_A": nc.dram_tensor("lora_proj_A", [D, 64], F32, kind="ExternalInput"),
        "lora_proj_B": nc.dram_tensor("lora_proj_B", [64, D], F32, kind="ExternalInput"),
        "bn_gamma": nc.dram_tensor("bn_gamma", [1, 3072], F32, kind="ExternalInput"),
        "bn_beta": nc.dram_tensor("bn_beta", [1, 3072], F32, kind="ExternalInput"),
        "stream_out": nc.dram_tensor("stream_out", [TOK, D], F32, kind="ExternalOutput"),
        "ab_out": nc.dram_tensor("ab_out", [128, 64], F32, kind="ExternalOutput"),
    }
    if "dbg" in B3_CUT:
        io["dbg_ml"] = nc.dram_tensor("dbg_ml", [128, 16], F32, kind="ExternalOutput")
        io["dbg_O"] = nc.dram_tensor("dbg_O", [128, D], F32, kind="ExternalOutput")
        io["dbg_S"] = nc.dram_tensor("dbg_S", [128, D], F32, kind="ExternalOutput")
        io["dbg_mt"] = nc.dram_tensor("dbg_mt", [128, 39 * 128], BF16, kind="ExternalOutput")
        io["dbg_z"] = nc.dram_tensor("dbg_z", [128, 2], F32, kind="ExternalOutput")
    with ExitStack() as stack:
        p = Prog(nc, stack)
        with nc.Block() as block:
            p.common = Common(p, stack, need_ones512=False)
            build_l1_attn_b(p, stack, io)
            p.build(block)
    return nc


def phase3b_inmaps(inputs, streams, abs_, qk_list, v_list, stats_list):
    Wp = np.ascontiguousarray(np.asarray(inputs["L1_attn_W_proj"], np.float32))
    lA = np.ascontiguousarray(np.asarray(inputs["L1_attn_lora_proj_A"], np.float32))
    lB = np.ascontiguousarray(np.asarray(inputs["L1_attn_lora_proj_B"], np.float32))
    gam = np.asarray(inputs["L1_attn_bn_gamma"], np.float32).reshape(1, 3072)
    bet = np.asarray(inputs["L1_attn_bn_beta"], np.float32).reshape(1, 3072)
    stats_all = np.ascontiguousarray(
        np.concatenate([stats_list[c] for c in range(8)], axis=0))
    ci = const_inputs()
    maps = []
    for c in range(8):
        b, h = c // 2, c % 2
        c0, c1 = 2 * b, 2 * b + 1
        k_all = np.ascontiguousarray(
            np.concatenate([qk_list[c0][8 * 128:], qk_list[c1][8 * 128:]], axis=1))
        v_all = np.ascontiguousarray(
            np.concatenate([v_list[c0], v_list[c1]], axis=0))
        maps.append({
            "stream_in": streams[c], "ab_in": abs_[c],
            "xa_q": np.ascontiguousarray(qk_list[c][0:8 * 128]),
            "k_all": k_all, "v_all": v_all, "stats_all": stats_all,
            "zmask": np.ascontiguousarray(np.broadcast_to(
                np.array([1.0 if h == 1 else 0.0,
                          0.0 if h == 1 else 1.0], np.float32), (128, 2))),
            "W_proj": Wp, "lora_proj_A": lA, "lora_proj_B": lB,
            "bn_gamma": gam, "bn_beta": bet,
            "trif": ci["trif"],
            "c_identbf": ci["c_identbf"], "c_identf": ci["c_identf"],
        })
    return maps



def build_l1_ffn(p, stack, io):
    nc = p.nc
    com = p.common
    sIN = p.sem("sIN")
    sTE, sVE, sSE = p.sem("sTE"), p.sem("sVE"), p.sem("sSE")
    E, H = 8, 4096
    NHC = H // 128
    NQ = 4
    MEGA = 1024
    NMEGA = TOK // MEGA
    TPM = MEGA // 512

    absb = stack.enter_context(nc.sbuf_tensor("g_absb", [128, 64], F32))
    gates = stack.enter_context(nc.sbuf_tensor("g_gates", [128, NB * 8], F32))
    gfm = stack.enter_context(nc.sbuf_tensor("g_gfm", [8, NB * 128], BF16))
    b1c = stack.enter_context(nc.sbuf_tensor("g_b1c", [128, NHC * 8], F32))
    b2bf = stack.enter_context(nc.sbuf_tensor("g_b2bf", [8, D], BF16))
    wgbf = stack.enter_context(nc.sbuf_tensor("g_wgbf", [128, 64], BF16))
    bgb = stack.enter_context(nc.sbuf_tensor("g_bgb", [128, 8], F32))
    gc = stack.enter_context(nc.sbuf_tensor("g_gcols", [128, 16], F32))
    (nmcol, escol, rscol, m1c, rcol, bcol, mcol, vcol, epsc) = \
        [gc[:, j:j + 1] for j in range(9)]
    gsc = stack.enter_context(nc.sbuf_tensor("g_gsc", [128, 24], F32))
    lg, pr, mk1 = [gsc[:, j * 8:(j + 1) * 8] for j in range(3)]
    xtm = [stack.enter_context(nc.sbuf_tensor(f"g_xtm{i}", [128, D], F32)) for i in range(2)]
    xbf = stack.enter_context(nc.sbuf_tensor("g_xbf", [128, D], BF16))
    xfmb = stack.enter_context(nc.sbuf_tensor("g_xfmb", [128, 8 * 512], BF16))
    xfmt = [stack.enter_context(nc.sbuf_tensor(f"g_xfmt{i}", [128, 8 * 512], BF16))
            for i in range(2)]
    hsb = stack.enter_context(nc.sbuf_tensor("g_hsb", [128, 8 * 512], BF16))
    y_acc = stack.enter_context(nc.sbuf_tensor("g_yacc", [128, 8 * D], F32))
    w1q = [stack.enter_context(nc.sbuf_tensor(f"g_w1q{i}", [128, 8 * D], BF16))
           for i in range(2)]
    w2q = [stack.enter_context(nc.sbuf_tensor(f"g_w2q{i}", [128, 8 * D], BF16))
           for i in range(2)]
    xres = [stack.enter_context(nc.sbuf_tensor(f"g_xres{i}", [128, D], F32))
            for i in range(2)]
    outb = [stack.enter_context(nc.sbuf_tensor(f"g_outb{i}", [128, D], F32))
            for i in range(2)]
    sqs = stack.enter_context(nc.sbuf_tensor("g_sqs", [128, D], F32))

    pG = stack.enter_context(nc.psum_tensor("g_pG", [128, 8], F32))
    pTf = stack.enter_context(nc.psum_tensor("g_pTf", [128, 128], F32))
    pT = [stack.enter_context(nc.psum_tensor(f"g_pT{i}", [128, 128], BF16))
          for i in range(2)]
    pH = [stack.enter_context(nc.psum_tensor(f"g_pH{i}", [128, 512], F32))
          for i in range(2)]
    pY = stack.enter_context(nc.psum_tensor("g_pY", [128, 1024], F32))

    sXT = [p.sem("sXT0"), p.sem("sXT1")]
    sXF = [p.sem("sXF0"), p.sem("sXF1")]
    sW1s = [p.sem("sW1a"), p.sem("sW1b")]
    sW2s = [p.sem("sW2a"), p.sem("sW2b")]
    sXS = [p.sem("sXS0"), p.sem("sXS1")]
    sSP = p.sem("sSP")
    sB1 = p.sem("sB1")
    sWG = p.sem("sWG")

    vin = p.emit("sync", lambda e: e.dma_start(absb[:, :], io["ab_in"].ap()),
                 inc=sIN, k=16)
    for a in range(8):
        p.emit("gpsimd",
               (lambda a: lambda e: e.dma_start(
                   wgbf[:, a * 8:(a + 1) * 8],
                   io["wg"].ap()[a * 128:(a + 1) * 128, :]))(a),
               inc=sWG, k=16)
    p.emit("gpsimd", lambda e: e.dma_start(b2bf[:, :], io["b2"].ap()), inc=sWG, k=16)
    wg_ready = p.sem("sWG").v
    with ExitStack() as prep:
        bgrow = prep.enter_context(nc.sbuf_tensor("g_bgrow", [1, 8], F32))
        b1sb = prep.enter_context(nc.sbuf_tensor("g_b1sb", [8, 1024], F32))
        v1 = p.emit("sync", lambda e: e.dma_start(bgrow[:, :], io["bg"].ap()),
                    inc=sIN, k=16)
        vin = v1
        tv = p.emit("tensor", lambda e: e.matmul(pG[:, :], com.ones_row[0:1, 0:128],
                                                 bgrow[0:1, :], start=True, stop=True),
                    waits=[(sIN, vin), (sVE, com.ready_ve)], inc=sTE)
        bg_ready = p.emit("vector", lambda e: e.tensor_copy(bgb[:, :], pG[:, :]),
                          waits=[(sTE, tv)], inc=sVE)
        b1c_ready = 0
        ptf_free = 0
        b1_used = 0
        for hh in range(4):
            v2 = p.emit("sync",
                        (lambda hh: lambda e: e.dma_start(
                            b1sb[:, :], io["b1"].ap()[:, hh * 1024:(hh + 1) * 1024]))(hh),
                        waits=[(sTE, b1_used)], inc=sB1, k=16)
            for mm in range(8):
                m = hh * 8 + mm
                b1_used = tv = p.emit(
                    "tensor",
                    (lambda mm: lambda e: e.transpose(
                        pTf[0:128, 0:8], b1sb[:, mm * 128:(mm + 1) * 128],
                        com.identf[0:8, 0:8]))(mm),
                    waits=[(sB1, v2), (sIN, vin), (sVE, ptf_free)],
                    inc=sTE)
                b1c_ready = ptf_free = p.emit(
                    "vector",
                    (lambda m: lambda e: e.tensor_copy(
                        b1c[:, m * 8:(m + 1) * 8], pTf[0:128, 0:8]))(m),
                    waits=[(sTE, tv)], inc=sVE)

        xtm_free = [0, 0]
        xbf_free = 0
        pT_free = [0, 0]
        xfmb_free = 0
        gfm_last = 0
        for i in range(NT):
            for b4 in range(4):
                b = 4 * i + b4
                slot = b % 2
                lv = p.emit("sync",
                            (lambda b, slot: lambda e: e.dma_start(
                                xtm[slot][:, :],
                                io["stream_in"].ap()[b * 128:(b + 1) * 128, :]))(b, slot),
                            waits=[(sVE, xtm_free[slot])], inc=sXT[slot], k=16)
                av = p.emit("vector",
                            (lambda b, slot: lambda e: e.tensor_scalar(
                                xtm[slot][:, :], xtm[slot][:, :],
                                absb[:, b:b + 1], absb[:, 32 + b:32 + b + 1],
                                op0=ALU.mult, op1=ALU.add))(b, slot),
                            waits=[(sXT[slot], lv), (sIN, vin)], inc=sVE)
                cv = p.emit("vector",
                            (lambda slot: lambda e: e.tensor_copy(
                                xbf[:, :], xtm[slot][:, :]))(slot),
                            waits=[(sVE, av), (sTE, xbf_free)], inc=sVE)
                xtm_free[slot] = cv
                for k in range(8):
                    ts = k % 2
                    tv = p.emit("tensor",
                                (lambda k, ts: lambda e: e.transpose(
                                    pT[ts][:, :], xbf[:, k * 128:(k + 1) * 128],
                                    com.identbf[:, :]))(k, ts),
                                waits=[(sVE, cv), (sVE, pT_free[ts]),
                                       (sIN, com.ready_in)], inc=sTE)
                    if k == 7:
                        xbf_free = tv
                    pT_free[ts] = p.emit(
                        "vector",
                        (lambda k, b4, ts: lambda e: e.tensor_copy(
                            xfmb[:, k * 512 + b4 * 128: k * 512 + (b4 + 1) * 128],
                            pT[ts][:, :]))(k, b4, ts),
                        waits=[(sTE, tv), (sSP, xfmb_free)], inc=sVE)
                fm_b_ready = sVE.v
                gv = None
                for k in range(8):
                    gv = p.emit("tensor",
                                (lambda k, b4: lambda e: e.matmul(
                                    pG[:, :],
                                    xfmb[:, k * 512 + b4 * 128: k * 512 + (b4 + 1) * 128],
                                    wgbf[:, k * 8:(k + 1) * 8],
                                    start=(k == 0), stop=(k == 7)))(k, b4),
                                waits=[(sVE, fm_b_ready), (sWG, wg_ready),
                                       (sVE, bg_ready)],
                                inc=(sTE if k == 7 else None))
                vv = p.emit("vector", lambda e: e.tensor_tensor(lg, pG[:, :], bgb[:, :],
                                                                op=ALU.add),
                            waits=[(sTE, gv)], inc=sVE)
                vv = p.emit("vector", lambda e: e.reduce_max(nmcol, lg, axis=AX.X,
                                                             negate=True),
                            waits=[(sVE, vv)], inc=sVE)
                sv = p.emit("scalar", lambda e: e.activation(pr, lg, AF.Exp,
                                                             bias=nmcol),
                            waits=[(sVE, vv)], inc=sSE)
                vv = p.emit("vector", lambda e: e.reduce_sum(escol, pr, axis=AX.X),
                            waits=[(sSE, sv)], inc=sVE)
                vv = p.emit("vector", lambda e: e.reciprocal(rscol, escol),
                            waits=[(sVE, vv)], inc=sVE)
                vv = p.emit("vector", lambda e: e.tensor_scalar(pr, pr, rscol, None,
                                                                op0=ALU.mult),
                            waits=[(sVE, vv)], inc=sVE)
                vv = p.emit("vector", lambda e: e.reduce_max(m1c, pr, axis=AX.X),
                            waits=[(sVE, vv)], inc=sVE)
                vv = p.emit("vector", lambda e: e.tensor_scalar(mk1, pr, m1c, None,
                                                                op0=ALU.is_ge),
                            waits=[(sVE, vv)], inc=sVE)
                vv = p.emit("vector",
                            (lambda b: lambda e: e.tensor_scalar(
                                gates[:, b * 8:(b + 1) * 8], mk1, m1c, None,
                                op0=ALU.mult))(b),
                            waits=[(sVE, vv)], inc=sVE)
                tv = p.emit("tensor",
                            (lambda b: lambda e: e.transpose(
                                pTf[0:8, :], gates[:, b * 8:(b + 1) * 8],
                                com.identf[:, :]))(b),
                            waits=[(sVE, vv), (sVE, ptf_free)], inc=sTE)
                gfm_last = ptf_free = p.emit(
                    "vector",
                    (lambda b: lambda e: e.tensor_copy(
                        gfm[0:8, b * 128:(b + 1) * 128], pTf[0:8, 0:128]))(b),
                    waits=[(sTE, tv)], inc=sVE)
            xfmb_free = p.emit(
                "gpsimd",
                (lambda i: lambda e: e.dma_start(
                    bass.AP(io["xfm"], i * 512,
                            [[TOK, 128], [128 * TOK, 8], [1, 512]]),
                    xfmb[:, :].rearrange("p (k t) -> p k t", k=8)))(i),
                waits=[(sVE, sVE.v)], inc=sSP, k=16)

    def load_q(e, qu, par, g1, g2):
        l1 = l2 = 0
        for k in range(8):
            l1 = p.emit(
                "gpsimd",
                (lambda e, qu, k, par: lambda e_: e_.dma_start(
                    w1q[par][:, k * D:(k + 1) * D],
                    io["W1"].ap()[e * D + k * 128: e * D + (k + 1) * 128,
                                  qu * 1024:(qu + 1) * 1024]))(e, qu, k, par),
                waits=[(sTE, g1)], inc=sW1s[par], k=16)
        for k in range(8):
            l2 = p.emit(
                "gpsimd",
                (lambda e, qu, k, par: lambda e_: e_.dma_start(
                    w2q[par][:, k * D:(k + 1) * D],
                    io["W2"].ap()[e * H + qu * 1024 + k * 128:
                                  e * H + qu * 1024 + (k + 1) * 128, :]))(e, qu, k, par),
                waits=[(sTE, g2)], inc=sW2s[par], k=16)
        return l1, l2

    seq = [(mg, e, qu) for mg in range(NMEGA) for e in range(E) for qu in range(NQ)]
    ready1 = {}
    ready2 = {}
    w1_users = [0, 0]
    w2_users = [0, 0]
    ready1[0], ready2[0] = load_q(seq[0][1], seq[0][2], 0, 0, 0)
    ready1[1], ready2[1] = load_q(seq[1][1], seq[1][2], 1, 0, 0)
    xt_free = [0, 0]
    pH_free = [0, 0]
    hsb_free = 0
    pY_free = 0
    yacc_wr = 0
    xres_free = [0, 0]
    for idx, (mg, e, qu) in enumerate(seq):
        par = idx % 2
        if e == 0 and qu == 0:
            for sb in range(MEGA // 128):
                b = mg * (MEGA // 128) + sb
                tv = None
                for dh in range(2):
                    tv = p.emit("tensor",
                                (lambda b, dh: lambda e_: e_.matmul(
                                    pY[:, dh * 512:(dh + 1) * 512],
                                    gfm[0:8, b * 128:(b + 1) * 128],
                                    b2bf[0:8, dh * 512:(dh + 1) * 512],
                                    start=True, stop=True))(b, dh),
                                waits=[(sVE, gfm_last), (sWG, wg_ready),
                                       (sVE, pY_free)], inc=sTE)
                pY_free = yacc_wr = p.emit(
                    "vector",
                    (lambda sb: lambda e_: e_.tensor_copy(
                        y_acc[:, sb * D:(sb + 1) * D], pY[:, :]))(sb),
                    waits=[(sTE, tv), (sXS[0], xres_free[0]),
                           (sXS[1], xres_free[1])], inc=sVE)
        for ti in range(TPM):
            i = mg * TPM + ti
            slot = ti % 2
            lxv = p.emit(
                "sync",
                (lambda i, slot: lambda e_: e_.dma_start(
                    xfmt[slot][:, :].rearrange("p (k t) -> p k t", k=8),
                    bass.AP(io["xfm"], i * 512,
                            [[TOK, 128], [128 * TOK, 8], [1, 512]])))(i, slot),
                waits=[(sTE, xt_free[slot]), (sSP, xfmb_free)], inc=sXF[slot], k=16)
            hv = None
            for mm in range(8):
                hs = mm % 2
                tv = None
                for k in range(8):
                    tv = p.emit(
                        "tensor",
                        (lambda mm, k, hs, slot, par: lambda e_: e_.matmul(
                            pH[hs][:, :],
                            w1q[par][:, k * D + mm * 128: k * D + (mm + 1) * 128],
                            xfmt[slot][:, k * 512:(k + 1) * 512],
                            start=(k == 0), stop=(k == 7)))(mm, k, hs, slot, par),
                        waits=[(sXF[slot], lxv), (sW1s[par], ready1[idx]),
                               (sSE, pH_free[hs]), (sTE, hsb_free)],
                        inc=(sTE if k == 7 else None))
                    if mm == 7 and k == 7:
                        if ti == TPM - 1:
                            w1_users[par] = tv
                        xt_free[slot] = tv
                hv = p.emit("scalar",
                            (lambda mm, hs, qu, e: lambda e_: e_.activation(
                                hsb[:, mm * 512:(mm + 1) * 512], pH[hs][:, :],
                                AF.Relu,
                                bias=b1c[:, (qu * 8 + mm) * 8 + e:
                                         (qu * 8 + mm) * 8 + e + 1]))(mm, hs, qu, e),
                            waits=[(sTE, tv), (sVE, b1c_ready),
                                   (sTE, hsb_free)], inc=sSE)
                pH_free[hs] = hv
            for b4 in range(4):
                sb = ti * 4 + b4
                b = i * 4 + b4
                tv = None
                for dh in range(2):
                    for mm in range(8):
                        tv = p.emit(
                            "tensor",
                            (lambda mm, b4, dh, par: lambda e_: e_.matmul(
                                pY[:, dh * 512:(dh + 1) * 512],
                                hsb[:, mm * 512 + b4 * 128: mm * 512 + (b4 + 1) * 128],
                                w2q[par][:, mm * D + dh * 512: mm * D + (dh + 1) * 512],
                                start=(mm == 0), stop=(mm == 7)))(mm, b4, dh, par),
                            waits=[(sSE, hv), (sW2s[par], ready2[idx]),
                                   (sVE, pY_free)],
                            inc=(sTE if mm == 7 else None))
                if b4 == 3:
                    hsb_free = tv
                    if ti == TPM - 1:
                        w2_users[par] = tv
                pY_free = yacc_wr = p.emit(
                    "vector",
                    (lambda sb, b, e: lambda e_: e_.scalar_tensor_tensor(
                        y_acc[:, sb * D:(sb + 1) * D], pY[:, :],
                        gates[:, b * 8 + e:b * 8 + e + 1],
                        y_acc[:, sb * D:(sb + 1) * D],
                        op0=ALU.mult, op1=ALU.add))(sb, b, e),
                    waits=[(sTE, tv)], inc=sVE)
        if idx + 2 < len(seq):
            ne = seq[idx + 2][1]
            nqu = seq[idx + 2][2]
            ready1[idx + 2], ready2[idx + 2] = load_q(ne, nqu, par,
                                                      w1_users[par], w2_users[par])
        if e == E - 1 and qu == NQ - 1:
            for sb in range(MEGA // 128):
                b = mg * (MEGA // 128) + sb
                slot = b % 2
                lv = p.emit("sync",
                            (lambda b, slot: lambda e_: e_.dma_start(
                                xtm[slot][:, :],
                                io["stream_in"].ap()[b * 128:(b + 1) * 128, :]))(b, slot),
                            waits=[(sVE, xtm_free[slot])], inc=sXT[slot], k=16)
                av = p.emit("vector",
                            (lambda b, slot: lambda e_: e_.tensor_scalar(
                                xtm[slot][:, :], xtm[slot][:, :],
                                absb[:, b:b + 1], absb[:, 32 + b:32 + b + 1],
                                op0=ALU.mult, op1=ALU.add))(b, slot),
                            waits=[(sXT[slot], lv)], inc=sVE)
                xv = p.emit("vector",
                            (lambda sb, slot: lambda e_: e_.tensor_tensor(
                                xres[slot][:, :], y_acc[:, sb * D:(sb + 1) * D],
                                xtm[slot][:, :], op=ALU.add))(sb, slot),
                            waits=[(sVE, av), (sVE, yacc_wr)], inc=sVE)
                xtm_free[slot] = xv
                vv = p.emit("vector",
                            (lambda slot: lambda e_: e_.reduce_sum(
                                mcol, xres[slot][:, :], axis=AX.X))(slot),
                            waits=[(sVE, xv)], inc=sVE)
                sqv = p.emit("scalar",
                             (lambda slot: lambda e_: e_.activation(
                                 sqs[:, :], xres[slot][:, :], AF.Square))(slot),
                             waits=[(sVE, xv), (sSE, sSE.v)], inc=sSE)
                vv = p.emit("vector",
                            lambda e_: e_.reduce_sum(vcol, sqs[:, :], axis=AX.X),
                            waits=[(sSE, sqv), (sVE, vv)], inc=sVE)
                vv = p.emit("vector",
                            lambda e_: e_.tensor_scalar(mcol, mcol, 1.0 / D, None,
                                                        op0=ALU.mult),
                            waits=[(sVE, vv)], inc=sVE)
                vv = p.emit("vector",
                            lambda e_: e_.tensor_scalar(vcol, vcol, 1.0 / D, None,
                                                        op0=ALU.mult),
                            waits=[(sVE, vv)], inc=sVE)
                vv = p.emit("vector",
                            lambda e_: e_.scalar_tensor_tensor(
                                rcol, mcol, -1.0, mcol, op0=ALU.mult, op1=ALU.mult),
                            waits=[(sVE, vv)], inc=sVE)
                vv = p.emit("vector",
                            lambda e_: e_.tensor_tensor(vcol, vcol, rcol, op=ALU.add),
                            waits=[(sVE, vv)], inc=sVE)
                vv = p.emit("vector", lambda e_: e_.memset(epsc, 1e-5),
                            waits=[(sVE, vv)], inc=sVE)
                sv = p.emit("scalar",
                            lambda e_: e_.activation(rcol, vcol, AF.Ln, bias=epsc),
                            waits=[(sVE, vv)], inc=sSE)
                vv = p.emit("vector",
                            lambda e_: e_.tensor_scalar(rcol, rcol, -0.5, None,
                                                        op0=ALU.mult),
                            waits=[(sSE, sv)], inc=sVE)
                sv = p.emit("scalar", lambda e_: e_.activation(rcol, rcol, AF.Exp),
                            waits=[(sVE, vv)], inc=sSE)
                vv = p.emit("vector",
                            lambda e_: e_.scalar_tensor_tensor(
                                bcol, mcol, -1.0, rcol, op0=ALU.mult, op1=ALU.mult),
                            waits=[(sSE, sv)], inc=sVE)
                ov = p.emit("vector",
                            (lambda slot: lambda e_: e_.tensor_scalar(
                                outb[slot][:, :], xres[slot][:, :], rcol, bcol,
                                op0=ALU.mult, op1=ALU.add))(slot),
                            waits=[(sVE, vv), (sXS[slot], xres_free[slot])],
                            inc=sVE)
                xres_free[slot] = p.emit(
                    "gpsimd",
                    (lambda slot, b: lambda e_: e_.dma_start(
                        io["out"].ap()[b * 128:(b + 1) * 128, :],
                        outb[slot][:, :]))(slot, b),
                    waits=[(sVE, ov)], inc=sXS[slot], k=16)
    p.emit("gpsimd", lambda e: e.engine_nop(),
           waits=[(sXS[0], sXS[0].v), (sXS[1], sXS[1].v)])


def build_phase5_program():
    nc = bass.Bass("TRN2", target_bir_lowering=False)
    io = {
        "stream_in": nc.dram_tensor("stream_in", [TOK, D], F32, kind="ExternalInput"),
        "ab_in": nc.dram_tensor("ab_in", [128, 64], F32, kind="ExternalInput"),
        "wg": nc.dram_tensor("wg", [D, 8], F32, kind="ExternalInput"),
        "bg": nc.dram_tensor("bg", [1, 8], F32, kind="ExternalInput"),
        "W1": nc.dram_tensor("W1", [8 * D, 4096], F32, kind="ExternalInput"),
        "b1": nc.dram_tensor("b1", [8, 4096], F32, kind="ExternalInput"),
        "W2": nc.dram_tensor("W2", [8 * 4096, D], F32, kind="ExternalInput"),
        "b2": nc.dram_tensor("b2", [8, D], F32, kind="ExternalInput"),
        "xfm": nc.dram_tensor("xfm", [8 * 128, TOK], BF16, kind="Internal"),
        "out": nc.dram_tensor("out", [TOK, D], F32, kind="ExternalOutput"),
    }
    with ExitStack() as stack:
        p = Prog(nc, stack)
        with nc.Block() as block:
            p.common = Common(p, stack, need_stg=False, need_ones512=False)
            build_l1_ffn(p, stack, io)
            p.build(block)
    return nc


def phase5_inmaps(inputs, streams, abs_):
    wg = np.ascontiguousarray(np.asarray(inputs["L1_ffn_w_gate"], np.float32))
    bg = np.asarray(inputs["L1_ffn_b_gate"], np.float32).reshape(1, 8)
    W1 = np.ascontiguousarray(np.asarray(inputs["L1_ffn_W1"], np.float32).reshape(8 * D, 4096))
    b1 = np.ascontiguousarray(np.asarray(inputs["L1_ffn_b1"], np.float32))
    W2 = np.ascontiguousarray(np.asarray(inputs["L1_ffn_W2"], np.float32).reshape(8 * 4096, D))
    b2 = np.ascontiguousarray(np.asarray(inputs["L1_ffn_b2"], np.float32))
    ci = const_inputs()
    return [{"stream_in": streams[c], "ab_in": abs_[c], "wg": wg, "bg": bg,
             "W1": W1, "b1": b1, "W2": W2, "b2": b2,
             "c_identbf": ci["c_identbf"], "c_identf": ci["c_identf"]}
            for c in range(8)]



def build_phase1_program():
    nc = bass.Bass("TRN2", target_bir_lowering=False)
    io = {
        "x_own": nc.dram_tensor("x_own", [TOK, D], F32, kind="ExternalInput"),
        "x_pre": nc.dram_tensor("x_pre", [TOK, D], F32, kind="ExternalInput"),
        "zmask": nc.dram_tensor("zmask", [1, 1], F32, kind="ExternalInput"),
        "W1": nc.dram_tensor("W1", [D, D], F32, kind="ExternalInput"),
        "b1": nc.dram_tensor("b1", [1, D], F32, kind="ExternalInput"),
        "w2": nc.dram_tensor("w2", [1, D], F32, kind="ExternalInput"),
        "b2": nc.dram_tensor("b2", [1, 1], F32, kind="ExternalInput"),
        "Wc": nc.dram_tensor("Wc", [D, 2 * D], F32, kind="ExternalInput"),
        "bc": nc.dram_tensor("bc", [1, D], F32, kind="ExternalInput"),
        "stream_out": nc.dram_tensor("stream_out", [TOK, D], F32, kind="ExternalOutput"),
        "ab_out": nc.dram_tensor("ab_out", [128, 64], F32, kind="ExternalOutput"),
    }
    if DEBUG_P1:
        io["dbg_cols"] = nc.dram_tensor("dbg_cols", [128, 4 * NB], F32, kind="ExternalOutput")
        io["dbg_misc"] = nc.dram_tensor("dbg_misc", [1, 144], F32, kind="ExternalOutput")
        io["dbg_carry"] = nc.dram_tensor("dbg_carry", [128, 8], F32, kind="ExternalOutput")
        io["dbg_ctx"] = nc.dram_tensor("dbg_ctx", [128, 8 * 512], BF16, kind="ExternalOutput")
        io["dbg_rden"] = nc.dram_tensor("dbg_rden", [128, 512], F32, kind="ExternalOutput")
        io["dbg_o2"] = nc.dram_tensor("dbg_o2", [128, D], F32, kind="ExternalOutput")
        io["dbg_c8"] = nc.dram_tensor("dbg_c8", [128, 72], F32, kind="ExternalOutput")
        io["dbg_wct"] = nc.dram_tensor("dbg_wct", [128, 16 * D], BF16, kind="ExternalOutput")
        io["dbg_bcb"] = nc.dram_tensor("dbg_bcb", [128, D], F32, kind="ExternalOutput")
    with ExitStack() as stack:
        p = Prog(nc, stack)
        with nc.Block() as block:
            p.common = Common(p, stack)
            build_l0_attn(p, stack, io)
            p.build(block)
    return nc


def build_phase2_program():
    nc = bass.Bass("TRN2", target_bir_lowering=False)
    io = {
        "stream_in": nc.dram_tensor("stream_in", [TOK, D], F32, kind="ExternalInput"),
        "ab_in": nc.dram_tensor("ab_in", [128, 64], F32, kind="ExternalInput"),
        "Wg": nc.dram_tensor("Wg", [D, 8], F32, kind="ExternalInput"),
        "bg": nc.dram_tensor("bg", [1, 8], F32, kind="ExternalInput"),
        "W1": nc.dram_tensor("W1", [8 * D, 2048], F32, kind="ExternalInput"),
        "b1": nc.dram_tensor("b1", [8, 2048], F32, kind="ExternalInput"),
        "W2": nc.dram_tensor("W2", [8 * 2048, D], F32, kind="ExternalInput"),
        "b2": nc.dram_tensor("b2", [8, D], F32, kind="ExternalInput"),
        "xfm": nc.dram_tensor("xfm", [8 * 128, TOK], BF16, kind="Internal"),
        "stream_out": nc.dram_tensor("stream_out", [TOK, D], F32, kind="ExternalOutput"),
        "ab_out": nc.dram_tensor("ab_out", [128, 64], F32, kind="ExternalOutput"),
    }
    with ExitStack() as stack:
        p = Prog(nc, stack)
        with nc.Block() as block:
            p.common = Common(p, stack, need_stg=False, need_ones512=False)
            build_l0_ffn(p, stack, io)
            p.build(block)
    return nc


def phase2_inmaps(inputs, streams, abs_):
    Wg = np.ascontiguousarray(np.asarray(inputs["L0_ffn_Wg"], np.float32))
    bg = np.asarray(inputs["L0_ffn_bg"], np.float32).reshape(1, 8)
    W1 = np.ascontiguousarray(np.asarray(inputs["L0_ffn_W1"], np.float32).reshape(8 * D, 2048))
    b1 = np.ascontiguousarray(np.asarray(inputs["L0_ffn_b1"], np.float32))
    W2 = np.ascontiguousarray(np.asarray(inputs["L0_ffn_W2"], np.float32).reshape(8 * 2048, D))
    b2 = np.ascontiguousarray(np.asarray(inputs["L0_ffn_b2"], np.float32))
    ci = const_inputs()
    return [{"stream_in": streams[c], "ab_in": abs_[c], "Wg": Wg, "bg": bg,
             "W1": W1, "b1": b1, "W2": W2, "b2": b2,
             "c_identbf": ci["c_identbf"], "c_identf": ci["c_identf"]}
            for c in range(8)]


def phase1_inmaps(inputs):
    x = np.ascontiguousarray(np.asarray(inputs["L0_attn_x"], dtype=np.float32))
    zeros = np.zeros((TOK, D), np.float32)
    W1 = np.ascontiguousarray(np.asarray(inputs["L0_attn_W1"], np.float32))
    b1 = np.asarray(inputs["L0_attn_b1"], np.float32).reshape(1, D)
    w2 = np.asarray(inputs["L0_attn_w2"], np.float32).reshape(1, D)
    b2 = np.asarray(inputs["L0_attn_b2"], np.float32).reshape(1, 1)
    Wc = np.ascontiguousarray(np.asarray(inputs["L0_attn_Wc"], np.float32))
    bc = np.asarray(inputs["L0_attn_bc"], np.float32).reshape(1, D)
    ci = const_inputs()
    maps = []
    for c in range(8):
        b, h = c // 2, c % 2
        maps.append({
            "c_identbf": ci["c_identbf"], "c_identf": ci["c_identf"],
            "x_own": np.ascontiguousarray(x[b, h * TOK:(h + 1) * TOK, :]),
            "x_pre": np.ascontiguousarray(x[b, 0:TOK, :]) if h == 1 else zeros,
            "zmask": np.array([[1.0 if h == 1 else 0.0]], np.float32),
            "W1": W1, "b1": b1, "w2": w2, "b2": b2, "Wc": Wc, "bc": bc,
        })
    return maps


_PROGRAMS = {}


def _program(name, builder):
    if name not in _PROGRAMS:
        _PROGRAMS[name] = builder()
    return _PROGRAMS[name]


def kernel(**inputs):
    cores = list(range(8))

    nc1 = _program("p1", build_phase1_program)
    res1 = run_bass_kernel_spmd(nc1, phase1_inmaps(inputs), core_ids=cores).results
    streams = [r["stream_out"] for r in res1]
    abs_ = [r["ab_out"] for r in res1]

    nc2 = _program("p2", build_phase2_program)
    res2 = run_bass_kernel_spmd(nc2, phase2_inmaps(inputs, streams, abs_),
                                core_ids=cores).results
    streams = [r["stream_out"] for r in res2]
    abs_ = [r["ab_out"] for r in res2]

    nc3a = _program("p3a", build_phase3a_program)
    res3a = run_bass_kernel_spmd(nc3a, phase3a_inmaps(inputs, streams, abs_),
                                 core_ids=cores).results
    qk = [r["xa_qk"] for r in res3a]
    vv = [r["v_tm"] for r in res3a]
    st = [r["stats"] for r in res3a]

    nc3b = _program("p3b", build_phase3b_program)
    res3b = run_bass_kernel_spmd(nc3b, phase3b_inmaps(inputs, streams, abs_, qk, vv, st),
                                 core_ids=cores).results
    streams = [r["stream_out"] for r in res3b]
    abs_ = [r["ab_out"] for r in res3b]

    nc5 = _program("p5", build_phase5_program)
    res5 = run_bass_kernel_spmd(nc5, phase5_inmaps(inputs, streams, abs_),
                                core_ids=cores).results

    out = np.empty((4, 8192, 1024), np.float32)
    for c in range(8):
        b, h = c // 2, c % 2
        out[b, h * TOK:(h + 1) * TOK] = res5[c]["out"]
    return out


# version 04
# speedup vs baseline: 1.2773x; 1.2773x over previous
import sys
if "/opt/trn_rl_repo" not in sys.path:
    sys.path.insert(0, "/opt/trn_rl_repo")
from contextlib import ExitStack

import numpy as np
import ml_dtypes

import concourse.bass as bass
import concourse.mybir as mybir
from concourse.bass_utils import run_bass_kernel_spmd

F32 = mybir.dt.float32
BF16 = mybir.dt.bfloat16
AF = mybir.ActivationFunctionType
ALU = mybir.AluOpType
AX = mybir.AxisListType

TOK = 4096
D = 1024
NB = TOK // 128
NT = TOK // 512
ENGINES = ("sync", "gpsimd", "vector", "scalar", "tensor")
DEBUG_P1 = False
A3_CUT = set()
B3_CUT = set()


def const_inputs():
    ident = np.eye(128, dtype=np.float32)
    tril = np.tril(np.ones((128, 128), bool))
    trif = np.where(tril, np.float32(0.0), np.float32(-1e30)).astype(np.float32)
    return {
        "c_identbf": ident.astype(ml_dtypes.bfloat16),
        "c_identf": ident,
        "trif": trif,
    }


class Sem:

    def __init__(self, stack, nc, name):
        self.h = stack.enter_context(nc.semaphore(name))
        self.v = 0


class Prog:

    def __init__(self, nc, stack):
        self.nc = nc
        self.stack = stack
        self.steps = {e: [] for e in ENGINES}
        self.hwm = {e: {} for e in ENGINES}
        self.sems = {}
        self.common = None

    def sem(self, name):
        if name not in self.sems:
            self.sems[name] = Sem(self.stack, self.nc, name)
        return self.sems[name]

    def emit(self, eng, fn, waits=(), inc=None, k=1):
        real_waits = []
        hw = self.hwm[eng]
        for s, v in waits:
            if v is None or v <= 0:
                continue
            if hw.get(s, 0) >= v:
                continue
            hw[s] = v
            real_waits.append((s.h, v))
        incs = ()
        ret = None
        if inc is not None:
            inc.v += k
            ret = inc.v
            incs = ((inc.h, k),)
        self.steps[eng].append((fn, tuple(real_waits), incs))
        return ret

    def barrier(self):
        vals = [(s, s.v) for s in self.sems.values() if s.v > 0]
        for e in ENGINES:
            for s, v in vals:
                if self.hwm[e].get(s, 0) < v:
                    self.hwm[e][s] = v
                    self.steps[e].append((None, ((s.h, v),), ()))

    def build(self, block):
        def section(steps):
            def body(eng):
                for fn, waits, incs in steps:
                    for h, v in waits:
                        eng.wait_ge(h, v)
                    if fn is None:
                        continue
                    inst = fn(eng)
                    for h, k in incs:
                        inst.then_inc(h, k)
            return body

        for name in ENGINES:
            getattr(block, name)(section(self.steps[name]))


class Common:

    def __init__(self, p, stack, need_stg=True, need_ones512=True):
        nc = p.nc
        self.identbf = stack.enter_context(nc.sbuf_tensor("identbf", [128, 128], BF16))
        self.identf = stack.enter_context(nc.sbuf_tensor("identf", [128, 128], F32))
        self.ident_in_bf = nc.dram_tensor("c_identbf", [128, 128], BF16,
                                          kind="ExternalInput")
        self.ident_in_f = nc.dram_tensor("c_identf", [128, 128], F32,
                                         kind="ExternalInput")
        self.ones_row = stack.enter_context(nc.sbuf_tensor("ones_row", [1, 512], F32))
        self.onescol = stack.enter_context(nc.sbuf_tensor("onescol", [128, 1], F32))
        self.ones512 = (stack.enter_context(nc.sbuf_tensor("ones512", [128, 512], F32))
                        if need_ones512 else None)
        self.stg = (stack.enter_context(nc.sbuf_tensor("wstg", [128, 4096], BF16))
                    if need_stg else None)
        self.stg_free = [0, 0]
        sIN = p.sem("sIN")
        sGE = p.sem("sGE")
        sVE = p.sem("sVE")
        p.emit("sync", lambda e: e.dma_start(self.identbf[:, :], self.ident_in_bf.ap()),
               inc=sIN, k=16)
        p.emit("sync", lambda e: e.dma_start(self.identf[:, :], self.ident_in_f.ap()),
               inc=sIN, k=16)
        self.ready_in = sIN.v
        p.emit("vector", lambda e: e.memset(self.ones_row[:, :], 1.0), inc=sVE)
        p.emit("vector", lambda e: e.memset(self.onescol[:, :], 1.0), inc=sVE)
        if need_ones512:
            p.emit("vector", lambda e: e.memset(self.ones512[:, :], 1.0), inc=sVE)
        self.ready_ge = sGE.v
        self.ready_ve = sVE.v
        self.sIN_sem = sIN


def stream_transpose_weight(p, stack, name, dram, rows, cols, pT, pT_free,
                            prep_ready=None, dst=None):
    nc = p.nc
    com = p.common
    sTE, sVE = p.sem("sTE"), p.sem("sVE")
    sIN = p.sem("sIN")
    sST = [p.sem("sST0"), p.sem("sST1")]
    nrc = rows // 128
    ncc = cols // 128
    assert cols <= 2048
    if dst is None:
        dst = stack.enter_context(nc.sbuf_tensor(name, [128, nrc * cols], BF16))
    last = 0
    idready = prep_ready if prep_ready is not None else (sIN, com.ready_in)
    for a in range(nrc):
        half = a % 2
        lv = p.emit(
            "gpsimd",
            (lambda a, half: lambda e: e.dma_start(
                com.stg[:, half * 2048: half * 2048 + cols],
                dram.ap()[a * 128:(a + 1) * 128, :]))(a, half),
            waits=[(sTE, com.stg_free[half])], inc=sST[half], k=16)
        for kk in range(ncc):
            slot = (a * ncc + kk) % 2
            waits = [(sST[half], lv), idready]
            if pT_free[slot]:
                waits.append((sVE, pT_free[slot]))
            tv = p.emit(
                "tensor",
                (lambda half, kk, slot: lambda e: e.transpose(
                    pT[slot][:, :],
                    com.stg[:, half * 2048 + kk * 128: half * 2048 + (kk + 1) * 128],
                    com.identbf[:, :]))(half, kk, slot),
                waits=waits, inc=sTE)
            if kk == ncc - 1:
                com.stg_free[half] = tv
            pT_free[slot] = last = p.emit(
                "vector",
                (lambda a, kk, slot: lambda e: e.tensor_copy(
                    dst[:, kk * rows + a * 128: kk * rows + (a + 1) * 128],
                    pT[slot][:, :]))(a, kk, slot),
                waits=[(sTE, tv)], inc=sVE)
    return dst, last


def columnize(p, row_ap_fn, n, dst, dst_col0, psum, guard, waits=()):
    sTE, sVE = p.sem("sTE"), p.sem("sVE")
    for i in range(n):
        w = list(waits)
        if guard[0]:
            w.append((sVE, guard[0]))
        tv = p.emit(
            "tensor",
            (lambda i: lambda e: e.matmul(
                psum[:, 0:1], row_ap_fn(i), p.common.ones_row[0:1, 0:1],
                start=True, stop=True))(i),
            waits=w, inc=sTE)
        guard[0] = p.emit(
            "vector",
            (lambda i: lambda e: e.tensor_copy(dst[:, dst_col0 + i:dst_col0 + i + 1],
                                               psum[:, 0:1]))(i),
            waits=[(sTE, tv)], inc=sVE)
    return guard[0]


def emit_ab(p, stack, ab_dram, st1, st2, stats_ready, pfx="", store_sem=None):
    nc = p.nc
    sVE, sSE, sOUT = p.sem("sVE"), p.sem("sSE"), p.sem("sOUT")
    if store_sem is None:
        store_sem = sOUT
    ab = stack.enter_context(nc.sbuf_tensor(pfx + "ab_sb", [128, 2 * NB], F32))
    tmp = stack.enter_context(nc.sbuf_tensor(pfx + "ab_tmp", [128, NB], F32))
    var = stack.enter_context(nc.sbuf_tensor(pfx + "ab_var", [128, NB], F32))
    mean = stack.enter_context(nc.sbuf_tensor(pfx + "ab_mean", [128, NB], F32))
    eps = stack.enter_context(nc.sbuf_tensor(pfx + "ab_eps", [128, 1], F32))
    inv = 1.0 / D
    p.emit("vector", lambda e: e.memset(eps[:, :], 1e-5), inc=sVE)
    v = p.emit("vector", lambda e: e.tensor_scalar(mean[:, :], st1[:, :], inv, None,
                                                   op0=ALU.mult),
               waits=stats_ready, inc=sVE)
    v = p.emit("vector", lambda e: e.tensor_scalar(var[:, :], st2[:, :], inv, None,
                                                   op0=ALU.mult), waits=[(sVE, v)], inc=sVE)
    v = p.emit("vector", lambda e: e.tensor_tensor(tmp[:, :], mean[:, :], mean[:, :],
                                                   op=ALU.mult), waits=[(sVE, v)], inc=sVE)
    v = p.emit("vector", lambda e: e.tensor_tensor(var[:, :], var[:, :], tmp[:, :],
                                                   op=ALU.subtract),
               waits=[(sVE, v)], inc=sVE)
    s = p.emit("scalar", lambda e: e.activation(tmp[:, :], var[:, :], AF.Ln,
                                                bias=eps[:, 0:1]),
               waits=[(sVE, v)], inc=sSE)
    v = p.emit("vector", lambda e: e.tensor_scalar(tmp[:, :], tmp[:, :], -0.5, None,
                                                   op0=ALU.mult),
               waits=[(sSE, s)], inc=sVE)
    s = p.emit("scalar", lambda e: e.activation(ab[:, 0:NB], tmp[:, :], AF.Exp),
               waits=[(sVE, v)], inc=sSE)
    v = p.emit("vector",
               lambda e: e.scalar_tensor_tensor(ab[:, NB:2 * NB], mean[:, :], -1.0,
                                                ab[:, 0:NB], op0=ALU.mult,
                                                op1=ALU.mult),
               waits=[(sSE, s)], inc=sVE)
    p.emit("gpsimd", lambda e: e.dma_start(ab_dram.ap()[:, :], ab[:, :]),
           waits=[(sVE, v)], inc=store_sem, k=16)



def build_l0_attn(p, stack, io):
    nc = p.nc
    com = p.common
    sIN, sWG = p.sem("sIN"), p.sem("sWG")
    sTE, sVE, sSE = p.sem("sTE"), p.sem("sVE"), p.sem("sSE")
    sOUT, sGE = p.sem("sOUT"), p.sem("sGE")

    cols = stack.enter_context(nc.sbuf_tensor("cols", [128, 4 * NB], F32))
    s_own_c, s_pre_c = cols[:, 0:NB], cols[:, NB:2 * NB]
    e_own_c, e_pre_c = cols[:, 2 * NB:3 * NB], cols[:, 3 * NB:4 * NB]
    cm = stack.enter_context(nc.sbuf_tensor("colmisc", [128, 8], F32))
    zm_col, moff_col, negm_col, b2c128 = cm[:, 0:1], cm[:, 1:2], cm[:, 2:3], cm[:, 3:4]
    mcol_own, mcol_pre, tcol = cm[:, 4:5], cm[:, 5:6], cm[:, 6:7]
    misc = stack.enter_context(nc.sbuf_tensor("miscrow", [1, 144], F32))
    negm, dc, zm, b2s = misc[0:1, 0:1], misc[0:1, 1:2], misc[0:1, 2:3], misc[0:1, 3:4]
    mrow = misc[0:1, 16:144]
    xtm = [stack.enter_context(nc.sbuf_tensor(f"xtm{i}", [128, D], F32)) for i in range(2)]
    xbf = [stack.enter_context(nc.sbuf_tensor(f"xbf{i}", [128, D], BF16)) for i in range(2)]
    hbf = [stack.enter_context(nc.sbuf_tensor(f"hbf{i}", [128, 512], BF16)) for i in range(2)]
    sXT = [p.sem("sXT0"), p.sem("sXT1")]
    sXS = [p.sem("sXS0"), p.sem("sXS1")]
    sAB = p.sem("sAB")
    b1c = stack.enter_context(nc.sbuf_tensor("b1c", [128, 8], F32))
    w2c = stack.enter_context(nc.sbuf_tensor("w2c", [128, 8], BF16))
    wtm = stack.enter_context(nc.sbuf_tensor("wtm", [128, D], F32))
    carry = stack.enter_context(nc.sbuf_tensor("carry", [128, 8], F32))
    bc_b = stack.enter_context(nc.sbuf_tensor("bc_b", [128, D], F32))
    st1 = stack.enter_context(nc.sbuf_tensor("st1", [128, NB], F32))
    st2 = stack.enter_context(nc.sbuf_tensor("st2", [128, NB], F32))
    x_fm = stack.enter_context(nc.sbuf_tensor("x_fm", [128, 8 * TOK], BF16))
    wcT = stack.enter_context(nc.sbuf_tensor("wcT", [128, 16 * D], BF16))

    pB = stack.enter_context(nc.psum_tensor("pB", [128, 512], F32))

    with ExitStack() as ps1:
        pT = [ps1.enter_context(nc.psum_tensor(f"pT{i}", [128, 128], BF16))
              for i in range(2)]
        pH = [ps1.enter_context(nc.psum_tensor(f"pH{i}", [128, 512], F32))
              for i in range(2)]
        pSc = ps1.enter_context(nc.psum_tensor("pSc", [128, 4], F32))
        pCa = ps1.enter_context(nc.psum_tensor("pCa", [128, 8], F32))
        pT_free = [0, 0]

        x_fm_pre = ps1.enter_context(nc.sbuf_tensor("x_fm_pre", [128, 8 * 512], BF16))
        b1row = ps1.enter_context(nc.sbuf_tensor("b1row", [1, D], F32))
        w2row = ps1.enter_context(nc.sbuf_tensor("w2row", [1, D], F32))
        bcrow = ps1.enter_context(nc.sbuf_tensor("bcrow", [1, D], F32))
        vin = 0
        for name, ap in [("b1", b1row[:, :]), ("w2", w2row[:, :]),
                         ("bc", bcrow[:, :]), ("b2", b2s), ("zmask", zm)]:
            vin = p.emit("sync",
                         (lambda name, ap: lambda e: e.dma_start(ap, io[name].ap()))(name, ap),
                         inc=sIN, k=16)

        _, wcT_ready = stream_transpose_weight(p, stack, "wcT", io["Wc"], D, 2 * D,
                                               pT, pT_free, prep_ready=(sIN, vin),
                                               dst=wcT)
        w1T, w1T_ready = stream_transpose_weight(p, ps1, "w1T", io["W1"], D, D,
                                                 pT, pT_free, prep_ready=(sIN, vin))
        guard = [0]
        columnize(p, lambda i: b1row[0:1, i * 128:(i + 1) * 128], 8, b1c, 0,
                  pB, guard, waits=[(sIN, vin), (sVE, com.ready_ve)])
        b1c_ready = guard[0]
        columnize(p, lambda i: w2row[0:1, i * 128:(i + 1) * 128], 8, w2c, 0,
                  pB, guard, waits=[(sIN, vin)])
        w2c_ready = guard[0]
        for src, dstc in [(zm, zm_col), (b2s, b2c128)]:
            tv = p.emit("tensor",
                        (lambda src: lambda e: e.matmul(
                            pB[:, 0:1], com.ones_row[0:1, 0:128], src,
                            start=True, stop=True))(src),
                        waits=[(sIN, vin), (sVE, guard[0])], inc=sTE)
            guard[0] = p.emit("vector",
                              (lambda dstc: lambda e: e.tensor_copy(dstc, pB[:, 0:1]))(dstc),
                              waits=[(sTE, tv)], inc=sVE)
        p.emit("vector", lambda e: e.tensor_scalar(moff_col, zm_col, -1.0, 1e30,
                                                   op0=ALU.add, op1=ALU.mult),
               inc=sVE)
        for hh in range(2):
            tv = p.emit("tensor",
                        (lambda hh: lambda e: e.matmul(
                            pB[:, :], com.ones_row[0:1, 0:128],
                            bcrow[0:1, hh * 512:(hh + 1) * 512],
                            start=True, stop=True))(hh),
                        waits=[(sIN, vin), (sVE, guard[0])], inc=sTE)
            guard[0] = p.emit("vector",
                              (lambda hh: lambda e: e.tensor_copy(
                                  bc_b[:, hh * 512:(hh + 1) * 512], pB[:, :]))(hh),
                              waits=[(sTE, tv)], inc=sVE)

        xtm_free = [0, 0]
        xbf_free = [0, 0]
        pH_free = [0, 0]
        hbf_free = [0, 0]
        fm_free = [0]
        pSc_evac = [0]

        def pass1(src_dram, fm_dst, fm_off_fn, s_dst, reuse_fm):
            for i in range(NT):
                for b4 in range(4):
                    b = 4 * i + b4
                    slot = b % 2
                    lv = p.emit("sync",
                                (lambda b, slot: lambda e: e.dma_start(
                                    xtm[slot][:, :],
                                    src_dram.ap()[b * 128:(b + 1) * 128, :]))(b, slot),
                                waits=[(sVE, xtm_free[slot])], inc=sXT[slot], k=16)
                    cv = p.emit("vector",
                                (lambda slot: lambda e: e.tensor_copy(
                                    xbf[slot][:, :], xtm[slot][:, :]))(slot),
                                waits=[(sXT[slot], lv), (sTE, xbf_free[slot])], inc=sVE)
                    xtm_free[slot] = cv
                    for k in range(8):
                        ts = (b * 8 + k) % 2
                        tv = p.emit("tensor",
                                    (lambda slot, k, ts: lambda e: e.transpose(
                                        pT[ts][:, :], xbf[slot][:, k * 128:(k + 1) * 128],
                                        com.identbf[:, :]))(slot, k, ts),
                                    waits=[(sVE, cv), (sVE, pT_free[ts]),
                                           (sIN, com.ready_in)], inc=sTE)
                        if k == 7:
                            xbf_free[slot] = tv
                        o0 = fm_off_fn(k, b)
                        pT_free[ts] = p.emit(
                            "vector",
                            (lambda o0, ts: lambda e: e.tensor_copy(
                                fm_dst[:, o0:o0 + 128], pT[ts][:, :]))(o0, ts),
                            waits=[(sTE, tv)] + ([(sTE, fm_free[0])] if reuse_fm else []),
                            inc=sVE)
                fm_ready = sVE.v
                for a in range(8):
                    hs = a % 2
                    tv = None
                    for k in range(8):
                        o0 = fm_off_fn(k, 4 * i)
                        tv = p.emit(
                            "tensor",
                            (lambda a, k, hs, o0: lambda e: e.matmul(
                                pH[hs][:, :],
                                w1T[:, k * D + a * 128: k * D + (a + 1) * 128],
                                fm_dst[:, o0:o0 + 512],
                                start=(k == 0), stop=(k == 7)))(a, k, hs, o0),
                            waits=[(sVE, fm_ready), (sVE, w1T_ready),
                                   (sSE, pH_free[hs])],
                            inc=(sTE if k == 7 else None))
                    av = p.emit("scalar",
                                (lambda a, hs: lambda e: e.activation(
                                    hbf[hs][:, :], pH[hs][:, :], AF.Tanh,
                                    bias=b1c[:, a:a + 1]))(a, hs),
                                waits=[(sTE, tv), (sTE, hbf_free[hs]),
                                       (sVE, b1c_ready)], inc=sSE)
                    pH_free[hs] = av
                    sv = None
                    for b4 in range(4):
                        sv = p.emit("tensor",
                                    (lambda a, hs, b4: lambda e: e.matmul(
                                        pSc[:, b4:b4 + 1], hbf[hs][:, b4 * 128:(b4 + 1) * 128],
                                        w2c[:, a:a + 1],
                                        start=(a == 0 and b4 == 0),
                                        stop=(a == 7 and b4 == 3)))(a, hs, b4),
                                    waits=[(sSE, av), (sVE, pSc_evac[0]),
                                           (sVE, w2c_ready)],
                                    inc=(sTE if b4 == 3 else None))
                    hbf_free[hs] = sv
                    if a == 7:
                        for b4 in range(4):
                            pSc_evac[0] = p.emit(
                                "vector",
                                (lambda i, b4: lambda e: e.tensor_scalar(
                                    s_dst[:, 4 * i + b4: 4 * i + b4 + 1],
                                    pSc[:, b4:b4 + 1], b2c128, None,
                                    op0=ALU.add))(i, b4),
                                waits=[(sTE, sv), (sVE, guard[0])], inc=sVE)
                if reuse_fm:
                    fm_free[0] = sTE.v

        pass1(io["x_own"], x_fm, lambda k, b: k * TOK + b * 128, s_own_c, False)
        pass1(io["x_pre"], x_fm_pre, lambda k, b: k * 512 + (b % 4) * 128, s_pre_c, True)

        p.emit("vector", lambda e: e.reduce_max(mcol_own, s_own_c, axis=AX.X),
               waits=[(sVE, pSc_evac[0])], inc=sVE)
        p.emit("vector", lambda e: e.reduce_max(mcol_pre, s_pre_c, axis=AX.X), inc=sVE)
        p.emit("vector", lambda e: e.tensor_scalar(mcol_pre, mcol_pre, zm_col, moff_col,
                                                   op0=ALU.mult, op1=ALU.add), inc=sVE)
        v0 = p.emit("vector", lambda e: e.tensor_tensor(tcol, mcol_own, mcol_pre,
                                                        op=ALU.max), inc=sVE)
        tv = p.emit("tensor", lambda e: e.transpose(pB[0:1, 0:128], tcol, com.identf[:, :]),
                    waits=[(sVE, v0)], inc=sTE)
        v1 = p.emit("vector", lambda e: e.tensor_copy(mrow, pB[0:1, 0:128]),
                    waits=[(sTE, tv)], inc=sVE)
        v2 = p.emit("vector", lambda e: e.reduce_max(negm, mrow, axis=AX.X, negate=True),
                    inc=sVE)
        tv = p.emit("tensor", lambda e: e.matmul(pB[:, 0:1], com.ones_row[0:1, 0:128],
                                                 negm, start=True, stop=True),
                    waits=[(sVE, v2)], inc=sTE)
        v3 = p.emit("vector", lambda e: e.tensor_copy(negm_col, pB[:, 0:1]),
                    waits=[(sTE, tv)], inc=sVE)
        e1 = p.emit("scalar", lambda e: e.activation(e_own_c, s_own_c, AF.Exp,
                                                     bias=negm_col),
                    waits=[(sVE, v3)], inc=sSE)
        e2 = p.emit("scalar", lambda e: e.activation(e_pre_c, s_pre_c, AF.Exp,
                                                     bias=negm_col),
                    waits=[(sVE, v3)], inc=sSE)
        e3 = p.emit("vector", lambda e: e.tensor_scalar(e_pre_c, e_pre_c, zm_col, None,
                                                        op0=ALU.mult),
                    waits=[(sSE, e2)], inc=sVE)
        v4 = p.emit("vector", lambda e: e.reduce_sum(tcol, e_pre_c, axis=AX.X),
                    waits=[(sVE, e3)], inc=sVE)
        tv = p.emit("tensor", lambda e: e.transpose(pB[0:1, 0:128], tcol, com.identf[:, :]),
                    waits=[(sVE, v4)], inc=sTE)
        v5 = p.emit("vector", lambda e: e.tensor_copy(mrow, pB[0:1, 0:128]),
                    waits=[(sTE, tv)], inc=sVE)
        p.emit("vector", lambda e: e.reduce_sum(dc, mrow, axis=AX.X), inc=sVE)

        if DEBUG_P1:
            p.emit("gpsimd", lambda e: e.dma_start(io["dbg_cols"].ap()[:, :], cols[:, :]),
                   waits=[(sVE, sVE.v), (sSE, sSE.v)], inc=sOUT, k=16)
            p.emit("gpsimd", lambda e: e.dma_start(io["dbg_misc"].ap()[:, :], misc[:, :]),
                   waits=[(sVE, sVE.v)], inc=sOUT, k=16)

        wtm_free = 0
        for b in range(NB):
            slot = b % 2
            lv = p.emit("sync",
                        (lambda b, slot: lambda e: e.dma_start(
                            xtm[slot][:, :], io["x_pre"].ap()[b * 128:(b + 1) * 128, :]))(b, slot),
                        waits=[(sVE, xtm_free[slot])], inc=sXT[slot], k=16)
            wv = p.emit("vector",
                        (lambda slot, b: lambda e: e.tensor_scalar(
                            wtm[:, :], xtm[slot][:, :], e_pre_c[:, b:b + 1], None,
                            op0=ALU.mult))(slot, b),
                        waits=[(sXT[slot], lv), (sVE, e3), (sTE, wtm_free)], inc=sVE)
            xtm_free[slot] = wv
            for c in range(8):
                v = p.emit(
                    "tensor",
                    (lambda b, c: lambda e: e.matmul(
                        pCa[:, c:c + 1], wtm[:, c * 128:(c + 1) * 128],
                        com.onescol[:, 0:1],
                        start=(b == 0 and c == 0),
                        stop=(b == NB - 1 and c == 7)))(b, c),
                    waits=[(sVE, wv)], inc=(sTE if c == 7 else None))
                if c == 7:
                    wtm_free = v
        cv = p.emit("vector", lambda e: e.tensor_copy(carry[:, :], pCa[:, :]),
                    waits=[(sTE, wtm_free)], inc=sVE)
        psum_clear = cv
        if DEBUG_P1:
            p.emit("gpsimd", lambda e: e.dma_start(io["dbg_carry"].ap()[:, :], carry[:, :]),
                   waits=[(sVE, cv)], inc=sOUT, k=16)

    with ExitStack() as ps2:
        ebc = ps2.enter_context(nc.sbuf_tensor("ebc", [128, 512], F32))
        rden = ps2.enter_context(nc.sbuf_tensor("rden", [128, 512], F32))
        wfm = ps2.enter_context(nc.sbuf_tensor("wfm", [128, 512], F32))
        num = ps2.enter_context(nc.sbuf_tensor("num", [128, 512], F32))
        ctxbf = ps2.enter_context(nc.sbuf_tensor("ctxbf", [128, 8 * 512], BF16))
        den_row = ps2.enter_context(nc.sbuf_tensor("den_row", [1, 512], F32))
        e_row = ps2.enter_context(nc.sbuf_tensor("e_row", [1, 512], F32))
        of32 = ps2.enter_context(nc.sbuf_tensor("of32", [128, D], F32))
        o2 = ps2.enter_context(nc.sbuf_tensor("o2", [128, D], F32))
        sqs = ps2.enter_context(nc.sbuf_tensor("sqs", [128, D], F32))
        xres = [ps2.enter_context(nc.sbuf_tensor(f"xres{i}", [128, D], F32))
                for i in range(2)]
        pO = ps2.enter_context(nc.psum_tensor("pO", [128, 1024], F32))
        pR = ps2.enter_context(nc.psum_tensor("pR", [1, 512], F32))
        if DEBUG_P1:
            dbg_c8 = ps2.enter_context(nc.sbuf_tensor("dbg_c8_sb", [128, 72], F32))
        xres_free = [0, 0]
        pB_free = sVE.v
        pO_free = psum_clear
        ctx_free = 0
        er_ev = 0
        den_mm = 0
        xtm_free2 = xtm_free
        for i in range(NT):
            tvs = None
            for b4 in range(4):
                tvs = p.emit("tensor",
                             (lambda i, b4: lambda e: e.transpose(
                                 pR[0:1, b4 * 128:(b4 + 1) * 128],
                                 e_own_c[:, 4 * i + b4:4 * i + b4 + 1],
                                 com.identf[:, :]))(i, b4),
                             waits=[(sSE, e1), (sVE, er_ev)], inc=sTE)
            erv = p.emit("vector", lambda e: e.tensor_copy(e_row[0:1, :], pR[0:1, :]),
                         waits=[(sTE, tvs), (sTE, den_mm)], inc=sVE)
            er_ev = erv
            tv = p.emit("tensor",
                        lambda e: e.matmul(pB[:, :], com.ones_row[0:1, 0:128],
                                           e_row[0:1, :], start=True, stop=True),
                        waits=[(sVE, erv), (sVE, pB_free)], inc=sTE)
            bv = p.emit("vector", lambda e: e.tensor_copy(ebc[:, :], pB[:, :]),
                        waits=[(sTE, tv)], inc=sVE)
            dv = p.emit("vector",
                        (lambda i: lambda e: e.tensor_tensor_scan(
                            den_row[0:1, :], com.ones512[0:1, :],
                            e_row[0:1, :],
                            dc, op0=ALU.mult, op1=ALU.add))(i),
                        waits=[(sVE, erv)], inc=sVE)
            p.emit("vector", lambda e: e.tensor_copy(dc, den_row[0:1, 511:512]),
                   waits=[(sVE, dv)], inc=sVE)
            den_mm = p.emit("tensor",
                            lambda e: e.matmul(pB[:, :], com.ones_row[0:1, 0:128],
                                               den_row[0:1, :], start=True, stop=True),
                            waits=[(sVE, dv), (sVE, bv)], inc=sTE)
            rv = p.emit("vector", lambda e: e.reciprocal(rden[:, :], pB[:, :]),
                        waits=[(sTE, den_mm)], inc=sVE)
            pB_free = rv
            for k in range(8):
                p.emit("vector",
                       (lambda i, k: lambda e: e.tensor_tensor(
                           wfm[:, :],
                           x_fm[:, k * TOK + i * 512: k * TOK + i * 512 + 512],
                           ebc[:, :], op=ALU.mult))(i, k),
                       waits=[(sVE, bv)], inc=sVE)
                sv_k = p.emit("vector",
                              (lambda k: lambda e: e.tensor_tensor_scan(
                                  num[:, :], com.ones512[:, :], wfm[:, :],
                                  carry[:, k:k + 1], op0=ALU.mult, op1=ALU.add))(k),
                              inc=sVE)
                p.emit("vector",
                       (lambda k: lambda e: e.tensor_copy(
                           carry[:, k:k + 1], num[:, 511:512]))(k),
                       waits=[(sVE, sv_k)], inc=sVE)
                p.emit("vector",
                       (lambda k: lambda e: e.tensor_tensor(
                           ctxbf[:, k * 512:(k + 1) * 512], num[:, :],
                           rden[:, :], op=ALU.mult))(k),
                       waits=[(sVE, rv), (sTE, ctx_free)], inc=sVE)
            ctx_ready = sVE.v
            if DEBUG_P1:
                p.emit("vector",
                       (lambda i: lambda e: e.tensor_copy(
                           dbg_c8[:, i * 8:(i + 1) * 8], carry[:, :]))(i),
                       inc=sVE)
                p.emit("vector",
                       (lambda i: lambda e: e.tensor_copy(
                           dbg_c8[0:1, 64 + i:64 + i + 1], dc))(i),
                       inc=sVE)
            if DEBUG_P1 and i == 0:
                dctx = ps2.enter_context(nc.sbuf_tensor("dctx", [128, 8 * 512], BF16))
                drd = ps2.enter_context(nc.sbuf_tensor("drd", [128, 512], F32))
                dv1 = p.emit("vector", lambda e: e.tensor_copy(dctx[:, :], ctxbf[:, :]),
                             waits=[(sVE, ctx_ready)], inc=sVE)
                dv2 = p.emit("vector", lambda e: e.tensor_copy(drd[:, :], rden[:, :]),
                             inc=sVE)
                p.emit("gpsimd", lambda e: e.dma_start(io["dbg_ctx"].ap()[:, :], dctx[:, :]),
                       waits=[(sVE, dv1)], inc=sOUT, k=16)
                p.emit("gpsimd", lambda e: e.dma_start(io["dbg_rden"].ap()[:, :], drd[:, :]),
                       waits=[(sVE, dv2)], inc=sOUT, k=16)
            for b4 in range(4):
                b = 4 * i + b4
                slot = b % 2
                lv = p.emit("sync",
                            (lambda b, slot: lambda e: e.dma_start(
                                xtm[slot][:, :],
                                io["x_own"].ap()[b * 128:(b + 1) * 128, :]))(b, slot),
                            waits=[(sVE, xtm_free2[slot])], inc=sXT[slot], k=16)
                last_mm = None
                for dh in range(2):
                    for c in range(8):
                        p.emit("tensor",
                               (lambda b4, dh, c: lambda e: e.matmul(
                                   pO[:, dh * 512:(dh + 1) * 512],
                                   ctxbf[:, c * 512 + b4 * 128: c * 512 + b4 * 128 + 128],
                                   wcT[:, c * D + dh * 512: c * D + dh * 512 + 512],
                                   start=(c == 0), stop=False))(b4, dh, c),
                               waits=[(sVE, ctx_ready), (sVE, wcT_ready),
                                      (sVE, pO_free)])
                        last_mm = p.emit(
                            "tensor",
                            (lambda b, dh, c: lambda e: e.matmul(
                                pO[:, dh * 512:(dh + 1) * 512],
                                x_fm[:, c * TOK + b * 128: c * TOK + b * 128 + 128],
                                wcT[:, (8 + c) * D + dh * 512: (8 + c) * D + dh * 512 + 512],
                                start=False, stop=(c == 7)))(b, dh, c),
                            inc=(sTE if (c == 7 and dh == 1) else None))
                ctx_free = last_mm
                ov = p.emit("vector",
                            lambda e: e.tensor_tensor(of32[:, :], pO[:, :], bc_b[:, :],
                                                      op=ALU.add),
                            waits=[(sTE, last_mm)], inc=sVE)
                pO_free = ov
                av = p.emit("scalar",
                            lambda e: e.activation(o2[:, :], of32[:, :], AF.Tanh),
                            waits=[(sVE, ov)], inc=sSE)
                if DEBUG_P1 and b == 0:
                    do2 = ps2.enter_context(nc.sbuf_tensor("do2", [128, D], F32))
                    dv3 = p.emit("vector", lambda e: e.tensor_copy(do2[:, :], o2[:, :]),
                                 waits=[(sSE, av)], inc=sVE)
                    p.emit("gpsimd", lambda e: e.dma_start(io["dbg_o2"].ap()[:, :],
                                                           do2[:, :]),
                           waits=[(sVE, dv3)], inc=sOUT, k=16)
                    p.emit("gpsimd", lambda e: e.dma_start(io["dbg_wct"].ap()[:, :],
                                                           wcT[:, :]),
                           waits=[(sVE, wcT_ready)], inc=sOUT, k=16)
                    p.emit("gpsimd", lambda e: e.dma_start(io["dbg_bcb"].ap()[:, :],
                                                           bc_b[:, :]),
                           waits=[(sVE, sVE.v)], inc=sOUT, k=16)
                xv = p.emit("vector",
                            (lambda slot: lambda e: e.tensor_tensor(
                                xres[slot][:, :], o2[:, :], xtm[slot][:, :],
                                op=ALU.add))(slot),
                            waits=[(sSE, av), (sXT[slot], lv),
                                   (sXS[slot], xres_free[slot])],
                            inc=sVE)
                xtm_free2[slot] = xv
                p.emit("vector",
                       (lambda slot, b: lambda e: e.reduce_sum(
                           st1[:, b:b + 1], xres[slot][:, :], axis=AX.X))(slot, b),
                       waits=[(sVE, xv)], inc=sVE)
                sqv = p.emit("scalar",
                             (lambda slot, b: lambda e: e.activation(
                                 sqs[:, :], xres[slot][:, :], AF.Square,
                                 accum_out=st2[:, b:b + 1]))(slot, b),
                             waits=[(sVE, xv)], inc=sSE)
                xres_free[slot] = p.emit(
                    "gpsimd",
                    (lambda slot, b: lambda e: e.dma_start(
                        io["stream_out"].ap()[b * 128:(b + 1) * 128, :],
                        xres[slot][:, :]))(slot, b),
                    waits=[(sVE, xv), (sSE, sqv)], inc=sXS[slot], k=16)

        if DEBUG_P1:
            p.emit("gpsimd", lambda e: e.dma_start(io["dbg_c8"].ap()[:, :], dbg_c8[:, :]),
                   waits=[(sVE, sVE.v)], inc=sAB, k=16)

    emit_ab(p, stack, io["ab_out"], st1, st2,
            stats_ready=[(sVE, sVE.v), (sSE, sSE.v)], pfx="p1", store_sem=sAB)

    p.emit("gpsimd", lambda e: e.engine_nop(),
           waits=[(sXS[0], sXS[0].v), (sXS[1], sXS[1].v), (sAB, sAB.v)])



def build_l0_ffn(p, stack, io):
    nc = p.nc
    com = p.common
    sIN, sWG = p.sem("sIN"), p.sem("sWG")
    sTE, sVE, sSE = p.sem("sTE"), p.sem("sVE"), p.sem("sSE")
    sOUT, sGE = p.sem("sOUT"), p.sem("sGE")
    E, H = 8, 2048
    NHC = H // 128
    MEGA = 1024
    NMEGA = TOK // MEGA
    TPM = MEGA // 512

    absb = stack.enter_context(nc.sbuf_tensor("f_absb", [128, 64], F32))
    gates = stack.enter_context(nc.sbuf_tensor("f_gates", [128, NB * 8], F32))
    gfm = stack.enter_context(nc.sbuf_tensor("f_gfm", [8, NB * 128], BF16))
    b1c = stack.enter_context(nc.sbuf_tensor("f_b1c", [128, NHC * 8], F32))
    b2bf = stack.enter_context(nc.sbuf_tensor("f_b2bf", [8, D], BF16))
    wgbf = stack.enter_context(nc.sbuf_tensor("f_wgbf", [128, 64], BF16))
    bgb = stack.enter_context(nc.sbuf_tensor("f_bgb", [128, 8], F32))
    gc = stack.enter_context(nc.sbuf_tensor("f_gcols", [128, 16], F32))
    (nmcol, escol, rscol, m1c, m2c, dnc, rgc) = [gc[:, j:j + 1] for j in range(7)]
    gsc = stack.enter_context(nc.sbuf_tensor("f_gsc", [128, 48], F32))
    lg, pr, mk1, im, p2t, mk2 = [gsc[:, j * 8:(j + 1) * 8] for j in range(6)]
    xtm = [stack.enter_context(nc.sbuf_tensor(f"f_xtm{i}", [128, D], F32)) for i in range(2)]
    xbf = stack.enter_context(nc.sbuf_tensor("f_xbf", [128, D], BF16))
    xfmb = stack.enter_context(nc.sbuf_tensor("f_xfmb", [128, 8 * 512], BF16))
    xfmt = [stack.enter_context(nc.sbuf_tensor(f"f_xfmt{i}", [128, 8 * 512], BF16))
            for i in range(2)]
    hsb = stack.enter_context(nc.sbuf_tensor("f_hsb", [128, 8 * 512], BF16))
    y_acc = stack.enter_context(nc.sbuf_tensor("f_yacc", [128, 8 * D], F32))
    w1sb = [stack.enter_context(nc.sbuf_tensor(f"f_w1sb{i}", [128, 8 * H], BF16))
            for i in range(2)]
    w2sb = stack.enter_context(nc.sbuf_tensor("f_w2sb", [128, NHC * D], BF16))
    xres = [stack.enter_context(nc.sbuf_tensor(f"f_xres{i}", [128, D], F32))
            for i in range(2)]
    st1 = stack.enter_context(nc.sbuf_tensor("f_st1", [128, NB], F32))
    st2 = stack.enter_context(nc.sbuf_tensor("f_st2", [128, NB], F32))

    pH = [stack.enter_context(nc.psum_tensor(f"f_pH{i}", [128, 512], F32))
          for i in range(2)]
    psB = ExitStack()
    pG = psB.enter_context(nc.psum_tensor("f_pG", [128, 8], F32))
    pTf = psB.enter_context(nc.psum_tensor("f_pTf", [128, 128], F32))
    pT = [psB.enter_context(nc.psum_tensor(f"f_pT{i}", [128, 128], BF16))
          for i in range(2)]
    sXT = [p.sem("sXT0"), p.sem("sXT1")]
    sXF = [p.sem("sXF0"), p.sem("sXF1")]
    sW1s = [p.sem("sW1a"), p.sem("sW1b")]
    sW2s = p.sem("sW2")
    sXS = [p.sem("sXS0"), p.sem("sXS1")]
    sSP = p.sem("sSP")
    sAB = p.sem("sAB")
    sB1 = p.sem("sB1")

    vin = p.emit("sync", lambda e: e.dma_start(absb[:, :], io["ab_in"].ap()),
                 inc=sIN, k=16)
    for a in range(8):
        p.emit("gpsimd",
               (lambda a: lambda e: e.dma_start(
                   wgbf[:, a * 8:(a + 1) * 8],
                   io["Wg"].ap()[a * 128:(a + 1) * 128, :]))(a),
               inc=sWG, k=16)
    p.emit("gpsimd", lambda e: e.dma_start(b2bf[:, :], io["b2"].ap()), inc=sWG, k=16)
    wg_ready = b2_ready = p.sem("sWG").v
    with ExitStack() as prep:
        bgrow = prep.enter_context(nc.sbuf_tensor("f_bgrow", [1, 8], F32))
        b1sb = prep.enter_context(nc.sbuf_tensor("f_b1sb", [8, H // 2], F32))
        v1 = p.emit("sync", lambda e: e.dma_start(bgrow[:, :], io["bg"].ap()),
                    inc=sIN, k=16)
        vin = v1
        tv = p.emit("tensor", lambda e: e.matmul(pG[:, :], com.ones_row[0:1, 0:128],
                                                 bgrow[0:1, :], start=True, stop=True),
                    waits=[(sIN, v1), (sVE, com.ready_ve)], inc=sTE)
        bg_ready = p.emit("vector", lambda e: e.tensor_copy(bgb[:, :], pG[:, :]),
                          waits=[(sTE, tv)], inc=sVE)
        b1c_ready = 0
        ptf_free = 0
        b1_used = 0
        for hh in range(2):
            v2 = p.emit("sync",
                        (lambda hh: lambda e: e.dma_start(
                            b1sb[:, :], io["b1"].ap()[:, hh * 1024:(hh + 1) * 1024]))(hh),
                        waits=[(sTE, b1_used)], inc=sB1, k=16)
            for mm in range(NHC // 2):
                m = hh * (NHC // 2) + mm
                b1_used = tv = p.emit(
                    "tensor",
                    (lambda mm: lambda e: e.transpose(
                        pTf[0:128, 0:8], b1sb[:, mm * 128:(mm + 1) * 128],
                        com.identf[0:8, 0:8]))(mm),
                    waits=[(sB1, v2), (sIN, vin), (sVE, ptf_free)],
                    inc=sTE)
                b1c_ready = ptf_free = p.emit(
                    "vector",
                    (lambda m: lambda e: e.tensor_copy(
                        b1c[:, m * 8:(m + 1) * 8], pTf[0:128, 0:8]))(m),
                    waits=[(sTE, tv)], inc=sVE)

        xtm_free = [0, 0]
        xbf_free = 0
        pT_free = [0, 0]
        xfmb_free = 0
        gfm_last = 0
        for i in range(NT):
            evs = []
            for b4 in range(4):
                b = 4 * i + b4
                slot = b % 2
                lv = p.emit("sync",
                            (lambda b, slot: lambda e: e.dma_start(
                                xtm[slot][:, :],
                                io["stream_in"].ap()[b * 128:(b + 1) * 128, :]))(b, slot),
                            waits=[(sVE, xtm_free[slot])], inc=sXT[slot], k=16)
                av = p.emit("vector",
                            (lambda b, slot: lambda e: e.tensor_scalar(
                                xtm[slot][:, :], xtm[slot][:, :],
                                absb[:, b:b + 1], absb[:, 32 + b:32 + b + 1],
                                op0=ALU.mult, op1=ALU.add))(b, slot),
                            waits=[(sXT[slot], lv), (sIN, vin)], inc=sVE)
                cv = p.emit("vector",
                            (lambda slot: lambda e: e.tensor_copy(
                                xbf[:, :], xtm[slot][:, :]))(slot),
                            waits=[(sVE, av), (sTE, xbf_free)], inc=sVE)
                xtm_free[slot] = cv
                for k in range(8):
                    ts = k % 2
                    tv = p.emit("tensor",
                                (lambda k, ts: lambda e: e.transpose(
                                    pT[ts][:, :], xbf[:, k * 128:(k + 1) * 128],
                                    com.identbf[:, :]))(k, ts),
                                waits=[(sVE, cv), (sVE, pT_free[ts])], inc=sTE)
                    if k == 7:
                        xbf_free = tv
                    pT_free[ts] = p.emit(
                        "vector",
                        (lambda k, b4, ts: lambda e: e.tensor_copy(
                            xfmb[:, k * 512 + b4 * 128: k * 512 + (b4 + 1) * 128],
                            pT[ts][:, :]))(k, b4, ts),
                        waits=[(sTE, tv), (sSP, xfmb_free)], inc=sVE)
                fm_b_ready = sVE.v
                gv = None
                for k in range(8):
                    gv = p.emit("tensor",
                                (lambda k, b4: lambda e: e.matmul(
                                    pG[:, :],
                                    xfmb[:, k * 512 + b4 * 128: k * 512 + (b4 + 1) * 128],
                                    wgbf[:, k * 8:(k + 1) * 8],
                                    start=(k == 0), stop=(k == 7)))(k, b4),
                                waits=[(sVE, fm_b_ready), (sWG, wg_ready),
                                       (sVE, bg_ready)],
                                inc=(sTE if k == 7 else None))
                vv = p.emit("vector", lambda e: e.tensor_tensor(lg, pG[:, :], bgb[:, :],
                                                                op=ALU.add),
                            waits=[(sTE, gv)], inc=sVE)
                vv = p.emit("vector", lambda e: e.tensor_scalar(lg, lg, 0.2, None,
                                                                op0=ALU.mult),
                            waits=[(sVE, vv)], inc=sVE)
                vv = p.emit("vector", lambda e: e.reduce_max(nmcol, lg, axis=AX.X,
                                                             negate=True),
                            waits=[(sVE, vv)], inc=sVE)
                sv = p.emit("scalar", lambda e: e.activation(pr, lg, AF.Exp,
                                                             bias=nmcol),
                            waits=[(sVE, vv)], inc=sSE)
                vv = p.emit("vector", lambda e: e.reduce_sum(escol, pr, axis=AX.X),
                            waits=[(sSE, sv)], inc=sVE)
                vv = p.emit("vector", lambda e: e.reciprocal(rscol, escol),
                            waits=[(sVE, vv)], inc=sVE)
                vv = p.emit("vector", lambda e: e.tensor_scalar(pr, pr, rscol, None,
                                                                op0=ALU.mult),
                            waits=[(sVE, vv)], inc=sVE)
                vv = p.emit("vector", lambda e: e.reduce_max(m1c, pr, axis=AX.X),
                            waits=[(sVE, vv)], inc=sVE)
                vv = p.emit("vector", lambda e: e.tensor_scalar(mk1, pr, m1c, None,
                                                                op0=ALU.is_ge),
                            waits=[(sVE, vv)], inc=sVE)
                vv = p.emit("vector", lambda e: e.tensor_scalar(im, mk1, -1.0, 1.0,
                                                                op0=ALU.mult,
                                                                op1=ALU.add),
                            waits=[(sVE, vv)], inc=sVE)
                vv = p.emit("vector", lambda e: e.tensor_tensor(p2t, pr, im,
                                                                op=ALU.mult),
                            waits=[(sVE, vv)], inc=sVE)
                vv = p.emit("vector", lambda e: e.reduce_max(m2c, p2t, axis=AX.X),
                            waits=[(sVE, vv)], inc=sVE)
                vv = p.emit("vector", lambda e: e.tensor_scalar(mk2, p2t, m2c, None,
                                                                op0=ALU.is_ge),
                            waits=[(sVE, vv)], inc=sVE)
                vv = p.emit("vector", lambda e: e.tensor_tensor(dnc, m1c, m2c,
                                                                op=ALU.add),
                            waits=[(sVE, vv)], inc=sVE)
                vv = p.emit("vector", lambda e: e.tensor_scalar(dnc, dnc, 1e-9, None,
                                                                op0=ALU.add),
                            waits=[(sVE, vv)], inc=sVE)
                vv = p.emit("vector", lambda e: e.reciprocal(rgc, dnc),
                            waits=[(sVE, vv)], inc=sVE)
                vv = p.emit("vector", lambda e: e.tensor_tensor(mk1, mk1, mk2,
                                                                op=ALU.add),
                            waits=[(sVE, vv)], inc=sVE)
                vv = p.emit("vector", lambda e: e.tensor_tensor(mk1, pr, mk1,
                                                                op=ALU.mult),
                            waits=[(sVE, vv)], inc=sVE)
                vv = p.emit("vector",
                            (lambda b: lambda e: e.tensor_scalar(
                                gates[:, b * 8:(b + 1) * 8], mk1, rgc, None,
                                op0=ALU.mult))(b),
                            waits=[(sVE, vv)], inc=sVE)
                tv = p.emit("tensor",
                            (lambda b: lambda e: e.transpose(
                                pTf[0:8, :], gates[:, b * 8:(b + 1) * 8],
                                com.identf[:, :]))(b),
                            waits=[(sVE, vv), (sVE, ptf_free)], inc=sTE)
                gfm_last = ptf_free = p.emit(
                    "vector",
                    (lambda b: lambda e: e.tensor_copy(
                        gfm[0:8, b * 128:(b + 1) * 128], pTf[0:8, 0:128]))(b),
                    waits=[(sTE, tv)], inc=sVE)
            xfmb_free = p.emit(
                "gpsimd",
                (lambda i: lambda e: e.dma_start(
                    bass.AP(io["xfm"], i * 512,
                            [[TOK, 128], [128 * TOK, 8], [1, 512]]),
                    xfmb[:, :].rearrange("p (k t) -> p k t", k=8)))(i),
                waits=[(sVE, sVE.v)], inc=sSP, k=16)

    psB.close()
    pYs = [stack.enter_context(nc.psum_tensor(f"f_pY{i}", [128, 1024], F32))
           for i in range(2)]
    psum_gate = p.sem("sVE").v

    def load_w1(e, par, guard):
        last = 0
        for k in range(8):
            last = p.emit(
                "gpsimd",
                (lambda e, k, par: lambda e_: e_.dma_start(
                    w1sb[par][:, k * H:(k + 1) * H],
                    io["W1"].ap()[e * D + k * 128: e * D + (k + 1) * 128, :]))(e, k, par),
                waits=[(sTE, guard)], inc=sW1s[par], k=16)
        return last

    def load_w2(e, guard):
        last = 0
        for m in range(NHC):
            last = p.emit(
                "gpsimd",
                (lambda e, m: lambda e_: e_.dma_start(
                    w2sb[:, m * D:(m + 1) * D],
                    io["W2"].ap()[e * H + m * 128: e * H + (m + 1) * 128, :]))(e, m),
                waits=[(sTE, guard)], inc=sW2s, k=16)
        return last

    w1_ready = [0, 0]
    w1_users = [0, 0]
    w2_ready = 0
    w2_users = 0
    w1_ready[0] = load_w1(0, 0, 0)
    w2_ready = load_w2(0, 0)
    xt_free = [0, 0]
    pH_free = [0, 0]
    hsb_free = 0
    pY_frees = [0, 0]
    yacc_wr = 0
    xres_free = [0, 0]
    for mg in range(NMEGA):
        for sb in range(MEGA // 128):
            b = mg * (MEGA // 128) + sb
            ys = sb % 2
            tv = None
            for dh in range(2):
                tv = p.emit("tensor",
                            (lambda b, dh, ys: lambda e: e.matmul(
                                pYs[ys][:, dh * 512:(dh + 1) * 512],
                                gfm[0:8, b * 128:(b + 1) * 128],
                                b2bf[0:8, dh * 512:(dh + 1) * 512],
                                start=True, stop=True))(b, dh, ys),
                            waits=[(sVE, gfm_last), (sWG, b2_ready),
                                   (sVE, pY_frees[ys]), (sVE, psum_gate)], inc=sTE)
            pY_frees[ys] = yacc_wr = p.emit(
                "vector",
                (lambda sb, ys: lambda e: e.tensor_copy(
                    y_acc[:, sb * D:(sb + 1) * D], pYs[ys][:, :]))(sb, ys),
                waits=[(sTE, tv), (sXS[0], xres_free[0]), (sXS[1], xres_free[1])],
                inc=sVE)
        for e in range(8):
            par = e % 2
            nxt = e + 1
            if nxt < 8:
                w1_ready[1 - par] = load_w1(nxt, 1 - par, w1_users[1 - par])
            for ti in range(TPM):
                i = mg * TPM + ti
                slot = ti % 2
                lxv = p.emit(
                    "sync",
                    (lambda i, slot: lambda e_: e_.dma_start(
                        xfmt[slot][:, :].rearrange("p (k t) -> p k t", k=8),
                        bass.AP(io["xfm"], i * 512,
                                [[TOK, 128], [128 * TOK, 8], [1, 512]])))(i, slot),
                    waits=[(sTE, xt_free[slot]), (sSP, xfmb_free)], inc=sXF[slot], k=16)
                for half in range(2):
                    hv = None
                    for mm in range(8):
                        m = half * 8 + mm
                        hs = mm % 2
                        tv = None
                        for k in range(8):
                            tv = p.emit(
                                "tensor",
                                (lambda m, k, hs, slot, par: lambda e_: e_.matmul(
                                    pH[hs][:, :],
                                    w1sb[par][:, k * H + m * 128: k * H + (m + 1) * 128],
                                    xfmt[slot][:, k * 512:(k + 1) * 512],
                                    start=(k == 0), stop=(k == 7)))(m, k, hs, slot, par),
                                waits=[(sXF[slot], lxv), (sW1s[par], w1_ready[par]),
                                       (sSE, pH_free[hs]), (sTE, hsb_free)],
                                inc=(sTE if k == 7 else None))
                            if mm == 7 and k == 7:
                                if half == 1:
                                    w1_users[par] = tv
                                    xt_free[slot] = tv
                        hv = p.emit("scalar",
                                    (lambda mm, hs, e: lambda e_: e_.activation(
                                        hsb[:, mm * 512:(mm + 1) * 512], pH[hs][:, :],
                                        AF.Gelu,
                                        bias=b1c[:, (half * 8 + mm) * 8 + e:
                                                 (half * 8 + mm) * 8 + e + 1]))(mm, hs, e),
                                    waits=[(sTE, tv), (sVE, b1c_ready),
                                           (sTE, hsb_free)], inc=sSE)
                        pH_free[hs] = hv
                    for b4 in range(4):
                        sb = ti * 4 + b4
                        b = i * 4 + b4
                        ys = b4 % 2
                        tv = None
                        for dh in range(2):
                            for mm in range(8):
                                m = half * 8 + mm
                                tv = p.emit(
                                    "tensor",
                                    (lambda m, mm, b4, dh, ys: lambda e_: e_.matmul(
                                        pYs[ys][:, dh * 512:(dh + 1) * 512],
                                        hsb[:, mm * 512 + b4 * 128: mm * 512 + (b4 + 1) * 128],
                                        w2sb[:, m * D + dh * 512: m * D + (dh + 1) * 512],
                                        start=(mm == 0), stop=(mm == 7)))(m, mm, b4, dh, ys),
                                    waits=[(sSE, hv), (sW2s, w2_ready),
                                           (sVE, pY_frees[ys])],
                                    inc=(sTE if mm == 7 else None))
                        if b4 == 3:
                            hsb_free = tv
                            if half == 1:
                                w2_users = tv
                        pY_frees[ys] = yacc_wr = p.emit(
                            "vector",
                            (lambda sb, b, e, ys: lambda e_: e_.scalar_tensor_tensor(
                                y_acc[:, sb * D:(sb + 1) * D], pYs[ys][:, :],
                                gates[:, b * 8 + e:b * 8 + e + 1],
                                y_acc[:, sb * D:(sb + 1) * D],
                                op0=ALU.mult, op1=ALU.add))(sb, b, e, ys),
                            waits=[(sTE, tv)], inc=sVE)
            if nxt < 8:
                w2_ready = load_w2(nxt, w2_users)
            elif mg + 1 < NMEGA:
                w1_ready[0] = load_w1(0, 0, w1_users[0])
                w2_ready = load_w2(0, w2_users)
        for sb in range(MEGA // 128):
            b = mg * (MEGA // 128) + sb
            slot = b % 2
            lv = p.emit("sync",
                        (lambda b, slot: lambda e_: e_.dma_start(
                            xtm[slot][:, :],
                            io["stream_in"].ap()[b * 128:(b + 1) * 128, :]))(b, slot),
                        waits=[(sVE, xtm_free[slot])], inc=sXT[slot], k=16)
            av = p.emit("vector",
                        (lambda b, slot: lambda e_: e_.tensor_scalar(
                            xtm[slot][:, :], xtm[slot][:, :],
                            absb[:, b:b + 1], absb[:, 32 + b:32 + b + 1],
                            op0=ALU.mult, op1=ALU.add))(b, slot),
                        waits=[(sXT[slot], lv)], inc=sVE)
            xv = p.emit("vector",
                        (lambda sb, slot: lambda e_: e_.tensor_tensor(
                            xres[slot][:, :], y_acc[:, sb * D:(sb + 1) * D],
                            xtm[slot][:, :], op=ALU.add))(sb, slot),
                        waits=[(sVE, av), (sVE, yacc_wr), (sXS[slot], xres_free[slot])],
                        inc=sVE)
            xtm_free[slot] = xv
            p.emit("vector",
                   (lambda slot, b: lambda e_: e_.reduce_sum(
                       st1[:, b:b + 1], xres[slot][:, :], axis=AX.X))(slot, b),
                   waits=[(sVE, xv)], inc=sVE)
            sqv = p.emit("scalar",
                         (lambda slot, b, sb: lambda e_: e_.activation(
                             y_acc[:, sb * D:(sb + 1) * D], xres[slot][:, :],
                             AF.Square, accum_out=st2[:, b:b + 1]))(slot, b, sb),
                         waits=[(sVE, xv)], inc=sSE)
            xres_free[slot] = p.emit(
                "gpsimd",
                (lambda slot, b: lambda e_: e_.dma_start(
                    io["stream_out"].ap()[b * 128:(b + 1) * 128, :],
                    xres[slot][:, :]))(slot, b),
                waits=[(sVE, xv), (sSE, sqv)], inc=sXS[slot], k=16)

    emit_ab(p, stack, io["ab_out"], st1, st2,
            stats_ready=[(sVE, p.sem("sVE").v), (sSE, p.sem("sSE").v)], pfx="p2",
            store_sem=sAB)
    p.emit("gpsimd", lambda e: e.engine_nop(),
           waits=[(sXS[0], sXS[0].v), (sXS[1], sXS[1].v), (sAB, sAB.v)])



def build_l1_attn_a(p, stack, io):
    nc = p.nc
    com = p.common
    sIN, sWG = p.sem("sIN"), p.sem("sWG")
    sTE, sVE, sSE = p.sem("sTE"), p.sem("sVE"), p.sem("sSE")
    sOUT, sGE = p.sem("sOUT"), p.sem("sGE")
    C3 = 3072

    absb = stack.enter_context(nc.sbuf_tensor("a_absb", [128, 64], F32))
    stats = stack.enter_context(nc.sbuf_tensor("a_stats", [128, 48], F32))
    scols = stack.enter_context(nc.sbuf_tensor("a_scols", [128, 4], F32))
    c1, c2 = scols[:, 0:1], scols[:, 1:2]
    xtm = [stack.enter_context(nc.sbuf_tensor(f"a_xtm{i}", [128, D], F32)) for i in range(2)]
    xbf = stack.enter_context(nc.sbuf_tensor("a_xbf", [128, D], BF16))
    xfmb = stack.enter_context(nc.sbuf_tensor("a_xfmb", [128, 8 * 512], BF16))
    xabuf = stack.enter_context(nc.sbuf_tensor("a_xabuf", [128, 24 * 512], BF16))
    vtmb = stack.enter_context(nc.sbuf_tensor("a_vtmb", [128, 4 * D], BF16))
    sqscr = stack.enter_context(nc.sbuf_tensor("a_sqscr", [128, 512], F32))
    bbf = stack.enter_context(nc.sbuf_tensor("a_bbf", [64, C3], BF16))
    atT = stack.enter_context(nc.sbuf_tensor("a_atT", [64, D], BF16))

    pB = stack.enter_context(nc.psum_tensor("a_pB", [128, 512], F32))
    pT = [stack.enter_context(nc.psum_tensor(f"a_pT{i}", [128, 128], BF16))
          for i in range(2)]
    pXA = [stack.enter_context(nc.psum_tensor(f"a_pXA{i}", [128, 512], F32))
           for i in range(2)]
    pT_free = [0, 0]
    sXT = [p.sem("sXT0"), p.sem("sXT1")]
    sSP, sSPV, sSB = p.sem("sSP"), p.sem("sSPV"), p.sem("sSB")

    vin = p.emit("sync", lambda e: e.dma_start(absb[:, :], io["ab_in"].ap()),
                 inc=sIN, k=16)
    wmT, wmT_ready = stream_transpose_weight(p, stack, "a_wmT", io["W_attn"],
                                             C3, D, pT, pT_free,
                                             prep_ready=(sIN, vin))
    with ExitStack() as prep:
        abf = prep.enter_context(nc.sbuf_tensor("a_abf", [128, 8 * 64], BF16))
        lv = p.emit("gpsimd", lambda e: e.dma_start(
            abf[:, :].rearrange("p (a r) -> p a r", a=8),
            io["lora_attn_A"].ap().rearrange("(a p) r -> p a r", p=128)),
            inc=sWG, k=16)
        bv = p.emit("gpsimd", lambda e: e.dma_start(bbf[:, :], io["lora_attn_B"].ap()),
                    inc=sWG, k=16)
        atv = 0
        for a in range(8):
            ts = a % 2
            tv = p.emit("tensor",
                        (lambda a, ts: lambda e: e.transpose(
                            pT[ts][0:64, :], abf[:, a * 64:(a + 1) * 64],
                            com.identbf[:, :]))(a, ts),
                        waits=[(sWG, bv), (sVE, pT_free[ts]), (sIN, vin)],
                        inc=sTE)
            atv = pT_free[ts] = p.emit(
                "vector",
                (lambda a, ts: lambda e: e.tensor_copy(
                    atT[0:64, a * 128:(a + 1) * 128], pT[ts][0:64, 0:128]))(a, ts),
                waits=[(sTE, tv)], inc=sVE)
        lora_done = 0
        pB_free = 0
        for k3 in (range(3) if "lora" not in A3_CUT else range(0)):
            for kk in range(8):
                for ih in range(2):
                    tv = p.emit(
                        "tensor",
                        (lambda k3, kk, ih: lambda e: e.matmul(
                            pB[:, :],
                            bbf[0:64, 1024 * k3 + kk * 128: 1024 * k3 + (kk + 1) * 128],
                            atT[0:64, ih * 512:(ih + 1) * 512],
                            start=True, stop=True))(k3, kk, ih),
                        waits=[(sWG, bv), (sVE, atv), (sVE, wmT_ready),
                               (sVE, pB_free)],
                        inc=sTE)
                    base = kk * C3 + k3 + 3 * (ih * 512)
                    lora_done = pB_free = p.emit(
                        "vector",
                        (lambda base: lambda e: e.tensor_tensor(
                            wmT[:, base:base + 3 * 512 - 2:3],
                            wmT[:, base:base + 3 * 512 - 2:3],
                            pB[:, :], op=ALU.add))(base),
                        waits=[(sTE, tv)], inc=sVE)

    xtm_free = [0, 0]
    xbf_free = 0
    pXA_free_v = [0, 0]
    pXA_free_s = [0, 0]
    xabuf_free = 0
    vtmb_free = 0
    st_ready_v = 0
    for i in range(NT):
        for b4 in range(4):
            b = 4 * i + b4
            slot = b % 2
            lv = p.emit("sync",
                        (lambda b, slot: lambda e: e.dma_start(
                            xtm[slot][:, :],
                            io["stream_in"].ap()[b * 128:(b + 1) * 128, :]))(b, slot),
                        waits=[(sVE, xtm_free[slot])], inc=sXT[slot], k=16)
            av = p.emit("vector",
                        (lambda b, slot: lambda e: e.tensor_scalar(
                            xtm[slot][:, :], xtm[slot][:, :],
                            absb[:, b:b + 1], absb[:, 32 + b:32 + b + 1],
                            op0=ALU.mult, op1=ALU.add))(b, slot),
                        waits=[(sXT[slot], lv), (sIN, vin)], inc=sVE)
            cv = p.emit("vector",
                        (lambda slot: lambda e: e.tensor_copy(
                            xbf[:, :], xtm[slot][:, :]))(slot),
                        waits=[(sVE, av), (sTE, xbf_free)], inc=sVE)
            xtm_free[slot] = cv
            for k in range(8):
                ts = k % 2
                tv = p.emit("tensor",
                            (lambda k, ts: lambda e: e.transpose(
                                pT[ts][:, :], xbf[:, k * 128:(k + 1) * 128],
                                com.identbf[:, :]))(k, ts),
                            waits=[(sVE, cv), (sVE, pT_free[ts])], inc=sTE)
                if k == 7:
                    xbf_free = tv
                pT_free[ts] = p.emit(
                    "vector",
                    (lambda k, b4, ts: lambda e: e.tensor_copy(
                        xfmb[:, k * 512 + b4 * 128: k * 512 + (b4 + 1) * 128],
                        pT[ts][:, :]))(k, b4, ts),
                    waits=[(sTE, tv)], inc=sVE)
        fm_ready = sVE.v
        for m in range(24):
            ms = m % 2
            tv = None
            for kk in range(8):
                tv = p.emit(
                    "tensor",
                    (lambda m, kk, ms: lambda e: e.matmul(
                        pXA[ms][:, :],
                        wmT[:, kk * C3 + m * 128: kk * C3 + (m + 1) * 128],
                        xfmb[:, kk * 512:(kk + 1) * 512],
                        start=(kk == 0), stop=(kk == 7)))(m, kk, ms),
                    waits=[(sVE, fm_ready), (sVE, lora_done),
                           (sVE, pXA_free_v[ms]), (sSE, pXA_free_s[ms])],
                    inc=(sTE if kk == 7 else None))
            ev = p.emit("vector",
                        (lambda m, ms: lambda e: e.tensor_copy(
                            xabuf[:, m * 512:(m + 1) * 512], pXA[ms][:, :]))(m, ms),
                        waits=[(sTE, tv), (sSP, xabuf_free)], inc=sVE)
            if "stats" in A3_CUT:
                pXA_free_v[ms] = ev
                continue
            v1 = p.emit("vector",
                        (lambda ms: lambda e: e.reduce_sum(c1, pXA[ms][:, :],
                                                           axis=AX.X))(ms),
                        waits=[(sVE, ev)], inc=sVE)
            pXA_free_v[ms] = v1
            sq = p.emit("vector",
                        (lambda m: lambda e: e.tensor_tensor(
                            sqscr[:, :], xabuf[:, m * 512:(m + 1) * 512],
                            xabuf[:, m * 512:(m + 1) * 512], op=ALU.mult))(m),
                        waits=[(sVE, v1)], inc=sVE)
            v2 = p.emit("vector",
                        lambda e: e.reduce_sum(c2, sqscr[:, :], axis=AX.X),
                        waits=[(sVE, sq)], inc=sVE)
            if i == 0:
                v3 = p.emit("vector",
                            (lambda m: lambda e: e.tensor_copy(stats[:, m:m + 1], c1))(m),
                            waits=[(sVE, v2)], inc=sVE)
                v4 = p.emit("vector",
                            (lambda m: lambda e: e.tensor_copy(stats[:, 24 + m:25 + m], c2))(m),
                            waits=[(sVE, v3)], inc=sVE)
            else:
                v3 = p.emit("vector",
                            (lambda m: lambda e: e.tensor_tensor(
                                stats[:, m:m + 1], stats[:, m:m + 1], c1,
                                op=ALU.add))(m),
                            waits=[(sVE, v2)], inc=sVE)
                v4 = p.emit("vector",
                            (lambda m: lambda e: e.tensor_tensor(
                                stats[:, 24 + m:25 + m], stats[:, 24 + m:25 + m], c2,
                                op=ALU.add))(m),
                            waits=[(sVE, v3)], inc=sVE)
            st_ready_v = v4
        last_vt = 0
        for m8 in (range(8) if "vt" not in A3_CUT else range(0)):
            m = 16 + m8
            for b4 in range(4):
                ts = (m8 * 4 + b4) % 2
                tv = p.emit("tensor",
                            (lambda m, b4, ts: lambda e: e.transpose(
                                pT[ts][:, :],
                                xabuf[:, m * 512 + b4 * 128: m * 512 + (b4 + 1) * 128],
                                com.identbf[:, :]))(m, b4, ts),
                            waits=[(sVE, sVE.v if (m8 == 0 and b4 == 0) else 0),
                                   (sVE, pT_free[ts]), (sSPV, vtmb_free)],
                            inc=sTE)
                pT_free[ts] = last_vt = p.emit(
                    "vector",
                    (lambda m8, b4, ts: lambda e: e.tensor_copy(
                        vtmb[:, b4 * D + m8 * 128: b4 * D + (m8 + 1) * 128],
                        pT[ts][:, :]))(m8, b4, ts),
                    waits=[(sTE, tv)], inc=sVE)
        if "spill" in A3_CUT:
            continue
        xabuf_free = p.emit(
            "gpsimd",
            (lambda i: lambda e: e.dma_start(
                bass.AP(io["xa_qk"], i * 512,
                        [[TOK, 128], [128 * TOK, 16], [1, 512]]),
                xabuf[:, 0:16 * 512].rearrange("p (k t) -> p k t", k=16)))(i),
            waits=[(sVE, sVE.v)], inc=sSP, k=16)
        vtmb_free = p.emit(
            "gpsimd",
            (lambda i: lambda e: e.dma_start(
                bass.AP(io["v_tm"], i * 512 * D,
                        [[D, 128], [128 * D, 4], [1, D]]),
                vtmb[:, :].rearrange("p (s c) -> p s c", s=4)))(i),
            waits=[(sVE, last_vt)], inc=sSPV, k=16)
    p.emit("gpsimd", lambda e: e.dma_start(io["stats"].ap()[:, :], stats[:, :]),
           waits=[(sVE, p.sem("sVE").v)], inc=sSB, k=16)
    p.emit("gpsimd", lambda e: e.engine_nop(),
           waits=[(sSP, sSP.v), (sSPV, sSPV.v), (sSB, sSB.v)])


def build_phase3a_program():
    nc = bass.Bass("TRN2", target_bir_lowering=False)
    io = {
        "stream_in": nc.dram_tensor("stream_in", [TOK, D], F32, kind="ExternalInput"),
        "ab_in": nc.dram_tensor("ab_in", [128, 64], F32, kind="ExternalInput"),
        "W_attn": nc.dram_tensor("W_attn", [3072, D], F32, kind="ExternalInput"),
        "lora_attn_A": nc.dram_tensor("lora_attn_A", [D, 64], F32, kind="ExternalInput"),
        "lora_attn_B": nc.dram_tensor("lora_attn_B", [64, 3072], F32, kind="ExternalInput"),
        "xa_qk": nc.dram_tensor("xa_qk", [16 * 128, TOK], BF16, kind="ExternalOutput"),
        "v_tm": nc.dram_tensor("v_tm", [TOK, D], BF16, kind="ExternalOutput"),
        "stats": nc.dram_tensor("stats", [128, 48], F32, kind="ExternalOutput"),
    }
    with ExitStack() as stack:
        p = Prog(nc, stack)
        with nc.Block() as block:
            p.common = Common(p, stack, need_ones512=False)
            build_l1_attn_a(p, stack, io)
            p.build(block)
    return nc


def phase3a_inmaps(inputs, streams, abs_):
    Wa = np.ascontiguousarray(np.asarray(inputs["L1_attn_W_attn"], np.float32))
    lA = np.ascontiguousarray(np.asarray(inputs["L1_attn_lora_attn_A"], np.float32))
    lB = np.ascontiguousarray(np.asarray(inputs["L1_attn_lora_attn_B"], np.float32))
    ci = const_inputs()
    return [{"stream_in": streams[c], "ab_in": abs_[c],
             "W_attn": Wa, "lora_attn_A": lA, "lora_attn_B": lB,
             "c_identbf": ci["c_identbf"], "c_identf": ci["c_identf"]}
            for c in range(8)]



def build_l1_attn_b(p, stack, io):
    nc = p.nc
    com = p.common
    sIN, sWG = p.sem("sIN"), p.sem("sWG")
    sTE, sVE, sSE = p.sem("sTE"), p.sem("sVE"), p.sem("sSE")
    sOUT, sGE = p.sem("sOUT"), p.sem("sGE")
    QG = 512
    NG = TOK // QG
    NMT = 39

    absb = stack.enter_context(nc.sbuf_tensor("b_absb", [128, 64], F32))
    anbn = stack.enter_context(nc.sbuf_tensor("b_anbn", [128, 96], F32))
    an, bn = anbn[:, 0:24], anbn[:, 24:48]
    anq, bnq = anbn[:, 48:56], anbn[:, 56:64]
    bias1c = anbn[:, 64:72]
    mcols = stack.enter_context(nc.sbuf_tensor("b_mcols", [128, 16], F32))
    bnvbf = stack.enter_context(nc.sbuf_tensor("b_bnvbf", [128, 8], BF16))
    st1 = stack.enter_context(nc.sbuf_tensor("b_st1", [128, NB], F32))
    st2 = stack.enter_context(nc.sbuf_tensor("b_st2", [128, NB], F32))
    MT = stack.enter_context(nc.sbuf_tensor("b_MT", [128, NMT * 128], BF16))
    Avb = stack.enter_context(nc.sbuf_tensor("b_Avb", [128, D], F32))
    wpT = None

    pS = stack.enter_context(nc.psum_tensor("b_pS", [128, 1024], F32))
    pO = stack.enter_context(nc.psum_tensor("b_pO", [128, 1024], F32))
    pPT = [stack.enter_context(nc.psum_tensor(f"b_pPT{i}", [128, 128], BF16))
           for i in range(2)]
    pB = stack.enter_context(nc.psum_tensor("b_pB", [128, 512], F32))
    pT_free = [0, 0]
    sXT = [p.sem("sXT0"), p.sem("sXT1")]
    sXS = [p.sem("sXS0"), p.sem("sXS1")]
    sQF = p.sem("sQF")
    sKF = [p.sem("sK0"), p.sem("sK1")]
    sVT = [p.sem("sV0"), p.sem("sV1")]
    sAB = p.sem("sAB")

    wmpT = stack.enter_context(nc.sbuf_tensor("b_wmpT", [128, 8 * D], BF16))
    zsc = stack.enter_context(nc.sbuf_tensor("b_zsc", [128, 2], F32))
    zcol, iecol = zsc[:, 0:1], zsc[:, 1:2]
    vin = p.emit("sync", lambda e: e.dma_start(absb[:, :], io["ab_in"].ap()),
                 inc=sIN, k=16)
    vz = p.emit("sync", lambda e: e.dma_start(zsc[:, :], io["zmask"].ap()),
                inc=sIN, k=16)
    zsc_in = [vz]
    wpT, wpT_ready = stream_transpose_weight(p, stack, "b_wpT", io["W_proj"],
                                             D, D, pPT, pT_free,
                                             prep_ready=(sIN, vz))

    with ExitStack() as prep:
        abf = prep.enter_context(nc.sbuf_tensor("b_abf", [128, 8 * 64], BF16))
        bpbf = prep.enter_context(nc.sbuf_tensor("b_bpbf", [64, D], BF16))
        atT = prep.enter_context(nc.sbuf_tensor("b_atT", [64, D], BF16))
        grow = prep.enter_context(nc.sbuf_tensor("b_grow", [1, 3072], F32))
        brow = prep.enter_context(nc.sbuf_tensor("b_brow", [1, 3072], F32))
        TRI0 = prep.enter_context(nc.sbuf_tensor("b_TRI", [128, 128], F32))
        srows = prep.enter_context(nc.sbuf_tensor("b_srows", [128, 8 * 48], F32))
        tots = prep.enter_context(nc.sbuf_tensor("b_tots", [128, 48], F32))
        tmp24 = prep.enter_context(nc.sbuf_tensor("b_tmp24", [128, 24], F32))
        gcol = prep.enter_context(nc.sbuf_tensor("b_gcol", [128, 24], F32))
        bcol = prep.enter_context(nc.sbuf_tensor("b_bcol", [128, 24], F32))
        epsc = prep.enter_context(nc.sbuf_tensor("b_epsc", [128, 1], F32))
        arow = prep.enter_context(nc.sbuf_tensor("b_arow", [1, D], F32))


        lv = p.emit("gpsimd", lambda e: e.dma_start(
            abf[:, :].rearrange("p (a r) -> p a r", a=8),
            io["lora_proj_A"].ap().rearrange("(a p) r -> p a r", p=128)),
            inc=sWG, k=16)
        bv = p.emit("gpsimd", lambda e: e.dma_start(bpbf[:, :], io["lora_proj_B"].ap()),
                    inc=sWG, k=16)
        atv = 0
        for a in range(8):
            ts = a % 2
            tv = p.emit("tensor",
                        (lambda a, ts: lambda e: e.transpose(
                            pPT[ts][0:64, :], abf[:, a * 64:(a + 1) * 64],
                            com.identbf[:, :]))(a, ts),
                        waits=[(sWG, lv), (sVE, pT_free[ts]), (sIN, com.ready_in)],
                        inc=sTE)
            atv = pT_free[ts] = p.emit(
                "vector",
                (lambda a, ts: lambda e: e.tensor_copy(
                    atT[0:64, a * 128:(a + 1) * 128], pPT[ts][0:64, 0:128]))(a, ts),
                waits=[(sTE, tv)], inc=sVE)
        pB_free = 0
        for kk in range(8):
            for ih in range(2):
                tv = p.emit(
                    "tensor",
                    (lambda kk, ih: lambda e: e.matmul(
                        pB[:, :], bpbf[0:64, kk * 128:(kk + 1) * 128],
                        atT[0:64, ih * 512:(ih + 1) * 512],
                        start=True, stop=True))(kk, ih),
                    waits=[(sWG, bv), (sVE, atv), (sVE, wpT_ready), (sVE, pB_free)],
                    inc=sTE)
                pB_free = p.emit(
                    "vector",
                    (lambda kk, ih: lambda e: e.tensor_tensor(
                        wmpT[:, kk * D + ih * 512: kk * D + (ih + 1) * 512],
                        wpT[:, kk * D + ih * 512: kk * D + (ih + 1) * 512],
                        pB[:, :], op=ALU.add))(kk, ih),
                    waits=[(sTE, tv)], inc=sVE)
        wmpT_ready = pB_free

        v = p.emit("sync", lambda e: e.dma_start(
            srows[:, :].rearrange("p (r f) -> p r f", r=8),
            io["stats_all"].ap().rearrange("(r p) f -> p r f", p=128)),
            inc=sIN, k=16)
        vg = p.emit("sync", lambda e: e.dma_start(grow[:, :], io["bn_gamma"].ap()),
                    inc=sIN, k=16)
        vg = p.emit("sync", lambda e: e.dma_start(brow[:, :], io["bn_beta"].ap()),
                    inc=sIN, k=16)
        vtr = p.emit("sync", lambda e: e.dma_start(TRI0[:, :], io["trif"].ap()),
                     inc=sIN, k=16)
        prep_tot = p.sem("sIN").v
        vv = p.emit("vector", lambda e: e.tensor_copy(tots[:, :], srows[:, 0:48]),
                    waits=[(sIN, prep_tot)], inc=sVE)
        for r in range(1, 8):
            vv = p.emit("vector",
                        (lambda r: lambda e: e.tensor_tensor(
                            tots[:, :], tots[:, :],
                            srows[:, r * 48:(r + 1) * 48], op=ALU.add))(r),
                        waits=[(sVE, vv)], inc=sVE)
        ninv = 1.0 / (8 * TOK)
        vv = p.emit("vector", lambda e: e.tensor_scalar(tots[:, 0:24], tots[:, 0:24],
                                                        ninv, None, op0=ALU.mult),
                    waits=[(sVE, vv)], inc=sVE)
        vv = p.emit("vector", lambda e: e.tensor_scalar(tots[:, 24:48], tots[:, 24:48],
                                                        ninv, None, op0=ALU.mult),
                    waits=[(sVE, vv)], inc=sVE)
        vv = p.emit("vector", lambda e: e.tensor_tensor(tmp24[:, :], tots[:, 0:24],
                                                        tots[:, 0:24], op=ALU.mult),
                    waits=[(sVE, vv)], inc=sVE)
        vv = p.emit("vector", lambda e: e.tensor_tensor(tots[:, 24:48], tots[:, 24:48],
                                                        tmp24[:, :], op=ALU.subtract),
                    waits=[(sVE, vv)], inc=sVE)
        vv = p.emit("vector", lambda e: e.memset(epsc[:, :], 1e-5), waits=[(sVE, vv)],
                    inc=sVE)
        sv = p.emit("scalar", lambda e: e.activation(tmp24[:, :], tots[:, 24:48],
                                                     AF.Ln, bias=epsc[:, 0:1]),
                    waits=[(sVE, vv)], inc=sSE)
        vv = p.emit("vector", lambda e: e.tensor_scalar(tmp24[:, :], tmp24[:, :],
                                                        -0.5, None, op0=ALU.mult),
                    waits=[(sSE, sv)], inc=sVE)
        sv = p.emit("scalar", lambda e: e.activation(tmp24[:, :], tmp24[:, :], AF.Exp),
                    waits=[(sVE, vv)], inc=sSE)
        guard = [0]
        columnize(p, lambda i: grow[0:1, i * 128:(i + 1) * 128], 24, gcol, 0,
                  pB, guard, waits=[(sIN, prep_tot), (sVE, wmpT_ready)])
        columnize(p, lambda i: brow[0:1, i * 128:(i + 1) * 128], 24, bcol, 0,
                  pB, guard, waits=[(sIN, prep_tot)])
        vv = p.emit("vector", lambda e: e.tensor_tensor(an, gcol[:, :], tmp24[:, :],
                                                        op=ALU.mult),
                    waits=[(sSE, sv), (sVE, guard[0])], inc=sVE)
        vv = p.emit("vector",
                    lambda e: e.scalar_tensor_tensor(bn, tots[:, 0:24], -1.0,
                                                     an, op0=ALU.mult, op1=ALU.mult),
                    waits=[(sVE, vv)], inc=sVE)
        vv = p.emit("vector", lambda e: e.tensor_tensor(bn, bn, bcol[:, :], op=ALU.add),
                    waits=[(sVE, vv)], inc=sVE)
        vv = p.emit("vector", lambda e: e.tensor_scalar(anq, an[:, 0:8], 1.0 / 32, None,
                                                        op0=ALU.mult),
                    waits=[(sVE, vv)], inc=sVE)
        vv = p.emit("vector", lambda e: e.tensor_scalar(bnq, bn[:, 0:8], 1.0 / 32, None,
                                                        op0=ALU.mult),
                    waits=[(sVE, vv)], inc=sVE)
        vv = p.emit("vector", lambda e: e.tensor_copy(bnvbf[:, :], bn[:, 16:24]),
                    waits=[(sVE, vv)], inc=sVE)
        an_ready = vv
        avv = 0
        for j in range(8):
            tv = p.emit("tensor",
                        (lambda j: lambda e: e.transpose(
                            pB[0:1, 0:128], an[:, 16 + j:17 + j], com.identf[:, :]))(j),
                        waits=[(sVE, an_ready), (sVE, avv)], inc=sTE)
            avv = p.emit("vector",
                         (lambda j: lambda e: e.tensor_copy(
                             arow[0:1, j * 128:(j + 1) * 128], pB[0:1, 0:128]))(j),
                         waits=[(sTE, tv)], inc=sVE)
        for hh in range(2):
            tv = p.emit("tensor",
                        (lambda hh: lambda e: e.matmul(
                            pB[:, :], com.ones_row[0:1, 0:128],
                            arow[0:1, hh * 512:(hh + 1) * 512],
                            start=True, stop=True))(hh),
                        waits=[(sVE, avv)], inc=sTE)
            avv = p.emit("vector",
                         (lambda hh: lambda e: e.tensor_copy(
                             Avb[:, hh * 512:(hh + 1) * 512], pB[:, :]))(hh),
                         waits=[(sTE, tv)], inc=sVE)
        av_ready = avv
        b1v = 0
        for dch in range(8):
            tv = None
            for c in range(8):
                tv = p.emit("tensor",
                            (lambda dch, c: lambda e: e.matmul(
                                pB[:, 0:1],
                                wpT[:, c * D + dch * 128: c * D + (dch + 1) * 128],
                                bnvbf[:, c:c + 1],
                                start=(c == 0), stop=(c == 7)))(dch, c),
                            waits=[(sVE, an_ready), (sVE, b1v), (sVE, av_ready)],
                            inc=(sTE if c == 7 else None))
            b1v = p.emit("vector",
                         (lambda dch: lambda e: e.tensor_copy(
                             bias1c[:, dch:dch + 1], pB[:, 0:1]))(dch),
                         waits=[(sTE, tv)], inc=sVE)
        bias1_ready = b1v
        zc_ready = zsc_in[0]
        TRI = TRI0
        gvv = prep_tot
        vv = p.emit("vector", lambda e: e.memset(MT[:, 0:3 * 128], 0.0), inc=sVE)
        vv = p.emit("vector", lambda e: e.memset(MT[:, 4 * 128:NMT * 128], -1e30),
                    waits=[(sVE, vv)], inc=sVE)
        vv = p.emit("vector", lambda e: e.tensor_scalar(MT[:, 3 * 128:4 * 128],
                                                        TRI[:, :], iecol, None,
                                                        op0=ALU.mult),
                    waits=[(sIN, prep_tot), (sVE, vv)], inc=sVE)
        vv = p.emit("vector", lambda e: e.tensor_scalar(MT[:, 4 * 128:35 * 128],
                                                        MT[:, 4 * 128:35 * 128],
                                                        iecol, None, op0=ALU.mult),
                    waits=[(sVE, vv)], inc=sVE)
        vv = p.emit("vector", lambda e: e.tensor_scalar(MT[:, 35 * 128:36 * 128],
                                                        MT[:, 35 * 128:36 * 128],
                                                        iecol, None, op0=ALU.mult),
                    waits=[(sVE, vv)], inc=sVE)
        vv = p.emit("vector",
                    lambda e: e.scalar_tensor_tensor(
                        MT[:, 35 * 128:36 * 128], TRI[:, :], zcol,
                        MT[:, 35 * 128:36 * 128], op0=ALU.mult, op1=ALU.add),
                    waits=[(sVE, vv)], inc=sVE)
        mt_ready = vv
        if "dbg" in B3_CUT:
            p.emit("gpsimd", lambda e: e.dma_start(io["dbg_mt"].ap(), MT[:, :]),
                   waits=[(sVE, mt_ready)], inc=sOUT, k=16)
            p.emit("gpsimd", lambda e: e.dma_start(io["dbg_z"].ap(), zsc[:, :]),
                   waits=[(sVE, mt_ready)], inc=sOUT, k=16)

    p.barrier()
    qf = stack.enter_context(nc.sbuf_tensor("b_qf", [128, 8 * QG], BF16))
    kf = [stack.enter_context(nc.sbuf_tensor(f"b_kf{i}", [128, 8 * 1024], BF16))
          for i in range(2)]
    vt = [stack.enter_context(nc.sbuf_tensor(f"b_vt{i}", [128, 8 * 1024], BF16))
          for i in range(2)]
    O_sb = stack.enter_context(nc.sbuf_tensor("b_O", [128, 4 * D], F32))
    S_sb = stack.enter_context(nc.sbuf_tensor("b_S", [128, 1024], F32))
    P_bf = stack.enter_context(nc.sbuf_tensor("b_P", [128, 1024], BF16))
    PT_sb = stack.enter_context(nc.sbuf_tensor("b_PT", [128, 8 * 128], BF16))
    ocfm = stack.enter_context(nc.sbuf_tensor("b_ocfm", [128, 8 * 512], BF16))
    ocb = stack.enter_context(nc.sbuf_tensor("b_ocb", [128, D], BF16))
    y1bf = stack.enter_context(nc.sbuf_tensor("b_y1", [128, 8 * 512], BF16))
    xtm = [stack.enter_context(nc.sbuf_tensor(f"b_xtm{i}", [128, D], F32)) for i in range(2)]
    xres = [stack.enter_context(nc.sbuf_tensor(f"b_xres{i}", [128, D], F32)) for i in range(2)]
    sqs = stack.enter_context(nc.sbuf_tensor("b_sqs", [128, D], F32))
    oc32 = stack.enter_context(nc.sbuf_tensor("b_oc32", [128, D], F32))

    m_run = mcols[:, 0:4]
    l_run = mcols[:, 4:8]
    ms_c, dm_c, sc_c, ng_c, rl_c = (mcols[:, 8:9], mcols[:, 9:10], mcols[:, 10:11],
                                    mcols[:, 11:12], mcols[:, 12:13])

    kf_free = [0, 0]
    vt_free = [0, 0]
    kf_norm = [0, 0]
    vt_ready = [0, 0]
    qf_norm = 0
    pS_free = 0
    pS_free_s = 0
    pO_free = 0
    PT_free = 0
    P_free = 0
    S_free = [0]
    oc_free = 0
    ocfm_free = 0
    y1_free = 0
    xtm_free = [0, 0]
    xres_free = [0, 0]
    ml_guard = 0

    for g in (range(NG) if "flash" not in B3_CUT else range(0)):
        lq = p.emit("sync",
                    (lambda g: lambda e: e.dma_start(
                        qf[:, :].rearrange("p (c t) -> p c t", c=8),
                        bass.AP(io["xa_q"], g * QG,
                                [[TOK, 128], [128 * TOK, 8], [1, QG]])))(g),
                    waits=[(sTE, oc_free)], inc=sQF, k=16)
        qv = 0
        for c in range(8):
            qv = p.emit("vector",
                        (lambda c: lambda e: e.tensor_scalar(
                            qf[:, c * QG:(c + 1) * QG], qf[:, c * QG:(c + 1) * QG],
                            anq[:, c:c + 1], bnq[:, c:c + 1],
                            op0=ALU.mult, op1=ALU.add))(c),
                        waits=[(sQF, lq), (sVE, an_ready)], inc=sVE)
        qf_norm = qv
        smax = (32 + 4 * g + 3) // 8 + 1
        for s in range(smax):
            ks = s % 2
            lk = p.emit("sync",
                        (lambda s, ks: lambda e: e.dma_start(
                            kf[ks][:, :].rearrange("p (c t) -> p c t", c=8),
                            bass.AP(io["k_all"], s * 1024,
                                    [[2 * TOK, 128], [128 * 2 * TOK, 8], [1, 1024]])))(s, ks),
                        waits=[(sTE, kf_free[ks])], inc=sKF[ks], k=16)
            kn = 0
            for c in range(8):
                kn = p.emit("vector",
                            (lambda c, ks: lambda e: e.tensor_scalar(
                                kf[ks][:, c * 1024:(c + 1) * 1024],
                                kf[ks][:, c * 1024:(c + 1) * 1024],
                                an[:, 8 + c:9 + c], bn[:, 8 + c:9 + c],
                                op0=ALU.mult, op1=ALU.add))(c, ks),
                            waits=[(sKF[ks], lk), (sVE, an_ready)], inc=sVE)
            kf_norm[ks] = kn
            lv2 = p.emit("sync",
                         (lambda s, ks: lambda e: e.dma_start(
                             vt[ks][:, :].rearrange("p (j c) -> p j c", j=8),
                             bass.AP(io["v_all"], s * 1024 * D,
                                     [[D, 128], [128 * D, 8], [1, D]])))(s, ks),
                         waits=[(sTE, vt_free[ks])], inc=sVT[ks], k=16)
            vt_ready[ks] = lv2
            for qt in range(4):
                QT = 4 * g + qt
                if 8 * s > 32 + QT:
                    continue
                nh = sum(1 for half in range(2) if 4 * (2 * s + half) <= 32 + QT)
                W = nh * 512
                last_qk = None
                for half in range(nh):
                    for c in range(8):
                        last_qk = p.emit(
                            "tensor",
                            (lambda qt, half, c, ks: lambda e: e.matmul(
                                pS[:, half * 512:(half + 1) * 512],
                                qf[:, c * QG + qt * 128: c * QG + (qt + 1) * 128],
                                kf[ks][:, c * 1024 + half * 512: c * 1024 + (half + 1) * 512],
                                start=(c == 0), stop=(c == 7)))(qt, half, c, ks),
                            waits=[(sVE, qf_norm), (sVE, kf_norm[ks]),
                                   (sVE, pS_free), (sSE, pS_free_s),
                                   (sVE, mt_ready)],
                            inc=(sTE if c == 7 else None))
                ev = 0
                for half in range(nh):
                    kc512 = 2 * s + half
                    d0 = 4 * kc512 - QT
                    if d0 + 3 < 0:
                        ev = p.emit("vector",
                                    (lambda half: lambda e: e.tensor_copy(
                                        S_sb[:, half * 512:(half + 1) * 512],
                                        pS[:, half * 512:(half + 1) * 512]))(half),
                                    waits=[(sTE, last_qk), (sSE, S_free[0])], inc=sVE)
                    else:
                        t0 = (d0 + 3) * 128
                        ev = p.emit("vector",
                                    (lambda half, t0: lambda e: e.tensor_tensor(
                                        S_sb[:, half * 512:(half + 1) * 512],
                                        pS[:, half * 512:(half + 1) * 512],
                                        MT[:, t0:t0 + 512], op=ALU.add))(half, t0),
                                    waits=[(sTE, last_qk), (sSE, S_free[0])], inc=sVE)
                pS_free = ev
                vv = p.emit("vector",
                            (lambda W: lambda e: e.reduce_max(ms_c, S_sb[:, 0:W],
                                                              axis=AX.X))(W),
                            waits=[(sVE, ev), (sVE, ml_guard)], inc=sVE)
                if s == 0:
                    vv = p.emit("vector",
                                (lambda qt: lambda e: e.tensor_copy(
                                    m_run[:, qt:qt + 1], ms_c))(qt),
                                waits=[(sVE, vv)], inc=sVE)
                else:
                    vv = p.emit("vector",
                                (lambda qt: lambda e: e.tensor_tensor(
                                    ms_c, ms_c, m_run[:, qt:qt + 1], op=ALU.max))(qt),
                                waits=[(sVE, vv)], inc=sVE)
                    vv = p.emit("vector",
                                (lambda qt: lambda e: e.tensor_tensor(
                                    dm_c, m_run[:, qt:qt + 1], ms_c,
                                    op=ALU.subtract))(qt),
                                waits=[(sVE, vv)], inc=sVE)
                    sv = p.emit("scalar", lambda e: e.activation(sc_c, dm_c, AF.Exp),
                                waits=[(sVE, vv)], inc=sSE)
                    vv = p.emit("vector",
                                (lambda qt: lambda e: e.tensor_copy(
                                    m_run[:, qt:qt + 1], ms_c))(qt),
                                waits=[(sSE, sv)], inc=sVE)
                vv = p.emit("vector", lambda e: e.tensor_scalar(ng_c, ms_c, -1.0, None,
                                                                op0=ALU.mult),
                            waits=[(sVE, vv)], inc=sVE)
                mxv = vv
                sv = p.emit("scalar",
                            (lambda W: lambda e: e.activation(
                                P_bf[:, 0:W], S_sb[:, 0:W], AF.Exp, bias=ng_c))(W),
                            waits=[(sVE, mxv), (sTE, P_free)], inc=sSE)
                S_free[0] = sv
                pS_free_s = sv
                lsv = p.emit("vector",
                             (lambda W: lambda e: e.reduce_sum(ms_c, P_bf[:, 0:W],
                                                               axis=AX.X))(W),
                             waits=[(sSE, sv)], inc=sVE)
                if s == 0:
                    lv3 = p.emit("vector",
                                 (lambda qt: lambda e: e.tensor_copy(
                                     l_run[:, qt:qt + 1], ms_c))(qt),
                                 waits=[(sVE, lsv)], inc=sVE)
                else:
                    lv3 = p.emit("vector",
                                 (lambda qt: lambda e: e.scalar_tensor_tensor(
                                     l_run[:, qt:qt + 1], l_run[:, qt:qt + 1],
                                     sc_c, ms_c, op0=ALU.mult, op1=ALU.add))(qt),
                                 waits=[(sVE, lsv)], inc=sVE)
                ml_guard = lv3
                ptv = 0
                for j in range(nh * 4):
                    ts = j % 2
                    tv = p.emit("tensor",
                                (lambda j, ts: lambda e: e.transpose(
                                    pPT[ts][:, :], P_bf[:, j * 128:(j + 1) * 128],
                                    com.identbf[:, :]))(j, ts),
                                waits=[(sSE, sv), (sVE, pT_free[ts])], inc=sTE)
                    if j == nh * 4 - 1:
                        P_free = tv
                    pT_free[ts] = ptv = p.emit(
                        "vector",
                        (lambda j, ts: lambda e: e.tensor_copy(
                            PT_sb[:, j * 128:(j + 1) * 128], pPT[ts][:, :]))(j, ts),
                        waits=[(sTE, tv), (sTE, PT_free)], inc=sVE)
                last_pv = None
                for j in range(nh * 4):
                    for dh in range(2):
                        last_pv = p.emit(
                            "tensor",
                            (lambda j, dh, ks: lambda e: e.matmul(
                                pO[:, dh * 512:(dh + 1) * 512],
                                PT_sb[:, j * 128:(j + 1) * 128],
                                vt[ks][:, j * 1024 + dh * 512: j * 1024 + (dh + 1) * 512],
                                start=(j == 0), stop=(j == nh * 4 - 1)))(j, dh, ks),
                            waits=[(sVE, ptv), (sVT[ks], vt_ready[ks]), (sVE, pO_free)],
                            inc=(sTE if (j == nh * 4 - 1 and dh == 1) else None))
                PT_free = last_pv
                if s == 0:
                    ov = p.emit("vector",
                                (lambda qt: lambda e: e.tensor_copy(
                                    O_sb[:, qt * D:(qt + 1) * D], pO[:, :]))(qt),
                                waits=[(sTE, last_pv)], inc=sVE)
                else:
                    ov = p.emit("vector",
                                (lambda qt: lambda e: e.scalar_tensor_tensor(
                                    O_sb[:, qt * D:(qt + 1) * D],
                                    O_sb[:, qt * D:(qt + 1) * D], sc_c, pO[:, :],
                                    op0=ALU.mult, op1=ALU.add))(qt),
                                waits=[(sTE, last_pv)], inc=sVE)
                pO_free = ov
            if s == smax - 1:
                kf_free[ks] = sTE.v
                vt_free[ks] = sTE.v
            else:
                kf_free[ks] = sTE.v
                vt_free[ks] = sTE.v
        if g == 0 and "dbg" in B3_CUT:
            dv = p.emit("vector", lambda e: e.tensor_copy(sqs[:, 0:16], mcols[:, :]),
                        waits=[(sVE, p.sem("sVE").v)], inc=sVE)
            p.emit("gpsimd", lambda e: e.dma_start(io["dbg_ml"].ap()[:, 0:16], sqs[:, 0:16]),
                   waits=[(sVE, dv)], inc=sOUT, k=16)
            dv = p.emit("vector", lambda e: e.tensor_copy(sqs[:, :], O_sb[:, 0:D]),
                        waits=[(sVE, dv)], inc=sVE)
            p.emit("gpsimd", lambda e: e.dma_start(io["dbg_O"].ap(), sqs[:, :]),
                   waits=[(sVE, dv)], inc=sOUT, k=16)
            dv = p.emit("vector", lambda e: e.tensor_copy(sqs[:, :], S_sb[:, 0:D]),
                        waits=[(sVE, dv)], inc=sVE)
            p.emit("gpsimd", lambda e: e.dma_start(io["dbg_S"].ap(), sqs[:, :]),
                   waits=[(sVE, dv)], inc=sOUT, k=16)
        if "proj" in B3_CUT:
            continue
        oc_last = 0
        for qt in range(4):
            vv = p.emit("vector",
                        (lambda qt: lambda e: e.reciprocal(rl_c, l_run[:, qt:qt + 1]))(qt),
                        waits=[(sVE, sVE.v)], inc=sVE)
            vv = p.emit("vector",
                        (lambda qt: lambda e: e.scalar_tensor_tensor(
                            oc32[:, :], O_sb[:, qt * D:(qt + 1) * D], rl_c,
                            Avb[:, :], op0=ALU.mult, op1=ALU.mult))(qt),
                        waits=[(sVE, vv), (sVE, av_ready)], inc=sVE)
            cv = p.emit("vector",
                        (lambda: lambda e: e.tensor_copy(ocb[:, :], oc32[:, :]))(),
                        waits=[(sVE, vv), (sTE, oc_free)], inc=sVE)
            for k in range(8):
                ts = k % 2
                tv = p.emit("tensor",
                            (lambda k, ts: lambda e: e.transpose(
                                pPT[ts][:, :], ocb[:, k * 128:(k + 1) * 128],
                                com.identbf[:, :]))(k, ts),
                            waits=[(sVE, cv), (sVE, pT_free[ts])], inc=sTE)
                if k == 7:
                    oc_free = tv
                pT_free[ts] = oc_last = p.emit(
                    "vector",
                    (lambda k, qt, ts: lambda e: e.tensor_copy(
                        ocfm[:, k * 512 + qt * 128: k * 512 + (qt + 1) * 128],
                        pPT[ts][:, :]))(k, qt, ts),
                    waits=[(sTE, tv), (sTE, ocfm_free)], inc=sVE)
        y1v = p.sem("sVE").v
        for dch in range(8):
            tv = None
            for c in range(8):
                tv = p.emit("tensor",
                            (lambda dch, c: lambda e: e.matmul(
                                pB[:, :],
                                wpT[:, c * D + dch * 128: c * D + (dch + 1) * 128],
                                ocfm[:, c * 512:(c + 1) * 512],
                                start=(c == 0), stop=(c == 7)))(dch, c),
                            waits=[(sVE, oc_last), (sVE, y1_free),
                                   (sVE, bias1_ready), (sVE, y1v)],
                            inc=(sTE if c == 7 else None))
            if dch == 7:
                ocfm_free = tv
            y1v = p.emit("vector",
                         (lambda dch: lambda e: e.tensor_scalar(
                             y1bf[:, dch * 512:(dch + 1) * 512], pB[:, :],
                             bias1c[:, dch:dch + 1], None, op0=ALU.add))(dch),
                         waits=[(sTE, tv)], inc=sVE)
        for qt in (range(4) if "resid" not in B3_CUT else range(0)):
            b = g * 4 + qt
            slot = b % 2
            lv4 = p.emit("sync",
                         (lambda b, slot: lambda e: e.dma_start(
                             xtm[slot][:, :],
                             io["stream_in"].ap()[b * 128:(b + 1) * 128, :]))(b, slot),
                         waits=[(sVE, xtm_free[slot])], inc=sXT[slot], k=16)
            av2 = p.emit("vector",
                         (lambda b, slot: lambda e: e.tensor_scalar(
                             xtm[slot][:, :], xtm[slot][:, :],
                             absb[:, b:b + 1], absb[:, 32 + b:32 + b + 1],
                             op0=ALU.mult, op1=ALU.add))(b, slot),
                         waits=[(sXT[slot], lv4), (sIN, p.sem("sIN").v)], inc=sVE)
            last_y2 = None
            for dh in range(2):
                for c in range(8):
                    last_y2 = p.emit(
                        "tensor",
                        (lambda qt, dh, c: lambda e: e.matmul(
                            pO[:, dh * 512:(dh + 1) * 512],
                            y1bf[:, c * 512 + qt * 128: c * 512 + (qt + 1) * 128],
                            wmpT[:, c * D + dh * 512: c * D + (dh + 1) * 512],
                            start=(c == 0), stop=(c == 7)))(qt, dh, c),
                        waits=[(sVE, y1v), (sVE, wmpT_ready), (sVE, pO_free)],
                        inc=(sTE if (c == 7 and dh == 1) else None))
            if qt == 3:
                y1_free = last_y2
            xv = p.emit("vector",
                        (lambda slot: lambda e: e.tensor_tensor(
                            xres[slot][:, :], pO[:, :], xtm[slot][:, :],
                            op=ALU.add))(slot),
                        waits=[(sTE, last_y2), (sVE, av2),
                               (sXS[slot], xres_free[slot])], inc=sVE)
            pO_free = xv
            xtm_free[slot] = xv
            p.emit("vector",
                   (lambda slot, b: lambda e: e.reduce_sum(
                       st1[:, b:b + 1], xres[slot][:, :], axis=AX.X))(slot, b),
                   waits=[(sVE, xv)], inc=sVE)
            sqv = p.emit("scalar",
                         (lambda slot, b: lambda e: e.activation(
                             sqs[:, :], xres[slot][:, :], AF.Square,
                             accum_out=st2[:, b:b + 1]))(slot, b),
                         waits=[(sVE, xv), (sVE, sVE.v)], inc=sSE)
            xres_free[slot] = p.emit(
                "gpsimd",
                (lambda slot, b: lambda e: e.dma_start(
                    io["stream_out"].ap()[b * 128:(b + 1) * 128, :],
                    xres[slot][:, :]))(slot, b),
                waits=[(sVE, xv), (sSE, sqv)], inc=sXS[slot], k=16)

    if "resid" not in B3_CUT:
        emit_ab(p, stack, io["ab_out"], st1, st2,
                stats_ready=[(sVE, p.sem("sVE").v), (sSE, p.sem("sSE").v)], pfx="p3",
                store_sem=sAB)
    p.emit("gpsimd", lambda e: e.engine_nop(),
           waits=[(sXS[0], sXS[0].v), (sXS[1], sXS[1].v), (sAB, sAB.v)])


def build_phase3b_program():
    nc = bass.Bass("TRN2", target_bir_lowering=False)
    io = {
        "stream_in": nc.dram_tensor("stream_in", [TOK, D], F32, kind="ExternalInput"),
        "ab_in": nc.dram_tensor("ab_in", [128, 64], F32, kind="ExternalInput"),
        "xa_q": nc.dram_tensor("xa_q", [8 * 128, TOK], BF16, kind="ExternalInput"),
        "k_all": nc.dram_tensor("k_all", [8 * 128, 2 * TOK], BF16, kind="ExternalInput"),
        "v_all": nc.dram_tensor("v_all", [2 * TOK, D], BF16, kind="ExternalInput"),
        "stats_all": nc.dram_tensor("stats_all", [8 * 128, 48], F32, kind="ExternalInput"),
        "zmask": nc.dram_tensor("zmask", [128, 2], F32, kind="ExternalInput"),
        "trif": nc.dram_tensor("trif", [128, 128], F32, kind="ExternalInput"),
        "W_proj": nc.dram_tensor("W_proj", [D, D], F32, kind="ExternalInput"),
        "lora_proj_A": nc.dram_tensor("lora_proj_A", [D, 64], F32, kind="ExternalInput"),
        "lora_proj_B": nc.dram_tensor("lora_proj_B", [64, D], F32, kind="ExternalInput"),
        "bn_gamma": nc.dram_tensor("bn_gamma", [1, 3072], F32, kind="ExternalInput"),
        "bn_beta": nc.dram_tensor("bn_beta", [1, 3072], F32, kind="ExternalInput"),
        "stream_out": nc.dram_tensor("stream_out", [TOK, D], F32, kind="ExternalOutput"),
        "ab_out": nc.dram_tensor("ab_out", [128, 64], F32, kind="ExternalOutput"),
    }
    if "dbg" in B3_CUT:
        io["dbg_ml"] = nc.dram_tensor("dbg_ml", [128, 16], F32, kind="ExternalOutput")
        io["dbg_O"] = nc.dram_tensor("dbg_O", [128, D], F32, kind="ExternalOutput")
        io["dbg_S"] = nc.dram_tensor("dbg_S", [128, D], F32, kind="ExternalOutput")
        io["dbg_mt"] = nc.dram_tensor("dbg_mt", [128, 39 * 128], BF16, kind="ExternalOutput")
        io["dbg_z"] = nc.dram_tensor("dbg_z", [128, 2], F32, kind="ExternalOutput")
    with ExitStack() as stack:
        p = Prog(nc, stack)
        with nc.Block() as block:
            p.common = Common(p, stack, need_ones512=False)
            build_l1_attn_b(p, stack, io)
            p.build(block)
    return nc


def phase3b_inmaps(inputs, streams, abs_, qk_list, v_list, stats_list):
    Wp = np.ascontiguousarray(np.asarray(inputs["L1_attn_W_proj"], np.float32))
    lA = np.ascontiguousarray(np.asarray(inputs["L1_attn_lora_proj_A"], np.float32))
    lB = np.ascontiguousarray(np.asarray(inputs["L1_attn_lora_proj_B"], np.float32))
    gam = np.asarray(inputs["L1_attn_bn_gamma"], np.float32).reshape(1, 3072)
    bet = np.asarray(inputs["L1_attn_bn_beta"], np.float32).reshape(1, 3072)
    stats_all = np.ascontiguousarray(
        np.concatenate([stats_list[c] for c in range(8)], axis=0))
    ci = const_inputs()
    maps = []
    for c in range(8):
        b, h = c // 2, c % 2
        c0, c1 = 2 * b, 2 * b + 1
        k_all = np.ascontiguousarray(
            np.concatenate([qk_list[c0][8 * 128:], qk_list[c1][8 * 128:]], axis=1))
        v_all = np.ascontiguousarray(
            np.concatenate([v_list[c0], v_list[c1]], axis=0))
        maps.append({
            "stream_in": streams[c], "ab_in": abs_[c],
            "xa_q": np.ascontiguousarray(qk_list[c][0:8 * 128]),
            "k_all": k_all, "v_all": v_all, "stats_all": stats_all,
            "zmask": np.ascontiguousarray(np.broadcast_to(
                np.array([1.0 if h == 1 else 0.0,
                          0.0 if h == 1 else 1.0], np.float32), (128, 2))),
            "W_proj": Wp, "lora_proj_A": lA, "lora_proj_B": lB,
            "bn_gamma": gam, "bn_beta": bet,
            "trif": ci["trif"],
            "c_identbf": ci["c_identbf"], "c_identf": ci["c_identf"],
        })
    return maps



def build_l1_ffn(p, stack, io):
    nc = p.nc
    com = p.common
    sIN = p.sem("sIN")
    sTE, sVE, sSE = p.sem("sTE"), p.sem("sVE"), p.sem("sSE")
    E, H = 8, 4096
    NHC = H // 128
    NQ = 4
    MEGA = 1024
    NMEGA = TOK // MEGA
    TPM = MEGA // 512

    absb = stack.enter_context(nc.sbuf_tensor("g_absb", [128, 64], F32))
    gates = stack.enter_context(nc.sbuf_tensor("g_gates", [128, NB * 8], F32))
    gfm = stack.enter_context(nc.sbuf_tensor("g_gfm", [8, NB * 128], BF16))
    b1c = stack.enter_context(nc.sbuf_tensor("g_b1c", [128, NHC * 8], F32))
    b2bf = stack.enter_context(nc.sbuf_tensor("g_b2bf", [8, D], BF16))
    wgbf = stack.enter_context(nc.sbuf_tensor("g_wgbf", [128, 64], BF16))
    bgb = stack.enter_context(nc.sbuf_tensor("g_bgb", [128, 8], F32))
    gc = stack.enter_context(nc.sbuf_tensor("g_gcols", [128, 16], F32))
    (nmcol, escol, rscol, m1c, rcol, bcol, mcol, vcol, epsc) = \
        [gc[:, j:j + 1] for j in range(9)]
    gsc = stack.enter_context(nc.sbuf_tensor("g_gsc", [128, 24], F32))
    lg, pr, mk1 = [gsc[:, j * 8:(j + 1) * 8] for j in range(3)]
    xtm = [stack.enter_context(nc.sbuf_tensor(f"g_xtm{i}", [128, D], F32)) for i in range(2)]
    xbf = stack.enter_context(nc.sbuf_tensor("g_xbf", [128, D], BF16))
    xfmb = stack.enter_context(nc.sbuf_tensor("g_xfmb", [128, 8 * 512], BF16))
    xfmt = [stack.enter_context(nc.sbuf_tensor(f"g_xfmt{i}", [128, 8 * 512], BF16))
            for i in range(2)]
    hsb = stack.enter_context(nc.sbuf_tensor("g_hsb", [128, 8 * 512], BF16))
    y_acc = stack.enter_context(nc.sbuf_tensor("g_yacc", [128, 8 * D], F32))
    w1q = [stack.enter_context(nc.sbuf_tensor(f"g_w1q{i}", [128, 8 * D], BF16))
           for i in range(2)]
    w2q = [stack.enter_context(nc.sbuf_tensor(f"g_w2q{i}", [128, 8 * D], BF16))
           for i in range(2)]
    xres = [stack.enter_context(nc.sbuf_tensor(f"g_xres{i}", [128, D], F32))
            for i in range(2)]
    outb = [stack.enter_context(nc.sbuf_tensor(f"g_outb{i}", [128, D], F32))
            for i in range(2)]
    sqs = stack.enter_context(nc.sbuf_tensor("g_sqs", [128, D], F32))

    pH = [stack.enter_context(nc.psum_tensor(f"g_pH{i}", [128, 512], F32))
          for i in range(2)]
    psB = ExitStack()
    pG = psB.enter_context(nc.psum_tensor("g_pG", [128, 8], F32))
    pTf = psB.enter_context(nc.psum_tensor("g_pTf", [128, 128], F32))
    pT = [psB.enter_context(nc.psum_tensor(f"g_pT{i}", [128, 128], BF16))
          for i in range(2)]

    sXT = [p.sem("sXT0"), p.sem("sXT1")]
    sXF = [p.sem("sXF0"), p.sem("sXF1")]
    sW1s = [p.sem("sW1a"), p.sem("sW1b")]
    sW2s = [p.sem("sW2a"), p.sem("sW2b")]
    sXS = [p.sem("sXS0"), p.sem("sXS1")]
    sSP = p.sem("sSP")
    sB1 = p.sem("sB1")
    sWG = p.sem("sWG")

    vin = p.emit("sync", lambda e: e.dma_start(absb[:, :], io["ab_in"].ap()),
                 inc=sIN, k=16)
    for a in range(8):
        p.emit("gpsimd",
               (lambda a: lambda e: e.dma_start(
                   wgbf[:, a * 8:(a + 1) * 8],
                   io["wg"].ap()[a * 128:(a + 1) * 128, :]))(a),
               inc=sWG, k=16)
    p.emit("gpsimd", lambda e: e.dma_start(b2bf[:, :], io["b2"].ap()), inc=sWG, k=16)
    wg_ready = p.sem("sWG").v
    with ExitStack() as prep:
        bgrow = prep.enter_context(nc.sbuf_tensor("g_bgrow", [1, 8], F32))
        b1sb = prep.enter_context(nc.sbuf_tensor("g_b1sb", [8, 1024], F32))
        v1 = p.emit("sync", lambda e: e.dma_start(bgrow[:, :], io["bg"].ap()),
                    inc=sIN, k=16)
        vin = v1
        tv = p.emit("tensor", lambda e: e.matmul(pG[:, :], com.ones_row[0:1, 0:128],
                                                 bgrow[0:1, :], start=True, stop=True),
                    waits=[(sIN, vin), (sVE, com.ready_ve)], inc=sTE)
        bg_ready = p.emit("vector", lambda e: e.tensor_copy(bgb[:, :], pG[:, :]),
                          waits=[(sTE, tv)], inc=sVE)
        b1c_ready = 0
        ptf_free = 0
        b1_used = 0
        for hh in range(4):
            v2 = p.emit("sync",
                        (lambda hh: lambda e: e.dma_start(
                            b1sb[:, :], io["b1"].ap()[:, hh * 1024:(hh + 1) * 1024]))(hh),
                        waits=[(sTE, b1_used)], inc=sB1, k=16)
            for mm in range(8):
                m = hh * 8 + mm
                b1_used = tv = p.emit(
                    "tensor",
                    (lambda mm: lambda e: e.transpose(
                        pTf[0:128, 0:8], b1sb[:, mm * 128:(mm + 1) * 128],
                        com.identf[0:8, 0:8]))(mm),
                    waits=[(sB1, v2), (sIN, vin), (sVE, ptf_free)],
                    inc=sTE)
                b1c_ready = ptf_free = p.emit(
                    "vector",
                    (lambda m: lambda e: e.tensor_copy(
                        b1c[:, m * 8:(m + 1) * 8], pTf[0:128, 0:8]))(m),
                    waits=[(sTE, tv)], inc=sVE)

        xtm_free = [0, 0]
        xbf_free = 0
        pT_free = [0, 0]
        xfmb_free = 0
        gfm_last = 0
        for i in range(NT):
            for b4 in range(4):
                b = 4 * i + b4
                slot = b % 2
                lv = p.emit("sync",
                            (lambda b, slot: lambda e: e.dma_start(
                                xtm[slot][:, :],
                                io["stream_in"].ap()[b * 128:(b + 1) * 128, :]))(b, slot),
                            waits=[(sVE, xtm_free[slot])], inc=sXT[slot], k=16)
                av = p.emit("vector",
                            (lambda b, slot: lambda e: e.tensor_scalar(
                                xtm[slot][:, :], xtm[slot][:, :],
                                absb[:, b:b + 1], absb[:, 32 + b:32 + b + 1],
                                op0=ALU.mult, op1=ALU.add))(b, slot),
                            waits=[(sXT[slot], lv), (sIN, vin)], inc=sVE)
                cv = p.emit("vector",
                            (lambda slot: lambda e: e.tensor_copy(
                                xbf[:, :], xtm[slot][:, :]))(slot),
                            waits=[(sVE, av), (sTE, xbf_free)], inc=sVE)
                xtm_free[slot] = cv
                for k in range(8):
                    ts = k % 2
                    tv = p.emit("tensor",
                                (lambda k, ts: lambda e: e.transpose(
                                    pT[ts][:, :], xbf[:, k * 128:(k + 1) * 128],
                                    com.identbf[:, :]))(k, ts),
                                waits=[(sVE, cv), (sVE, pT_free[ts]),
                                       (sIN, com.ready_in)], inc=sTE)
                    if k == 7:
                        xbf_free = tv
                    pT_free[ts] = p.emit(
                        "vector",
                        (lambda k, b4, ts: lambda e: e.tensor_copy(
                            xfmb[:, k * 512 + b4 * 128: k * 512 + (b4 + 1) * 128],
                            pT[ts][:, :]))(k, b4, ts),
                        waits=[(sTE, tv), (sSP, xfmb_free)], inc=sVE)
                fm_b_ready = sVE.v
                gv = None
                for k in range(8):
                    gv = p.emit("tensor",
                                (lambda k, b4: lambda e: e.matmul(
                                    pG[:, :],
                                    xfmb[:, k * 512 + b4 * 128: k * 512 + (b4 + 1) * 128],
                                    wgbf[:, k * 8:(k + 1) * 8],
                                    start=(k == 0), stop=(k == 7)))(k, b4),
                                waits=[(sVE, fm_b_ready), (sWG, wg_ready),
                                       (sVE, bg_ready)],
                                inc=(sTE if k == 7 else None))
                vv = p.emit("vector", lambda e: e.tensor_tensor(lg, pG[:, :], bgb[:, :],
                                                                op=ALU.add),
                            waits=[(sTE, gv)], inc=sVE)
                vv = p.emit("vector", lambda e: e.reduce_max(nmcol, lg, axis=AX.X,
                                                             negate=True),
                            waits=[(sVE, vv)], inc=sVE)
                sv = p.emit("scalar", lambda e: e.activation(pr, lg, AF.Exp,
                                                             bias=nmcol),
                            waits=[(sVE, vv)], inc=sSE)
                vv = p.emit("vector", lambda e: e.reduce_sum(escol, pr, axis=AX.X),
                            waits=[(sSE, sv)], inc=sVE)
                vv = p.emit("vector", lambda e: e.reciprocal(rscol, escol),
                            waits=[(sVE, vv)], inc=sVE)
                vv = p.emit("vector", lambda e: e.tensor_scalar(pr, pr, rscol, None,
                                                                op0=ALU.mult),
                            waits=[(sVE, vv)], inc=sVE)
                vv = p.emit("vector", lambda e: e.reduce_max(m1c, pr, axis=AX.X),
                            waits=[(sVE, vv)], inc=sVE)
                vv = p.emit("vector", lambda e: e.tensor_scalar(mk1, pr, m1c, None,
                                                                op0=ALU.is_ge),
                            waits=[(sVE, vv)], inc=sVE)
                vv = p.emit("vector",
                            (lambda b: lambda e: e.tensor_scalar(
                                gates[:, b * 8:(b + 1) * 8], mk1, m1c, None,
                                op0=ALU.mult))(b),
                            waits=[(sVE, vv)], inc=sVE)
                tv = p.emit("tensor",
                            (lambda b: lambda e: e.transpose(
                                pTf[0:8, :], gates[:, b * 8:(b + 1) * 8],
                                com.identf[:, :]))(b),
                            waits=[(sVE, vv), (sVE, ptf_free)], inc=sTE)
                gfm_last = ptf_free = p.emit(
                    "vector",
                    (lambda b: lambda e: e.tensor_copy(
                        gfm[0:8, b * 128:(b + 1) * 128], pTf[0:8, 0:128]))(b),
                    waits=[(sTE, tv)], inc=sVE)
            xfmb_free = p.emit(
                "gpsimd",
                (lambda i: lambda e: e.dma_start(
                    bass.AP(io["xfm"], i * 512,
                            [[TOK, 128], [128 * TOK, 8], [1, 512]]),
                    xfmb[:, :].rearrange("p (k t) -> p k t", k=8)))(i),
                waits=[(sVE, sVE.v)], inc=sSP, k=16)

    psB.close()
    pYs = [stack.enter_context(nc.psum_tensor(f"g_pY{i}", [128, 1024], F32))
           for i in range(2)]
    psum_gate = p.sem("sVE").v

    def load_q(e, qu, par, g1, g2):
        l1 = l2 = 0
        for k in range(8):
            l1 = p.emit(
                "gpsimd",
                (lambda e, qu, k, par: lambda e_: e_.dma_start(
                    w1q[par][:, k * D:(k + 1) * D],
                    io["W1"].ap()[e * D + k * 128: e * D + (k + 1) * 128,
                                  qu * 1024:(qu + 1) * 1024]))(e, qu, k, par),
                waits=[(sTE, g1)], inc=sW1s[par], k=16)
        for k in range(8):
            l2 = p.emit(
                "gpsimd",
                (lambda e, qu, k, par: lambda e_: e_.dma_start(
                    w2q[par][:, k * D:(k + 1) * D],
                    io["W2"].ap()[e * H + qu * 1024 + k * 128:
                                  e * H + qu * 1024 + (k + 1) * 128, :]))(e, qu, k, par),
                waits=[(sTE, g2)], inc=sW2s[par], k=16)
        return l1, l2

    seq = [(mg, e, qu) for mg in range(NMEGA) for e in range(E) for qu in range(NQ)]
    ready1 = {}
    ready2 = {}
    w1_users = [0, 0]
    w2_users = [0, 0]
    ready1[0], ready2[0] = load_q(seq[0][1], seq[0][2], 0, 0, 0)
    ready1[1], ready2[1] = load_q(seq[1][1], seq[1][2], 1, 0, 0)
    xt_free = [0, 0]
    pH_free = [0, 0]
    hsb_free = 0
    pY_frees = [0, 0]
    yacc_wr = 0
    xres_free = [0, 0]
    for idx, (mg, e, qu) in enumerate(seq):
        par = idx % 2
        if e == 0 and qu == 0:
            for sb in range(MEGA // 128):
                b = mg * (MEGA // 128) + sb
                ys = sb % 2
                tv = None
                for dh in range(2):
                    tv = p.emit("tensor",
                                (lambda b, dh, ys: lambda e_: e_.matmul(
                                    pYs[ys][:, dh * 512:(dh + 1) * 512],
                                    gfm[0:8, b * 128:(b + 1) * 128],
                                    b2bf[0:8, dh * 512:(dh + 1) * 512],
                                    start=True, stop=True))(b, dh, ys),
                                waits=[(sVE, gfm_last), (sWG, wg_ready),
                                       (sVE, pY_frees[ys]), (sVE, psum_gate)],
                                inc=sTE)
                pY_frees[ys] = yacc_wr = p.emit(
                    "vector",
                    (lambda sb, ys: lambda e_: e_.tensor_copy(
                        y_acc[:, sb * D:(sb + 1) * D], pYs[ys][:, :]))(sb, ys),
                    waits=[(sTE, tv), (sXS[0], xres_free[0]),
                           (sXS[1], xres_free[1])], inc=sVE)
        for ti in range(TPM):
            i = mg * TPM + ti
            slot = ti % 2
            lxv = p.emit(
                "sync",
                (lambda i, slot: lambda e_: e_.dma_start(
                    xfmt[slot][:, :].rearrange("p (k t) -> p k t", k=8),
                    bass.AP(io["xfm"], i * 512,
                            [[TOK, 128], [128 * TOK, 8], [1, 512]])))(i, slot),
                waits=[(sTE, xt_free[slot]), (sSP, xfmb_free)], inc=sXF[slot], k=16)
            hv = None
            for mm in range(8):
                hs = mm % 2
                tv = None
                for k in range(8):
                    tv = p.emit(
                        "tensor",
                        (lambda mm, k, hs, slot, par: lambda e_: e_.matmul(
                            pH[hs][:, :],
                            w1q[par][:, k * D + mm * 128: k * D + (mm + 1) * 128],
                            xfmt[slot][:, k * 512:(k + 1) * 512],
                            start=(k == 0), stop=(k == 7)))(mm, k, hs, slot, par),
                        waits=[(sXF[slot], lxv), (sW1s[par], ready1[idx]),
                               (sSE, pH_free[hs]), (sTE, hsb_free)],
                        inc=(sTE if k == 7 else None))
                    if mm == 7 and k == 7:
                        if ti == TPM - 1:
                            w1_users[par] = tv
                        xt_free[slot] = tv
                hv = p.emit("scalar",
                            (lambda mm, hs, qu, e: lambda e_: e_.activation(
                                hsb[:, mm * 512:(mm + 1) * 512], pH[hs][:, :],
                                AF.Relu,
                                bias=b1c[:, (qu * 8 + mm) * 8 + e:
                                         (qu * 8 + mm) * 8 + e + 1]))(mm, hs, qu, e),
                            waits=[(sTE, tv), (sVE, b1c_ready),
                                   (sTE, hsb_free)], inc=sSE)
                pH_free[hs] = hv
            for b4 in range(4):
                sb = ti * 4 + b4
                b = i * 4 + b4
                ys = b4 % 2
                tv = None
                for dh in range(2):
                    for mm in range(8):
                        tv = p.emit(
                            "tensor",
                            (lambda mm, b4, dh, par, ys: lambda e_: e_.matmul(
                                pYs[ys][:, dh * 512:(dh + 1) * 512],
                                hsb[:, mm * 512 + b4 * 128: mm * 512 + (b4 + 1) * 128],
                                w2q[par][:, mm * D + dh * 512: mm * D + (dh + 1) * 512],
                                start=(mm == 0), stop=(mm == 7)))(mm, b4, dh, par, ys),
                            waits=[(sSE, hv), (sW2s[par], ready2[idx]),
                                   (sVE, pY_frees[ys])],
                            inc=(sTE if mm == 7 else None))
                if b4 == 3:
                    hsb_free = tv
                    if ti == TPM - 1:
                        w2_users[par] = tv
                pY_frees[ys] = yacc_wr = p.emit(
                    "vector",
                    (lambda sb, b, e, ys: lambda e_: e_.scalar_tensor_tensor(
                        y_acc[:, sb * D:(sb + 1) * D], pYs[ys][:, :],
                        gates[:, b * 8 + e:b * 8 + e + 1],
                        y_acc[:, sb * D:(sb + 1) * D],
                        op0=ALU.mult, op1=ALU.add))(sb, b, e, ys),
                    waits=[(sTE, tv)], inc=sVE)
        if idx + 2 < len(seq):
            ne = seq[idx + 2][1]
            nqu = seq[idx + 2][2]
            ready1[idx + 2], ready2[idx + 2] = load_q(ne, nqu, par,
                                                      w1_users[par], w2_users[par])
        if e == E - 1 and qu == NQ - 1:
            for sb in range(MEGA // 128):
                b = mg * (MEGA // 128) + sb
                slot = b % 2
                lv = p.emit("sync",
                            (lambda b, slot: lambda e_: e_.dma_start(
                                xtm[slot][:, :],
                                io["stream_in"].ap()[b * 128:(b + 1) * 128, :]))(b, slot),
                            waits=[(sVE, xtm_free[slot])], inc=sXT[slot], k=16)
                av = p.emit("vector",
                            (lambda b, slot: lambda e_: e_.tensor_scalar(
                                xtm[slot][:, :], xtm[slot][:, :],
                                absb[:, b:b + 1], absb[:, 32 + b:32 + b + 1],
                                op0=ALU.mult, op1=ALU.add))(b, slot),
                            waits=[(sXT[slot], lv)], inc=sVE)
                xv = p.emit("vector",
                            (lambda sb, slot: lambda e_: e_.tensor_tensor(
                                xres[slot][:, :], y_acc[:, sb * D:(sb + 1) * D],
                                xtm[slot][:, :], op=ALU.add))(sb, slot),
                            waits=[(sVE, av), (sVE, yacc_wr)], inc=sVE)
                xtm_free[slot] = xv
                vv = p.emit("vector",
                            (lambda slot: lambda e_: e_.reduce_sum(
                                mcol, xres[slot][:, :], axis=AX.X))(slot),
                            waits=[(sVE, xv)], inc=sVE)
                sqv = p.emit("scalar",
                             (lambda slot: lambda e_: e_.activation(
                                 sqs[:, :], xres[slot][:, :], AF.Square))(slot),
                             waits=[(sVE, xv), (sSE, sSE.v)], inc=sSE)
                vv = p.emit("vector",
                            lambda e_: e_.reduce_sum(vcol, sqs[:, :], axis=AX.X),
                            waits=[(sSE, sqv), (sVE, vv)], inc=sVE)
                vv = p.emit("vector",
                            lambda e_: e_.tensor_scalar(mcol, mcol, 1.0 / D, None,
                                                        op0=ALU.mult),
                            waits=[(sVE, vv)], inc=sVE)
                vv = p.emit("vector",
                            lambda e_: e_.tensor_scalar(vcol, vcol, 1.0 / D, None,
                                                        op0=ALU.mult),
                            waits=[(sVE, vv)], inc=sVE)
                vv = p.emit("vector",
                            lambda e_: e_.scalar_tensor_tensor(
                                rcol, mcol, -1.0, mcol, op0=ALU.mult, op1=ALU.mult),
                            waits=[(sVE, vv)], inc=sVE)
                vv = p.emit("vector",
                            lambda e_: e_.tensor_tensor(vcol, vcol, rcol, op=ALU.add),
                            waits=[(sVE, vv)], inc=sVE)
                vv = p.emit("vector", lambda e_: e_.memset(epsc, 1e-5),
                            waits=[(sVE, vv)], inc=sVE)
                sv = p.emit("scalar",
                            lambda e_: e_.activation(rcol, vcol, AF.Ln, bias=epsc),
                            waits=[(sVE, vv)], inc=sSE)
                vv = p.emit("vector",
                            lambda e_: e_.tensor_scalar(rcol, rcol, -0.5, None,
                                                        op0=ALU.mult),
                            waits=[(sSE, sv)], inc=sVE)
                sv = p.emit("scalar", lambda e_: e_.activation(rcol, rcol, AF.Exp),
                            waits=[(sVE, vv)], inc=sSE)
                vv = p.emit("vector",
                            lambda e_: e_.scalar_tensor_tensor(
                                bcol, mcol, -1.0, rcol, op0=ALU.mult, op1=ALU.mult),
                            waits=[(sSE, sv)], inc=sVE)
                ov = p.emit("vector",
                            (lambda slot: lambda e_: e_.tensor_scalar(
                                outb[slot][:, :], xres[slot][:, :], rcol, bcol,
                                op0=ALU.mult, op1=ALU.add))(slot),
                            waits=[(sVE, vv), (sXS[slot], xres_free[slot])],
                            inc=sVE)
                xres_free[slot] = p.emit(
                    "gpsimd",
                    (lambda slot, b: lambda e_: e_.dma_start(
                        io["out"].ap()[b * 128:(b + 1) * 128, :],
                        outb[slot][:, :]))(slot, b),
                    waits=[(sVE, ov)], inc=sXS[slot], k=16)
    p.emit("gpsimd", lambda e: e.engine_nop(),
           waits=[(sXS[0], sXS[0].v), (sXS[1], sXS[1].v)])


def build_phase5_program():
    nc = bass.Bass("TRN2", target_bir_lowering=False)
    io = {
        "stream_in": nc.dram_tensor("stream_in", [TOK, D], F32, kind="ExternalInput"),
        "ab_in": nc.dram_tensor("ab_in", [128, 64], F32, kind="ExternalInput"),
        "wg": nc.dram_tensor("wg", [D, 8], F32, kind="ExternalInput"),
        "bg": nc.dram_tensor("bg", [1, 8], F32, kind="ExternalInput"),
        "W1": nc.dram_tensor("W1", [8 * D, 4096], F32, kind="ExternalInput"),
        "b1": nc.dram_tensor("b1", [8, 4096], F32, kind="ExternalInput"),
        "W2": nc.dram_tensor("W2", [8 * 4096, D], F32, kind="ExternalInput"),
        "b2": nc.dram_tensor("b2", [8, D], F32, kind="ExternalInput"),
        "xfm": nc.dram_tensor("xfm", [8 * 128, TOK], BF16, kind="Internal"),
        "out": nc.dram_tensor("out", [TOK, D], F32, kind="ExternalOutput"),
    }
    with ExitStack() as stack:
        p = Prog(nc, stack)
        with nc.Block() as block:
            p.common = Common(p, stack, need_stg=False, need_ones512=False)
            build_l1_ffn(p, stack, io)
            p.build(block)
    return nc


def phase5_inmaps(inputs, streams, abs_):
    wg = np.ascontiguousarray(np.asarray(inputs["L1_ffn_w_gate"], np.float32))
    bg = np.asarray(inputs["L1_ffn_b_gate"], np.float32).reshape(1, 8)
    W1 = np.ascontiguousarray(np.asarray(inputs["L1_ffn_W1"], np.float32).reshape(8 * D, 4096))
    b1 = np.ascontiguousarray(np.asarray(inputs["L1_ffn_b1"], np.float32))
    W2 = np.ascontiguousarray(np.asarray(inputs["L1_ffn_W2"], np.float32).reshape(8 * 4096, D))
    b2 = np.ascontiguousarray(np.asarray(inputs["L1_ffn_b2"], np.float32))
    ci = const_inputs()
    return [{"stream_in": streams[c], "ab_in": abs_[c], "wg": wg, "bg": bg,
             "W1": W1, "b1": b1, "W2": W2, "b2": b2,
             "c_identbf": ci["c_identbf"], "c_identf": ci["c_identf"]}
            for c in range(8)]



def build_phase1_program():
    nc = bass.Bass("TRN2", target_bir_lowering=False)
    io = {
        "x_own": nc.dram_tensor("x_own", [TOK, D], F32, kind="ExternalInput"),
        "x_pre": nc.dram_tensor("x_pre", [TOK, D], F32, kind="ExternalInput"),
        "zmask": nc.dram_tensor("zmask", [1, 1], F32, kind="ExternalInput"),
        "W1": nc.dram_tensor("W1", [D, D], F32, kind="ExternalInput"),
        "b1": nc.dram_tensor("b1", [1, D], F32, kind="ExternalInput"),
        "w2": nc.dram_tensor("w2", [1, D], F32, kind="ExternalInput"),
        "b2": nc.dram_tensor("b2", [1, 1], F32, kind="ExternalInput"),
        "Wc": nc.dram_tensor("Wc", [D, 2 * D], F32, kind="ExternalInput"),
        "bc": nc.dram_tensor("bc", [1, D], F32, kind="ExternalInput"),
        "stream_out": nc.dram_tensor("stream_out", [TOK, D], F32, kind="ExternalOutput"),
        "ab_out": nc.dram_tensor("ab_out", [128, 64], F32, kind="ExternalOutput"),
    }
    if DEBUG_P1:
        io["dbg_cols"] = nc.dram_tensor("dbg_cols", [128, 4 * NB], F32, kind="ExternalOutput")
        io["dbg_misc"] = nc.dram_tensor("dbg_misc", [1, 144], F32, kind="ExternalOutput")
        io["dbg_carry"] = nc.dram_tensor("dbg_carry", [128, 8], F32, kind="ExternalOutput")
        io["dbg_ctx"] = nc.dram_tensor("dbg_ctx", [128, 8 * 512], BF16, kind="ExternalOutput")
        io["dbg_rden"] = nc.dram_tensor("dbg_rden", [128, 512], F32, kind="ExternalOutput")
        io["dbg_o2"] = nc.dram_tensor("dbg_o2", [128, D], F32, kind="ExternalOutput")
        io["dbg_c8"] = nc.dram_tensor("dbg_c8", [128, 72], F32, kind="ExternalOutput")
        io["dbg_wct"] = nc.dram_tensor("dbg_wct", [128, 16 * D], BF16, kind="ExternalOutput")
        io["dbg_bcb"] = nc.dram_tensor("dbg_bcb", [128, D], F32, kind="ExternalOutput")
    with ExitStack() as stack:
        p = Prog(nc, stack)
        with nc.Block() as block:
            p.common = Common(p, stack)
            build_l0_attn(p, stack, io)
            p.build(block)
    return nc


def build_phase2_program():
    nc = bass.Bass("TRN2", target_bir_lowering=False)
    io = {
        "stream_in": nc.dram_tensor("stream_in", [TOK, D], F32, kind="ExternalInput"),
        "ab_in": nc.dram_tensor("ab_in", [128, 64], F32, kind="ExternalInput"),
        "Wg": nc.dram_tensor("Wg", [D, 8], F32, kind="ExternalInput"),
        "bg": nc.dram_tensor("bg", [1, 8], F32, kind="ExternalInput"),
        "W1": nc.dram_tensor("W1", [8 * D, 2048], F32, kind="ExternalInput"),
        "b1": nc.dram_tensor("b1", [8, 2048], F32, kind="ExternalInput"),
        "W2": nc.dram_tensor("W2", [8 * 2048, D], F32, kind="ExternalInput"),
        "b2": nc.dram_tensor("b2", [8, D], F32, kind="ExternalInput"),
        "xfm": nc.dram_tensor("xfm", [8 * 128, TOK], BF16, kind="Internal"),
        "stream_out": nc.dram_tensor("stream_out", [TOK, D], F32, kind="ExternalOutput"),
        "ab_out": nc.dram_tensor("ab_out", [128, 64], F32, kind="ExternalOutput"),
    }
    with ExitStack() as stack:
        p = Prog(nc, stack)
        with nc.Block() as block:
            p.common = Common(p, stack, need_stg=False, need_ones512=False)
            build_l0_ffn(p, stack, io)
            p.build(block)
    return nc


def phase2_inmaps(inputs, streams, abs_):
    Wg = np.ascontiguousarray(np.asarray(inputs["L0_ffn_Wg"], np.float32))
    bg = np.asarray(inputs["L0_ffn_bg"], np.float32).reshape(1, 8)
    W1 = np.ascontiguousarray(np.asarray(inputs["L0_ffn_W1"], np.float32).reshape(8 * D, 2048))
    b1 = np.ascontiguousarray(np.asarray(inputs["L0_ffn_b1"], np.float32))
    W2 = np.ascontiguousarray(np.asarray(inputs["L0_ffn_W2"], np.float32).reshape(8 * 2048, D))
    b2 = np.ascontiguousarray(np.asarray(inputs["L0_ffn_b2"], np.float32))
    ci = const_inputs()
    return [{"stream_in": streams[c], "ab_in": abs_[c], "Wg": Wg, "bg": bg,
             "W1": W1, "b1": b1, "W2": W2, "b2": b2,
             "c_identbf": ci["c_identbf"], "c_identf": ci["c_identf"]}
            for c in range(8)]


def phase1_inmaps(inputs):
    x = np.ascontiguousarray(np.asarray(inputs["L0_attn_x"], dtype=np.float32))
    zeros = np.zeros((TOK, D), np.float32)
    W1 = np.ascontiguousarray(np.asarray(inputs["L0_attn_W1"], np.float32))
    b1 = np.asarray(inputs["L0_attn_b1"], np.float32).reshape(1, D)
    w2 = np.asarray(inputs["L0_attn_w2"], np.float32).reshape(1, D)
    b2 = np.asarray(inputs["L0_attn_b2"], np.float32).reshape(1, 1)
    Wc = np.ascontiguousarray(np.asarray(inputs["L0_attn_Wc"], np.float32))
    bc = np.asarray(inputs["L0_attn_bc"], np.float32).reshape(1, D)
    ci = const_inputs()
    maps = []
    for c in range(8):
        b, h = c // 2, c % 2
        maps.append({
            "c_identbf": ci["c_identbf"], "c_identf": ci["c_identf"],
            "x_own": np.ascontiguousarray(x[b, h * TOK:(h + 1) * TOK, :]),
            "x_pre": np.ascontiguousarray(x[b, 0:TOK, :]) if h == 1 else zeros,
            "zmask": np.array([[1.0 if h == 1 else 0.0]], np.float32),
            "W1": W1, "b1": b1, "w2": w2, "b2": b2, "Wc": Wc, "bc": bc,
        })
    return maps


def build_fused_program():
    nc = bass.Bass("TRN2", target_bir_lowering=False)
    T = TOK
    ei = lambda n, sh, dt=F32: nc.dram_tensor(n, sh, dt, kind="ExternalInput")
    il = lambda n, sh, dt=F32: nc.dram_tensor(n, sh, dt, kind="Internal")
    stream1 = il("i_stream1", [T, D])
    ab1 = il("i_ab1", [128, 64])
    stream2 = il("i_stream2", [T, D])
    ab2 = il("i_ab2", [128, 64])
    stream3 = il("i_stream3", [T, D])
    ab3 = il("i_ab3", [128, 64])
    xfm2 = il("i_xfm2", [8 * 128, T], BF16)
    xfm5 = il("i_xfm5", [8 * 128, T], BF16)
    xa_qk = il("i_xaqk", [16 * 128, T], BF16)
    v_self = il("i_vself", [T, D], BF16)
    st_self = il("i_stself", [128, 48])
    k_gath = il("i_kgath", [2 * 1024, T], BF16)
    v_gath = il("i_vgath", [2 * T, D], BF16)
    st_gath = il("i_stgath", [8 * 128, 48])
    io1 = {
        "x_own": ei("x_own", [T, D]), "x_pre": ei("x_pre", [T, D]),
        "zmask": ei("zmask1", [1, 1]),
        "W1": ei("a0_W1", [D, D]), "b1": ei("a0_b1", [1, D]),
        "w2": ei("a0_w2", [1, D]), "b2": ei("a0_b2", [1, 1]),
        "Wc": ei("a0_Wc", [D, 2 * D]), "bc": ei("a0_bc", [1, D]),
        "stream_out": stream1, "ab_out": ab1,
    }
    io2 = {
        "stream_in": stream1, "ab_in": ab1,
        "Wg": ei("f0_Wg", [D, 8]), "bg": ei("f0_bg", [1, 8]),
        "W1": ei("f0_W1", [8 * D, 2048]), "b1": ei("f0_b1", [8, 2048]),
        "W2": ei("f0_W2", [8 * 2048, D]), "b2": ei("f0_b2", [8, D]),
        "xfm": xfm2, "stream_out": stream2, "ab_out": ab2,
    }
    io3a = {
        "stream_in": stream2, "ab_in": ab2,
        "W_attn": ei("a1_Wattn", [3072, D]),
        "lora_attn_A": ei("a1_lA", [D, 64]), "lora_attn_B": ei("a1_lB", [64, 3072]),
        "xa_qk": xa_qk, "v_tm": v_self, "stats": st_self,
    }
    io3b = {
        "stream_in": stream2, "ab_in": ab2,
        "xa_q": xa_qk,
        "k_all": k_gath, "v_all": v_gath, "stats_all": st_gath,
        "zmask": ei("zmask3", [128, 2]),
        "trif": ei("trif", [128, 128]),
        "W_proj": ei("a1_Wp", [D, D]),
        "lora_proj_A": ei("a1_pA", [D, 64]), "lora_proj_B": ei("a1_pB", [64, D]),
        "bn_gamma": ei("a1_gam", [1, 3072]), "bn_beta": ei("a1_bet", [1, 3072]),
        "stream_out": stream3, "ab_out": ab3,
    }
    io5 = {
        "stream_in": stream3, "ab_in": ab3,
        "wg": ei("f1_wg", [D, 8]), "bg": ei("f1_bg", [1, 8]),
        "W1": ei("f1_W1", [8 * D, 4096]), "b1": ei("f1_b1", [8, 4096]),
        "W2": ei("f1_W2", [8 * 4096, D]), "b2": ei("f1_b2", [8, D]),
        "xfm": xfm5, "out": nc.dram_tensor("out", [T, D], F32, kind="ExternalOutput"),
    }
    with ExitStack() as stack:
        p = Prog(nc, stack)
        with nc.Block() as block:
            p.common = Common(p, stack)
            with ExitStack() as s1:
                build_l0_attn(p, s1, io1)
            p.barrier()
            with ExitStack() as s2:
                build_l0_ffn(p, s2, io2)
            p.barrier()
            with ExitStack() as s3:
                build_l1_attn_a(p, s3, io3a)
            p.barrier()
            sCC = p.sem("sCC")
            pairs = [[0, 1], [2, 3], [4, 5], [6, 7]]
            allg = [[0, 1, 2, 3, 4, 5, 6, 7]]
            p.emit("gpsimd",
                   lambda e: e.collective_compute(
                       "AllGather", ALU.bypass, replica_groups=allg,
                       ins=[st_self.ap()], outs=[st_gath.ap()]),
                   inc=sCC, k=16)
            p.emit("gpsimd",
                   lambda e: e.collective_compute(
                       "AllGather", ALU.bypass, replica_groups=pairs,
                       ins=[xa_qk.ap()[8 * 128:16 * 128, :]], outs=[k_gath.ap()]),
                   inc=sCC, k=16)
            p.emit("gpsimd",
                   lambda e: e.collective_compute(
                       "AllGather", ALU.bypass, replica_groups=pairs,
                       ins=[v_self.ap()], outs=[v_gath.ap()]),
                   inc=sCC, k=16)
            p.barrier()
            with ExitStack() as s4:
                build_l1_attn_b(p, s4, io3b)
            p.barrier()
            with ExitStack() as s5:
                build_l1_ffn(p, s5, io5)
            p.build(block)
    return nc


def fused_inmaps(inputs):
    x = np.ascontiguousarray(np.asarray(inputs["L0_attn_x"], dtype=np.float32))
    zeros = np.zeros((TOK, D), np.float32)
    ci = const_inputs()
    com = {
        "c_identbf": ci["c_identbf"], "c_identf": ci["c_identf"],
        "trif": ci["trif"],
        "a0_W1": np.ascontiguousarray(np.asarray(inputs["L0_attn_W1"], np.float32)),
        "a0_b1": np.asarray(inputs["L0_attn_b1"], np.float32).reshape(1, D),
        "a0_w2": np.asarray(inputs["L0_attn_w2"], np.float32).reshape(1, D),
        "a0_b2": np.asarray(inputs["L0_attn_b2"], np.float32).reshape(1, 1),
        "a0_Wc": np.ascontiguousarray(np.asarray(inputs["L0_attn_Wc"], np.float32)),
        "a0_bc": np.asarray(inputs["L0_attn_bc"], np.float32).reshape(1, D),
        "f0_Wg": np.ascontiguousarray(np.asarray(inputs["L0_ffn_Wg"], np.float32)),
        "f0_bg": np.asarray(inputs["L0_ffn_bg"], np.float32).reshape(1, 8),
        "f0_W1": np.ascontiguousarray(np.asarray(inputs["L0_ffn_W1"], np.float32).reshape(8 * D, 2048)),
        "f0_b1": np.ascontiguousarray(np.asarray(inputs["L0_ffn_b1"], np.float32)),
        "f0_W2": np.ascontiguousarray(np.asarray(inputs["L0_ffn_W2"], np.float32).reshape(8 * 2048, D)),
        "f0_b2": np.ascontiguousarray(np.asarray(inputs["L0_ffn_b2"], np.float32)),
        "a1_Wattn": np.ascontiguousarray(np.asarray(inputs["L1_attn_W_attn"], np.float32)),
        "a1_lA": np.ascontiguousarray(np.asarray(inputs["L1_attn_lora_attn_A"], np.float32)),
        "a1_lB": np.ascontiguousarray(np.asarray(inputs["L1_attn_lora_attn_B"], np.float32)),
        "a1_Wp": np.ascontiguousarray(np.asarray(inputs["L1_attn_W_proj"], np.float32)),
        "a1_pA": np.ascontiguousarray(np.asarray(inputs["L1_attn_lora_proj_A"], np.float32)),
        "a1_pB": np.ascontiguousarray(np.asarray(inputs["L1_attn_lora_proj_B"], np.float32)),
        "a1_gam": np.asarray(inputs["L1_attn_bn_gamma"], np.float32).reshape(1, 3072),
        "a1_bet": np.asarray(inputs["L1_attn_bn_beta"], np.float32).reshape(1, 3072),
        "f1_wg": np.ascontiguousarray(np.asarray(inputs["L1_ffn_w_gate"], np.float32)),
        "f1_bg": np.asarray(inputs["L1_ffn_b_gate"], np.float32).reshape(1, 8),
        "f1_W1": np.ascontiguousarray(np.asarray(inputs["L1_ffn_W1"], np.float32).reshape(8 * D, 4096)),
        "f1_b1": np.ascontiguousarray(np.asarray(inputs["L1_ffn_b1"], np.float32)),
        "f1_W2": np.ascontiguousarray(np.asarray(inputs["L1_ffn_W2"], np.float32).reshape(8 * 4096, D)),
        "f1_b2": np.ascontiguousarray(np.asarray(inputs["L1_ffn_b2"], np.float32)),
    }
    maps = []
    for c in range(8):
        b, h = c // 2, c % 2
        m = dict(com)
        m["x_own"] = np.ascontiguousarray(x[b, h * TOK:(h + 1) * TOK, :])
        m["x_pre"] = np.ascontiguousarray(x[b, 0:TOK, :]) if h == 1 else zeros
        m["zmask1"] = np.array([[1.0 if h == 1 else 0.0]], np.float32)
        m["zmask3"] = np.ascontiguousarray(np.broadcast_to(
            np.array([1.0 if h == 1 else 0.0,
                      0.0 if h == 1 else 1.0], np.float32), (128, 2)))
        maps.append(m)
    return maps


def build_launchA_program():
    nc = bass.Bass("TRN2", target_bir_lowering=False)
    T = TOK
    ei = lambda n, sh, dt=F32: nc.dram_tensor(n, sh, dt, kind="ExternalInput")
    il = lambda n, sh, dt=F32: nc.dram_tensor(n, sh, dt, kind="Internal")
    eo = lambda n, sh, dt=F32: nc.dram_tensor(n, sh, dt, kind="ExternalOutput")
    stream1 = il("i_stream1", [T, D])
    ab1 = il("i_ab1", [128, 64])
    stream2 = eo("stream2", [T, D])
    ab2 = eo("ab2", [128, 64])
    xfm2 = il("i_xfm2", [8 * 128, T], BF16)
    io1 = {
        "x_own": ei("x_own", [T, D]), "x_pre": ei("x_pre", [T, D]),
        "zmask": ei("zmask1", [1, 1]),
        "W1": ei("a0_W1", [D, D]), "b1": ei("a0_b1", [1, D]),
        "w2": ei("a0_w2", [1, D]), "b2": ei("a0_b2", [1, 1]),
        "Wc": ei("a0_Wc", [D, 2 * D]), "bc": ei("a0_bc", [1, D]),
        "stream_out": stream1, "ab_out": ab1,
    }
    io2 = {
        "stream_in": stream1, "ab_in": ab1,
        "Wg": ei("f0_Wg", [D, 8]), "bg": ei("f0_bg", [1, 8]),
        "W1": ei("f0_W1", [8 * D, 2048]), "b1": ei("f0_b1", [8, 2048]),
        "W2": ei("f0_W2", [8 * 2048, D]), "b2": ei("f0_b2", [8, D]),
        "xfm": xfm2, "stream_out": stream2, "ab_out": ab2,
    }
    io3a = {
        "stream_in": stream2, "ab_in": ab2,
        "W_attn": ei("a1_Wattn", [3072, D]),
        "lora_attn_A": ei("a1_lA", [D, 64]), "lora_attn_B": ei("a1_lB", [64, 3072]),
        "xa_qk": eo("xa_qk", [16 * 128, T], BF16),
        "v_tm": eo("v_tm", [T, D], BF16),
        "stats": eo("stats", [128, 48]),
    }
    with ExitStack() as stack:
        p = Prog(nc, stack)
        with nc.Block() as block:
            p.common = Common(p, stack)
            with ExitStack() as s1:
                build_l0_attn(p, s1, io1)
            p.barrier()
            with ExitStack() as s2:
                build_l0_ffn(p, s2, io2)
            p.barrier()
            with ExitStack() as s3:
                build_l1_attn_a(p, s3, io3a)
            p.build(block)
    return nc


def build_launchB_program():
    nc = bass.Bass("TRN2", target_bir_lowering=False)
    T = TOK
    ei = lambda n, sh, dt=F32: nc.dram_tensor(n, sh, dt, kind="ExternalInput")
    il = lambda n, sh, dt=F32: nc.dram_tensor(n, sh, dt, kind="Internal")
    stream3 = il("i_stream3", [T, D])
    ab3 = il("i_ab3", [128, 64])
    xfm5 = il("i_xfm5", [8 * 128, T], BF16)
    io3b = {
        "stream_in": ei("stream2", [T, D]), "ab_in": ei("ab2", [128, 64]),
        "xa_q": ei("xa_q", [8 * 128, T], BF16),
        "k_all": ei("k_all", [8 * 128, 2 * T], BF16),
        "v_all": ei("v_all", [2 * T, D], BF16),
        "stats_all": ei("stats_all", [8 * 128, 48]),
        "zmask": ei("zmask3", [128, 2]),
        "trif": ei("trif", [128, 128]),
        "W_proj": ei("a1_Wp", [D, D]),
        "lora_proj_A": ei("a1_pA", [D, 64]), "lora_proj_B": ei("a1_pB", [64, D]),
        "bn_gamma": ei("a1_gam", [1, 3072]), "bn_beta": ei("a1_bet", [1, 3072]),
        "stream_out": stream3, "ab_out": ab3,
    }
    io5 = {
        "stream_in": stream3, "ab_in": ab3,
        "wg": ei("f1_wg", [D, 8]), "bg": ei("f1_bg", [1, 8]),
        "W1": ei("f1_W1", [8 * D, 4096]), "b1": ei("f1_b1", [8, 4096]),
        "W2": ei("f1_W2", [8 * 4096, D]), "b2": ei("f1_b2", [8, D]),
        "xfm": xfm5, "out": nc.dram_tensor("out", [T, D], F32, kind="ExternalOutput"),
    }
    with ExitStack() as stack:
        p = Prog(nc, stack)
        with nc.Block() as block:
            p.common = Common(p, stack)
            with ExitStack() as s4:
                build_l1_attn_b(p, s4, io3b)
            p.barrier()
            with ExitStack() as s5:
                build_l1_ffn(p, s5, io5)
            p.build(block)
    return nc


def kernel2(**inputs):
    cores = list(range(8))
    inputs = {k: np.asarray(v) for k, v in inputs.items()}
    ncA = _program("A", build_launchA_program)
    mapsA = []
    m1 = phase1_inmaps(inputs)
    m2 = phase2_inmaps(inputs, [None] * 8, [None] * 8)
    m3 = phase3a_inmaps(inputs, [None] * 8, [None] * 8)
    for c in range(8):
        m = {}
        for k, v in m1[c].items():
            m[{"zmask": "zmask1", "W1": "a0_W1", "b1": "a0_b1", "w2": "a0_w2",
               "b2": "a0_b2", "Wc": "a0_Wc", "bc": "a0_bc"}.get(k, k)] = v
        for k, v in m2[c].items():
            if v is None or k in ("c_identbf", "c_identf"):
                continue
            m[{"Wg": "f0_Wg", "bg": "f0_bg", "W1": "f0_W1", "b1": "f0_b1",
               "W2": "f0_W2", "b2": "f0_b2"}[k]] = v
        for k, v in m3[c].items():
            if v is None or k in ("c_identbf", "c_identf"):
                continue
            m[{"W_attn": "a1_Wattn", "lora_attn_A": "a1_lA",
               "lora_attn_B": "a1_lB"}[k]] = v
        mapsA.append(m)
    resA = run_bass_kernel_spmd(ncA, mapsA, core_ids=cores).results
    streams = [r["stream2"] for r in resA]
    abs_ = [r["ab2"] for r in resA]
    qk = [r["xa_qk"] for r in resA]
    vv = [r["v_tm"] for r in resA]
    st = [r["stats"] for r in resA]

    ncB = _program("B", build_launchB_program)
    m3b = phase3b_inmaps(inputs, streams, abs_, qk, vv, st)
    m5 = phase5_inmaps(inputs, [None] * 8, [None] * 8)
    mapsB = []
    for c in range(8):
        m = {}
        for k, v in m3b[c].items():
            m[{"stream_in": "stream2", "ab_in": "ab2", "zmask": "zmask3",
               "W_proj": "a1_Wp", "lora_proj_A": "a1_pA", "lora_proj_B": "a1_pB",
               "bn_gamma": "a1_gam", "bn_beta": "a1_bet"}.get(k, k)] = v
        for k, v in m5[c].items():
            if v is None or k in ("c_identbf", "c_identf", "stream_in", "ab_in"):
                continue
            m[{"wg": "f1_wg", "bg": "f1_bg", "W1": "f1_W1", "b1": "f1_b1",
               "W2": "f1_W2", "b2": "f1_b2"}[k]] = v
        mapsB.append(m)
    resB = run_bass_kernel_spmd(ncB, mapsB, core_ids=cores).results
    out = np.empty((4, 8192, 1024), np.float32)
    for c in range(8):
        b, h = c // 2, c % 2
        out[b, h * TOK:(h + 1) * TOK] = resB[c]["out"]
    return out


_PROGRAMS = {}


def _program(name, builder):
    if name not in _PROGRAMS:
        _PROGRAMS[name] = builder()
    return _PROGRAMS[name]


def kernel(**inputs):
    return kernel2(**inputs)


def kernel_unfused(**inputs):
    cores = list(range(8))

    nc1 = _program("p1", build_phase1_program)
    res1 = run_bass_kernel_spmd(nc1, phase1_inmaps(inputs), core_ids=cores).results
    streams = [r["stream_out"] for r in res1]
    abs_ = [r["ab_out"] for r in res1]

    nc2 = _program("p2", build_phase2_program)
    res2 = run_bass_kernel_spmd(nc2, phase2_inmaps(inputs, streams, abs_),
                                core_ids=cores).results
    streams = [r["stream_out"] for r in res2]
    abs_ = [r["ab_out"] for r in res2]

    nc3a = _program("p3a", build_phase3a_program)
    res3a = run_bass_kernel_spmd(nc3a, phase3a_inmaps(inputs, streams, abs_),
                                 core_ids=cores).results
    qk = [r["xa_qk"] for r in res3a]
    vv = [r["v_tm"] for r in res3a]
    st = [r["stats"] for r in res3a]

    nc3b = _program("p3b", build_phase3b_program)
    res3b = run_bass_kernel_spmd(nc3b, phase3b_inmaps(inputs, streams, abs_, qk, vv, st),
                                 core_ids=cores).results
    streams = [r["stream_out"] for r in res3b]
    abs_ = [r["ab_out"] for r in res3b]

    nc5 = _program("p5", build_phase5_program)
    res5 = run_bass_kernel_spmd(nc5, phase5_inmaps(inputs, streams, abs_),
                                core_ids=cores).results

    out = np.empty((4, 8192, 1024), np.float32)
    for c in range(8):
        b, h = c // 2, c % 2
        out[b, h * TOK:(h + 1) * TOK] = res5[c]["out"]
    return out
